# version 10
# baseline (speedup 1.0000x reference)
import math
import dataclasses
from contextlib import ExitStack, contextmanager
import numpy as np
import ml_dtypes

import concourse.bass as bass
import concourse.mybir as mybir
from concourse.bass_utils import run_bass_kernel_spmd

F32 = mybir.dt.float32
BF16 = mybir.dt.bfloat16
I32 = mybir.dt.int32
AF = mybir.ActivationFunctionType
ALU = mybir.AluOpType

D = 2048
B = 4
S = 4096
TOK = 2048
FT = D // 128

EPS = 1e-5
COST_LAMBDA = 1e-07
HID = [1024, 2048, 3072, 4096, 1024, 2048, 3072, 4096]
GELU = AF.Gelu

BF = ml_dtypes.bfloat16



class Sem:
    def __init__(self, nc, name):
        self.h = nc.alloc_semaphore(name=name)
        self.n = 0

    def inc(self, instr, k=1):
        instr.then_inc(self.h, k)
        self.n += k
        return self.n


class Prog:
    def __init__(self):
        self.nc = bass.Bass("TRN2", target_bir_lowering=False)
        nc = self.nc
        self.mm = Sem(nc, "mm")
        self.act = Sem(nc, "act")
        self.dve = Sem(nc, "dve")
        self.gp = Sem(nc, "gp")
        self.ld = Sem(nc, "ld")
        self.lg = Sem(nc, "lg")
        self.st = Sem(nc, "st")

    def sbuf(self, name, shape, dtype):
        if getattr(self, "_scopes", None):
            return self._scopes[-1].enter_context(self.nc.sbuf_tensor(name, shape, dtype))
        return self.nc.alloc_sbuf_tensor(name, shape, dtype)

    def psum(self, name, shape, dtype=F32):
        if getattr(self, "_scopes", None):
            return self._scopes[-1].enter_context(self.nc.psum_tensor(name, shape, dtype))
        return self.nc.alloc_psum_tensor(name, shape, dtype)

    def mark(self, name):
        try:
            self.nc.enter_named_scope(name, False)
            self.nc.leave_named_scope(name, 0, False)
        except Exception:
            pass

    @contextmanager
    def scope(self):
        if not hasattr(self, "_scopes"):
            self._scopes = []
        es = ExitStack()
        self._scopes.append(es)
        try:
            yield
        finally:
            self._scopes.pop()
            es.close()

    def barrier_engines(self, engines=None, extra=()):
        nc = self.nc
        engines = engines or [nc.tensor, nc.scalar, nc.vector, nc.sync, nc.gpsimd]
        for s in (self.mm, self.act, self.dve, self.gp, self.ld, self.st, *extra):
            for e in engines:
                e.wait_ge(s.h, s.n)


def hbm_fm_view(ap, p=128):
    return ap.rearrange("(f p) t -> p f t", p=p)


def offap(ap, toff):
    if isinstance(toff, int) and toff == 0:
        return ap
    return dataclasses.replace(ap, offset=toff + ap.offset)


def load_weight_fm(P, dst_sb, src_hbm, nM, nK, eng=None):
    nc = P.nc
    eng = eng or nc.gpsimd
    src = src_hbm.rearrange("(k p) mc -> p k mc", p=128)
    eng.wait_ge(P.lg.h, P.lg.n)
    d = eng.dma_start(out=dst_sb[:, :].rearrange("p (k mc) -> p k mc", k=nK), in_=src)
    P.lg.inc(d, 16)


def w_tile(w_sb, m, k, nM):
    c0 = (k * nM + m) * 128
    return w_sb[:, c0:c0 + 128]


def pack_bias(b):
    n = b.shape[0]
    return np.ascontiguousarray(b.reshape(n // 128, 128).T.astype(np.float32))



class LnScratch:
    def __init__(self, P, T, name="lns", xb=None, sq=None, st_ps=None, bc_ps=None):
        self.T = T
        self.xb = xb if xb is not None else P.sbuf(name + "_xb", [128, FT * T], BF16)
        self.sq = sq if sq is not None else P.sbuf(name + "_sq", [128, FT * T], BF16)
        self.st_ps = st_ps if st_ps is not None else P.psum(name + "_stps", [1, 2 * T], F32)
        self.bc_ps = bc_ps if bc_ps is not None else P.psum(name + "_bcps", [128, T], F32)
        self.stats = P.sbuf(name + "_stats", [1, 6 * T], F32)
        self.ab = P.sbuf(name + "_ab", [128, 2 * T], BF16)


def emit_ln(P, scr, src_slices, dst_slices, ntok, consts):
    nc = P.nc
    ones_bf, ones_row, cst = consts
    mm, act, dve = P.mm, P.act, P.dve
    T = scr.T
    v = nc.vector
    v.wait_ge(dve.h, dve.n)
    nc.scalar.wait_ge(dve.h, dve.n)
    cast = src_slices(0).dtype == F32
    if cast:
        for ft in range(FT):
            ins = v.tensor_copy(scr.xb[:, ft * ntok:(ft + 1) * ntok], src_slices(ft))
        dve.inc(ins)
        for e in (nc.scalar, nc.tensor):
            e.wait_ge(dve.h, dve.n)
        ssl = lambda ft: scr.xb[:, ft * ntok:(ft + 1) * ntok]
    else:
        ssl = src_slices
    for ft in range(FT):
        ins = nc.scalar.activation(scr.sq[:, ft * ntok:(ft + 1) * ntok], ssl(ft), AF.Square)
    act.inc(ins)
    nc.tensor.wait_ge(act.h, act.n)
    for ft in range(FT):
        nc.tensor.matmul(scr.st_ps[0:1, 0:ntok], ones_bf[:, 0:1], ssl(ft),
                         start=(ft == 0), stop=(ft == FT - 1))
    for ft in range(FT):
        ins = nc.tensor.matmul(scr.st_ps[0:1, T:T + ntok], ones_bf[:, 0:1],
                               scr.sq[:, ft * ntok:(ft + 1) * ntok],
                               start=(ft == 0), stop=(ft == FT - 1))
    mm.inc(ins)
    nc.vector.wait_ge(mm.h, mm.n)
    mean = scr.stats[0:1, 0:ntok]
    aa = scr.stats[0:1, T:T + ntok]
    bb = scr.stats[0:1, 2 * T:2 * T + ntok]
    ex2 = scr.stats[0:1, 3 * T:3 * T + ntok]
    var = scr.stats[0:1, 4 * T:4 * T + ntok]
    lnv = scr.stats[0:1, 5 * T:5 * T + ntok]
    dve.inc(v.tensor_scalar_mul(mean, scr.st_ps[0:1, 0:ntok], 1.0 / D))
    dve.inc(v.tensor_scalar_mul(ex2, scr.st_ps[0:1, T:T + ntok], 1.0 / D))
    v.wait_ge(dve.h, dve.n)
    dve.inc(v.tensor_tensor(var, mean, mean, ALU.mult))
    v.wait_ge(dve.h, dve.n)
    dve.inc(v.tensor_sub(var, ex2, var))
    nc.scalar.wait_ge(dve.h, dve.n)
    act.inc(nc.scalar.activation(lnv, var, AF.Ln, bias=cst[0:1, 0:1]))
    nc.scalar.wait_ge(act.h, act.n)
    act.inc(nc.scalar.activation(aa, lnv, AF.Exp, scale=cst[0:1, 1:2]))
    nc.vector.wait_ge(act.h, act.n)
    dve.inc(v.tensor_tensor(bb, mean, aa, ALU.mult))
    v.wait_ge(dve.h, dve.n)
    dve.inc(v.tensor_scalar_mul(bb, bb, -1.0))
    nc.tensor.wait_ge(dve.h, dve.n)
    mm.inc(nc.tensor.matmul(scr.bc_ps[:, 0:ntok], ones_row[0:1, :], aa, start=True, stop=True))
    nc.scalar.wait_ge(mm.h, mm.n)
    act.inc(nc.scalar.copy(scr.ab[:, 0:ntok], scr.bc_ps[:, 0:ntok]))
    nc.tensor.wait_ge(act.h, act.n)
    mm.inc(nc.tensor.matmul(scr.bc_ps[:, 0:ntok], ones_row[0:1, :], bb, start=True, stop=True))
    nc.scalar.wait_ge(mm.h, mm.n)
    act.inc(nc.scalar.copy(scr.ab[:, T:T + ntok], scr.bc_ps[:, 0:ntok]))
    nc.vector.wait_ge(act.h, act.n)
    for ft in range(FT):
        ins = v.tensor_tensor(dst_slices(ft), ssl(ft), scr.ab[:, 0:ntok], ALU.mult)
    dve.inc(ins)
    v.wait_ge(dve.h, dve.n)
    for ft in range(FT):
        ins = v.tensor_tensor(dst_slices(ft), dst_slices(ft), scr.ab[:, T:T + ntok], ALU.add)
    dve.inc(ins)
    for e in (nc.tensor, nc.scalar, nc.sync, nc.gpsimd):
        e.wait_ge(dve.h, dve.n)



def emit_attn(P, layer, io, pfx="a_"):
    nc = P.nc
    v = nc.vector
    if layer == 0:
        NQ, NKV, HD = 8, 8, 128
        scale = 1.0 / math.sqrt(128.0)
        MW = 1024
        VW = KW0 = 1024
        VST = HD
    else:
        NQ, NKV, HD = 16, 4, 64
        scale = 1.0 / math.sqrt(64.0)
        MW = 2688
        VST = 65
    QW = NQ * HD
    KW = NKV * HD
    VWID = NKV * VST
    QB = 512
    NQB = S // QB
    NKT = S // 128
    xT = io["xT"]; wqT = io["wqT"]; wkT = io["wkT"]; wvT = io["wvT"]
    maskm = io["maskm"]; attnout = io["attnout"]; roff = io.get("roff", 0)
    qh = io["qh"]; kh = io["kh"]; vh = io["vh"]
    mm, act, dve, ld, st = P.mm, P.act, P.dve, P.ld, P.st

    ones_bf = P.sbuf(pfx + "ones_bf", [128, 128], BF16)
    ones_row = P.sbuf(pfx + "ones_row", [1, 128], F32)
    cst = P.sbuf(pfx + "cst", [128, 3], F32)
    nc.gpsimd.memset(ones_bf[:, :], 1.0)
    nc.gpsimd.memset(ones_row[:, :], 1.0)
    nc.gpsimd.memset(cst[:, 0:1], EPS)
    nc.gpsimd.memset(cst[:, 1:2], -0.5)
    P.gp.inc(nc.gpsimd.memset(cst[:, 2:3], scale))
    for e in (nc.tensor, nc.vector, nc.scalar, nc.sync):
        e.wait_ge(P.gp.h, P.gp.n)

    mask_sb = P.sbuf(pfx + "mask_sb", [128, MW], BF16)
    nc.sync.dma_start(out=mask_sb[:, :], in_=maskm[:, :]).then_inc(ld.h, 16)
    ld.n += 16

    P.mark(pfx + "qkv")
    with P.scope():
        wq_sb = P.sbuf(pfx + "wq_sb", [128, (QW // 128) * FT * 128], BF16)
        wk_sb = P.sbuf(pfx + "wk_sb", [128, (KW // 128) * FT * 128], BF16)
        load_weight_fm(P, wq_sb, wqT, QW // 128, FT)
        load_weight_fm(P, wk_sb, wkT, KW // 128, FT)
        wv_sb = P.sbuf(pfx + "wv_sb", [128, FT * KW], BF16)
        nc.gpsimd.wait_ge(P.lg.h, P.lg.n)
        d = nc.gpsimd.dma_start(out=wv_sb[:, :].rearrange("p (f c) -> p f c", f=FT),
                                in_=hbm_fm_view(wvT))
        P.lg.inc(d, 16)

        TB = 512
        x_sb = P.sbuf(pfx + "x_sb", [128, FT * TB], BF16)
        ln_sb = P.sbuf(pfx + "ln_sb", [128, FT * TB], BF16)
        lnscr = LnScratch(P, TB, name=pfx + "lns")
        qk_ps = [P.psum(pfx + f"qk_ps{i}", [128, TB], F32) for i in range(2)]
        qko_sb = P.sbuf(pfx + "qko_sb", [128, 2 * TB], BF16)
        qsem = [Sem(nc, pfx + f"qsem{i}") for i in range(2)]
        vsem = [Sem(nc, pfx + f"vsem{i}") for i in range(2)]
        vo_sb = P.sbuf(pfx + "vo_sb", [128, 2 * VWID], BF16)
        if layer == 1:
            P.gp.inc(nc.gpsimd.memset(vo_sb[:, :], 1.0))
            nc.scalar.wait_ge(P.gp.h, P.gp.n)

        xv = hbm_fm_view(xT)
        nc.sync.wait_ge(ld.h, ld.n)
        nc.sync.wait_ge(P.lg.h, P.lg.n)
        evac_hist = []
        nmm = 0
        for tb in range(S // TB):
            nc.gpsimd.wait_ge(P.lg.h, P.lg.n)
            nc.gpsimd.wait_ge(dve.h, dve.n)
            nc.gpsimd.wait_ge(mm.h, mm.n)
            nc.gpsimd.dma_start(out=x_sb[:, :].rearrange("p (f t) -> p f t", f=FT),
                                in_=xv[:, :, tb * TB:(tb + 1) * TB]).then_inc(P.lg.h, 16)
            P.lg.n += 16
            for e in (nc.vector, nc.scalar, nc.tensor):
                e.wait_ge(P.lg.h, P.lg.n)
                e.wait_ge(ld.h, ld.n)
            emit_ln(P, lnscr,
                    lambda ft: x_sb[:, ft * TB:(ft + 1) * TB],
                    lambda ft: ln_sb[:, ft * TB:(ft + 1) * TB],
                    TB, (ones_bf, ones_row, cst))
            slot = 0
            for which, wsb, nm, dsth in ((0, wq_sb, QW // 128, qh), (1, wk_sb, KW // 128, kh)):
                for m in range(nm):
                    bank = slot % 2
                    nc.tensor.wait_ge(act.h, act.n)
                    for ft in range(FT):
                        ins = nc.tensor.matmul(qk_ps[bank][:, :], w_tile(wsb, m, ft, nm),
                                               ln_sb[:, ft * TB:(ft + 1) * TB],
                                               start=(ft == 0), stop=(ft == FT - 1))
                    mm.inc(ins)
                    nc.scalar.wait_ge(mm.h, mm.n)
                    nc.scalar.wait_ge(qsem[bank].h, qsem[bank].n)
                    act.inc(nc.scalar.copy(qko_sb[:, bank * TB:(bank + 1) * TB], qk_ps[bank][:, :]))
                    nc.sync.wait_ge(act.h, act.n)
                    d = nc.sync.dma_start(out=dsth[m * 128:(m + 1) * 128, tb * TB:(tb + 1) * TB],
                                          in_=qko_sb[:, bank * TB:(bank + 1) * TB])
                    qsem[bank].inc(d, 16)
                    slot += 1
            vhist = []
            for tt in range(TB // 128):
                bank = tt % 2
                nw = (KW + 511) // 512
                if len(vhist) >= 2:
                    pass
                for nb in range(nw):
                    ncol = min(512, KW - nb * 512)
                    nc.tensor.wait_ge(act.h, act.n)
                    for ft in range(FT):
                        ins = nc.tensor.matmul(
                            qk_ps[0][:, 0:ncol],
                            ln_sb[:, ft * TB + tt * 128: ft * TB + tt * 128 + 128],
                            wv_sb[:, ft * KW + nb * 512: ft * KW + nb * 512 + ncol],
                            start=(ft == 0), stop=(ft == FT - 1))
                    mm.inc(ins)
                    nc.scalar.wait_ge(mm.h, mm.n)
                    nc.scalar.wait_ge(vsem[bank].h, vsem[bank].n)
                    if layer == 0:
                        dstv = vo_sb[:, bank * VWID + nb * 512: bank * VWID + nb * 512 + ncol]
                    else:
                        dstv = vo_sb[:, bank * VWID:(bank + 1) * VWID].rearrange(
                            "p (g c) -> p g c", c=VST)[:, :, 0:64]
                    act.inc(nc.scalar.copy(dstv, qk_ps[0][:, 0:ncol].rearrange(
                        "p (g c) -> p g c", c=64) if layer == 1 else qk_ps[0][:, 0:ncol]))
                nc.sync.wait_ge(act.h, act.n)
                t0 = tb * TB + tt * 128
                d = nc.sync.dma_start(out=vh[t0:t0 + 128, :],
                                      in_=vo_sb[:, bank * VWID: bank * VWID + VWID])
                vsem[bank].inc(d, 16)

        P.barrier_engines(extra=qsem + vsem)

    P.mark(pfx + "core")
    def kt_list(qb):
        out = []
        if layer == 0:
            for kt in range(4 * qb + 4):
                if kt >= 4 * qb:
                    delta = kt * 128 - qb * QB
                    out.append((kt, 512 - delta))
                else:
                    out.append((kt, None))
        else:
            for kt in range(NKT):
                delta = kt * 128 - qb * QB
                if -513 <= delta <= -128:
                    continue
                if -1151 <= delta <= 510:
                    out.append((kt, 640 - delta))
                else:
                    out.append((kt, None))
        return out

    with P.scope():
        L1 = layer == 1
        k_sb = [P.sbuf(pfx + f"k_sb{i}", [128, S], BF16) for i in range(2)]
        v_sb = [P.sbuf(pfx + f"v_sb{i}", [128, NKT * VST], BF16) for i in range(2)]
        q_sb = [P.sbuf(pfx + f"q_sb{i}", [128, QB], BF16) for i in range(2)]
        e_pair = [P.sbuf(pfx + f"e_pair{i}", [128, 2 * QB], BF16) for i in range(2)]
        s_pair = [P.psum(pfx + f"s_pair{i}", [128, 2 * QB], F32) for i in range(2)]
        if L1:
            o_psA = P.psum(pfx + "o_psA", [65, QB], F32)
            o_psB = P.psum(pfx + "o_psB", [65, QB], F32)
            nm_ps = P.psum(pfx + "nm_ps", [128, QB], F32)
            t_sb = P.sbuf(pfx + "t_sb", [65, QB], F32)
        else:
            o_ps = [P.psum(pfx + f"o_ps{i}", [128, QB], F32) for i in range(2)]
            sum_ps = [P.psum(pfx + f"sum_ps{i}", [128, QB], F32) for i in range(2)]
        rsum = P.sbuf(pfx + "rsum", [1, 2 * QB], BF16)
        rsum32 = P.sbuf(pfx + "rsum32", [1, 2 * QB], F32)
        rb_sb = P.sbuf(pfx + "rb_sb", [128, 2 * QB], BF16)
        ao_sb = P.sbuf(pfx + "ao_sb", [HD, 2 * QB], BF16)
        ones_rbf = P.sbuf(pfx + "ones_rbf", [1, 128], BF16)
        P.gp.inc(nc.gpsimd.memset(ones_rbf[:, :], 1.0))
        for e in (nc.tensor, nc.vector, nc.scalar, nc.sync):
            e.wait_ge(P.gp.h, P.gp.n)
        asem = [Sem(nc, pfx + f"asem{i}") for i in range(2)]

        ghead = lambda h: h // (NQ // NKV)
        kvsem = [Sem(nc, pfx + f"kvsem{i}") for i in range(2)]
        q2sem = [Sem(nc, pfx + f"q2sem{i}") for i in range(2)]
        kvld = {}
        qld = {}
        last_head_mm = [0]
        qs_hist = {}
        sp_cnt_hist = []
        exp_free_hist = []
        av_free_hist = []
        norm_hist = {}
        pend = {}
        pend_rbfree = [0]

        def load_kv(h):
            hb = h % 2
            g = ghead(h)
            nc.sync.wait_ge(mm.h, last_head_mm[0])
            if L1:
                nc.sync.dma_start(out=k_sb[hb][0:64, :],
                                  in_=kh[g * 64:(g + 1) * 64, :]).then_inc(kvsem[hb].h, 16)
                kvsem[hb].n += 16
                nc.sync.dma_start(out=k_sb[hb][64:128, :],
                                  in_=kh[g * 64:(g + 1) * 64, :]).then_inc(kvsem[hb].h, 16)
                kvsem[hb].n += 16
            else:
                nc.sync.dma_start(out=k_sb[hb][:, :],
                                  in_=kh[g * HD:(g + 1) * HD, :]).then_inc(kvsem[hb].h, 16)
                kvsem[hb].n += 16
            vv = vh.rearrange("(tt p) c -> p tt c", p=128)[:, :, g * VST:(g + 1) * VST]
            nc.sync.dma_start(out=v_sb[hb][:, :].rearrange("p (tt c) -> p tt c", c=VST),
                              in_=vv).then_inc(kvsem[hb].h, 16)
            kvsem[hb].n += 16
            kvld[h] = (hb, kvsem[hb].n)

        def load_q(h, qb, qi):
            qbuf = qi % 2
            if qi >= 2:
                nc.sync.wait_ge(mm.h, qs_hist[qi - 2])
            src = qh[h * HD:(h + 1) * HD, qb * QB:(qb + 1) * QB]
            nc.sync.dma_start(out=q_sb[qbuf][0:HD, :], in_=src).then_inc(q2sem[qbuf].h, 16)
            q2sem[qbuf].n += 16
            if L1:
                nc.sync.dma_start(out=q_sb[qbuf][64:128, :], in_=src).then_inc(q2sem[qbuf].h, 16)
                q2sem[qbuf].n += 16
            qld[qi] = (qbuf, q2sem[qbuf].n)

        def norm_top():
            if not pend or not L1:
                return
            ob = pend["obank"]
            nc.vector.wait_ge(mm.h, pend["mm_last"])
            nc.vector.wait_ge(asem[ob].h, asem[ob].n)
            v.wait_ge(dve.h, dve.n)
            dve.inc(v.tensor_scalar_mul(t_sb[:, :], o_psA[:, :], 1.0))
            v.wait_ge(dve.h, dve.n)
            dve.inc(v.tensor_tensor(t_sb[:, :], o_psB[:, :], t_sb[:, :], ALU.add))
            pend["addB"] = dve.n

        def norm_recip():
            if not pend:
                return
            ob = pend["obank"]
            if L1:
                v.wait_ge(dve.h, pend["addB"])
                dve.inc(v.reciprocal(rsum32[0:1, ob * QB:(ob + 1) * QB], t_sb[64:65, :]))
            else:
                nc.vector.wait_ge(mm.h, pend["mm_last"])
                nc.vector.wait_ge(asem[ob].h, asem[ob].n)
                dve.inc(v.reciprocal(rsum32[0:1, ob * QB:(ob + 1) * QB], sum_ps[ob][0:1, :]))
            v.wait_ge(dve.h, dve.n)
            dve.inc(v.tensor_scalar_mul(rsum[0:1, ob * QB:(ob + 1) * QB],
                                        rsum32[0:1, ob * QB:(ob + 1) * QB], 1.0))
            pend["recip"] = dve.n

        def norm_rb():
            if not pend:
                return
            ob = pend["obank"]
            nc.tensor.wait_ge(dve.h, pend["recip"])
            if L1 and pend_rbfree[0]:
                nc.tensor.wait_ge(dve.h, pend_rbfree[0])
            tgt = nm_ps if L1 else sum_ps[ob]
            mm.inc(nc.tensor.matmul(tgt[:, :], ones_rbf[0:1, :],
                                    rsum[0:1, ob * QB:(ob + 1) * QB],
                                    start=True, stop=True))
            pend["rb"] = mm.n

        def norm_tail():
            if not pend:
                return
            ob = pend["obank"]
            nc.vector.wait_ge(mm.h, pend["rb"])
            if L1:
                dve.inc(v.tensor_tensor(ao_sb[0:HD, ob * QB:(ob + 1) * QB],
                                        nm_ps[0:HD, :], t_sb[0:HD, :], ALU.mult))
                rbfree = dve.n
            else:
                dve.inc(v.tensor_scalar_mul(rb_sb[:, ob * QB:(ob + 1) * QB],
                                            sum_ps[ob][:, :], 1.0))
                rbfree = dve.n
                v.wait_ge(dve.h, dve.n)
                dve.inc(v.tensor_tensor(ao_sb[0:HD, ob * QB:(ob + 1) * QB],
                                        o_ps[ob][0:HD, :],
                                        rb_sb[0:HD, ob * QB:(ob + 1) * QB], ALU.mult))
            nc.sync.wait_ge(dve.h, dve.n)
            d = nc.sync.dma_start(
                out=offap(attnout[pend["h"] * HD:(pend["h"] + 1) * HD,
                                  pend["qb"] * QB:(pend["qb"] + 1) * QB], roff),
                in_=ao_sb[0:HD, ob * QB:(ob + 1) * QB])
            asem[ob].inc(d, 16)
            norm_hist[pend["qi"]] = {"tt": dve.n, "rbfree": rbfree,
                                     "addB": pend.get("addB", 0)}
            pend_rbfree[0] = rbfree if L1 else 0
            pend.clear()

        load_kv(0)
        qi_total = 0
        for h in range(NQ):
            if h + 1 < NQ:
                load_kv(h + 1)
            hb = h % 2
            for qb in range(NQB):
                qi = qi_total
                if qb == 0:
                    load_q(h, 0, qi)
                if qb + 1 < NQB:
                    load_q(h, qb + 1, qi + 1)
                qbuf = qi % 2
                kts = kt_list(qb)
                n = len(kts)
                pairs = [kts[2 * j:2 * j + 2] for j in range((n + 1) // 2)]
                npairs = len(pairs)
                obank = qi % 2
                gp_base = len(sp_cnt_hist)
                j_recip = min(1, npairs - 1)
                j_rb = min(3, npairs - 1)
                j_tail = min(4, npairs - 1)

                def emit_SP(j):
                    gpj = gp_base + j
                    buf = gpj % 2
                    if gpj >= 2:
                        nc.tensor.wait_ge(act.h, exp_free_hist[gpj - 2])
                    for idx, (kt, moff) in enumerate(pairs[j]):
                        if L1:
                            ins = nc.tensor.matmul(
                                s_pair[buf][:, idx * QB:(idx + 1) * QB],
                                k_sb[hb][64 * idx:64 * idx + 64, kt * 128:(kt + 1) * 128],
                                q_sb[qbuf][64 * idx:64 * idx + 64, :],
                                start=True, stop=True, tile_position=(64 * idx, 0))
                        else:
                            ins = nc.tensor.matmul(
                                s_pair[buf][:, idx * QB:(idx + 1) * QB],
                                k_sb[hb][:, kt * 128:(kt + 1) * 128],
                                q_sb[qbuf][:, :], start=True, stop=True)
                        mm.inc(ins)
                    sp_cnt_hist.append(mm.n)

                norm_top()
                nc.tensor.wait_ge(kvsem[kvld[h][0]].h, kvld[h][1])
                nc.tensor.wait_ge(q2sem[qld[qi][0]].h, qld[qi][1])
                emit_SP(0)
                if npairs > 1:
                    emit_SP(1)
                for j in range(npairs):
                    gpj = gp_base + j
                    buf = gpj % 2
                    pr = pairs[j]
                    W = len(pr) * QB
                    nc.scalar.wait_ge(mm.h, sp_cnt_hist[gpj])
                    if gpj >= 2:
                        nc.scalar.wait_ge(mm.h, av_free_hist[gpj - 2])
                    act.inc(nc.scalar.activation(e_pair[buf][:, 0:W], s_pair[buf][:, 0:W],
                                                 AF.Exp, scale=cst[:, 2:3]))
                    exp_free_hist.append(act.n)
                    rdy = []
                    for idx, (kt, moff) in enumerate(pr):
                        if moff is not None:
                            nc.vector.wait_ge(act.h, act.n)
                            dve.inc(v.tensor_tensor(
                                e_pair[buf][:, idx * QB:(idx + 1) * QB],
                                e_pair[buf][:, idx * QB:(idx + 1) * QB],
                                mask_sb[:, moff:moff + QB], ALU.mult))
                            rdy.append((dve, dve.n))
                        else:
                            rdy.append((act, act.n))
                    if j == j_recip:
                        norm_recip()
                    if j == j_rb:
                        norm_rb()
                    if j == j_tail:
                        norm_tail()
                    if j + 2 < npairs:
                        emit_SP(j + 2)
                    first = j == 0
                    lastp = j == npairs - 1
                    for idx, (kt, moff) in enumerate(pr):
                        sem_c, cnt = rdy[idx]
                        nc.tensor.wait_ge(sem_c.h, cnt)
                        kfirst = first and idx == 0
                        klast = lastp and idx == len(pr) - 1
                        if L1:
                            if kfirst and pend and pend.get("addB"):
                                nc.tensor.wait_ge(dve.h, pend["addB"])
                            for ph in range(2):
                                ins = nc.tensor.matmul(
                                    (o_psA if ph == 0 else o_psB)[:, :],
                                    v_sb[hb][64 * ph:64 * ph + 64, kt * VST:kt * VST + VST],
                                    e_pair[buf][64 * ph:64 * ph + 64, idx * QB:(idx + 1) * QB],
                                    start=kfirst, stop=klast, tile_position=(64 * ph, 0))
                                mm.inc(ins)
                        else:
                            if kfirst and qi >= 2:
                                nc.tensor.wait_ge(dve.h, norm_hist[qi - 2]["tt"])
                                nc.tensor.wait_ge(dve.h, norm_hist[qi - 2]["rbfree"])
                            ins = nc.tensor.matmul(
                                o_ps[obank][:, :], v_sb[hb][:, kt * VST:(kt + 1) * VST],
                                e_pair[buf][:, idx * QB:(idx + 1) * QB],
                                start=kfirst, stop=klast)
                            mm.inc(ins)
                            ins = nc.tensor.matmul(
                                sum_ps[obank][0:1, :], ones_bf[:, 0:1],
                                e_pair[buf][:, idx * QB:(idx + 1) * QB],
                                start=kfirst, stop=klast)
                            mm.inc(ins)
                    av_free_hist.append(mm.n)
                qs_hist[qi] = mm.n
                pend.update(qi=qi, obank=obank, h=h, qb=qb, mm_last=mm.n)
                qi_total += 1
            last_head_mm[0] = qs_hist[qi_total - 1]
        norm_top()
        norm_recip()
        norm_rb()
        norm_tail()
        P.barrier_engines(extra=asem + kvsem + q2sem)
    nc.sync.wait_ge(st.h, st.n)


def build_attn(layer):
    P = Prog()
    nc = P.nc
    if layer == 0:
        QW, KW, MW, VW = 1024, 1024, 1024, 1024
    else:
        QW, KW, MW, VW = 1024, 256, 2688, 260
    io = {
        "xT": nc.dram_tensor("xT", [D, S], F32, kind="ExternalInput").ap(),
        "wqT": nc.dram_tensor("wqT", [D, QW], F32, kind="ExternalInput").ap(),
        "wkT": nc.dram_tensor("wkT", [D, KW], F32, kind="ExternalInput").ap(),
        "wvT": nc.dram_tensor("wvT", [D, KW], F32, kind="ExternalInput").ap(),
        "maskm": nc.dram_tensor("maskm", [128, MW], BF16, kind="ExternalInput").ap(),
        "attnout": nc.dram_tensor("attnout", [QW, S], BF16, kind="ExternalOutput").ap(),
        "qh": nc.dram_tensor("qh", [QW, S], BF16).ap(),
        "kh": nc.dram_tensor("kh", [KW, S], BF16).ap(),
        "vh": nc.dram_tensor("vh", [S, VW], BF16).ap(),
    }
    with P.scope():
        emit_attn(P, layer, io)
    return P




def emit_ffn0(P, io, pfx="f0_"):
    nc = P.nc
    v = nc.vector
    mm, act, dve, ld, st, lg = P.mm, P.act, P.dve, P.ld, P.st, P.lg
    T = TOK
    TBA = 512
    NTA = T // TBA
    TBB = 256
    NTB = T // TBB
    NE = 8
    toff = io.get("toff", 0)
    xT = io["xT"]; attnT = io["attnT"]; x2T = io["x2T"]
    woT = io["woT"]; down_w = io["down_w"]; down_b = io["down_b"]
    up_w = io["up_w"]; upcore_b = io["upcore_b"]
    router_w = io["router_w"]; router_b = io["router_b"]; cost_t = io["cost_t"]
    shared_w1 = io["shared_w1"]; shared_w2 = io["shared_w2"]; shared_b = io["shared_b"]
    core_w = io["core_w"]; ew1 = io["ew1"]; ew2 = io["ew2"]
    eb1 = io["eb1"]; eb2 = io["eb2"]; idn_h = io["idn_h"]; sel_h = io["sel_h"]
    x1h = io["x1h"]; ln2h = io["ln2h"]; gph = io["gph"]
    EHID = HID + [1024]

    ones_bf = P.sbuf(pfx + "ones_bf", [128, 1], BF16)
    ones_row = P.sbuf(pfx + "ones_row", [1, 128], F32)
    cst = P.sbuf(pfx + "cst", [128, 3], F32)
    idn = P.sbuf(pfx + "idn", [128, 128], F32)
    nc.gpsimd.memset(ones_bf[:, :], 1.0)
    nc.gpsimd.memset(ones_row[:, :], 1.0)
    nc.gpsimd.memset(cst[:, 0:1], EPS)
    P.gp.inc(nc.gpsimd.memset(cst[:, 1:2], -0.5))
    nc.sync.dma_start(out=idn[:, :], in_=idn_h[:, :]).then_inc(ld.h, 16); ld.n += 16
    sel_sb = P.sbuf(pfx + "sel_sb", [8, 1024], BF16)
    nc.sync.dma_start(out=sel_sb[:, :], in_=sel_h[:, :]).then_inc(ld.h, 16); ld.n += 16

    pool = P.psum(pfx + "pool", [128, 4096], F32)
    ln_st = pool[0:1, 0:1024]
    ln_bc = pool[:, 1024:1536]
    ps_m = pool[:, 1024:1536]
    ps_x = pool[:, 1536:2048]

    x_tb = P.sbuf(pfx + "x_tb", [128, FT * 512], F32)
    ln_tb = P.sbuf(pfx + "ln_tb", [128, FT * 512], BF16)
    rw_sb = P.sbuf(pfx + "rw_sb", [128, FT * 8], BF16)
    cb_sb = P.sbuf(pfx + "cb_sb", [128, T], BF16)
    ct_bf = P.sbuf(pfx + "ct_bf", [8, T], BF16)
    b2c_sb = P.sbuf(pfx + "b2c_sb", [8, 1024], BF16)
    ctot_sb = P.sbuf(pfx + "ctot_sb", [128, 8], F32)
    biases = P.sbuf(pfx + "biases", [128, 304], F32)
    db_c = biases[:, 0:8]
    ucb_c = biases[:, 8:40]
    fb_c = biases[:, 40:56]
    sb_c = biases[:, 56:72]
    b1_c = biases[:, 72:232]
    b2_c = biases[:, 232:296]
    small = P.sbuf(pfx + "small", [128, 64], F32)
    rbadj = small[0:1, 0:8]
    coef = P.sbuf(pfx + "coef", [128, 64], F32)
    gb1_sb = P.sbuf(pfx + "gb1_sb", [128, 160], BF16)
    ce_sb = P.sbuf(pfx + "ce_sb", [128, 64], F32)
    b2m = P.sbuf(pfx + "b2m", [128, 64], F32)

    def bar():
        P.barrier_engines()

    nc.gpsimd.wait_ge(lg.h, lg.n)
    nc.gpsimd.dma_start(out=rw_sb[:, :].rearrange("p (f c) -> p f c", f=FT),
                        in_=hbm_fm_view(router_w)).then_inc(lg.h, 16); lg.n += 16
    for dst, srcap in ((db_c, down_b), (ucb_c, upcore_b), (sb_c, shared_b),
                       (b1_c, eb1), (b2_c, eb2)):
        nc.sync.dma_start(out=dst, in_=srcap[:, :]).then_inc(ld.h, 16); ld.n += 16
    nc.sync.dma_start(out=small[0:1, 8:16], in_=router_b[:, :]).then_inc(ld.h, 16); ld.n += 16
    nc.sync.dma_start(out=small[0:1, 16:24], in_=cost_t[:, :]).then_inc(ld.h, 16); ld.n += 16
    bar()
    dve.inc(v.tensor_sub(rbadj, small[0:1, 8:16], small[0:1, 16:24]))
    dve.inc(v.tensor_add(fb_c, ucb_c[:, 0:16], ucb_c[:, 16:32]))
    bar()

    xv = hbm_fm_view(xT)
    av = hbm_fm_view(attnT)
    x1v = hbm_fm_view(x1h)
    lnv = hbm_fm_view(ln2h)

    P.mark("f0A")
    a_scope = P.scope(); a_scope.__enter__()
    big = P.sbuf(pfx + "big", [128, 3 * FT * 512], BF16)
    attn_tb = big[:, 0:FT * 512]
    ln_xb = big[:, FT * 512:2 * FT * 512]
    ln_sq = big[:, 2 * FT * 512:3 * FT * 512]
    lnscr = LnScratch(P, 512, name=pfx + "lns", xb=ln_xb, sq=ln_sq,
                      st_ps=pool[0:1, 0:1024], bc_ps=ln_bc)
    wo_sb = P.sbuf(pfx + "wo_sb", [128, FT * FT * 128], BF16)
    load_weight_fm(P, wo_sb, woT, FT, FT)
    psx = [pool[:, 1536:2048], pool[:, 2048:2560]]
    nc.tensor.wait_ge(lg.h, lg.n)
    add_cnt = {}
    gm = 0
    for tb in range(NTA):
        tsl = slice(tb * TBA, (tb + 1) * TBA)
        nc.sync.wait_ge(st.h, st.n)
        nc.sync.wait_ge(mm.h, mm.n)
        nc.sync.wait_ge(dve.h, dve.n)
        nc.sync.dma_start(out=x_tb[:, :].rearrange("p (f t) -> p f t", f=FT),
                          in_=offap(xv[:, :, tsl], toff)).then_inc(ld.h, 16); ld.n += 16
        nc.sync.dma_start(out=attn_tb.rearrange("p (f t) -> p f t", f=FT),
                          in_=offap(av[:, :, tsl], toff)).then_inc(ld.h, 16); ld.n += 16
        nc.tensor.wait_ge(ld.h, ld.n)
        for m in range(FT):
            if gm >= 2:
                nc.tensor.wait_ge(dve.h, add_cnt[gm - 2])
            for ft in range(FT):
                ins = nc.tensor.matmul(psx[gm % 2][:, :], w_tile(wo_sb, m, ft, FT),
                                       attn_tb[:, ft * 512:(ft + 1) * 512],
                                       start=(ft == 0), stop=(ft == FT - 1))
            mm.inc(ins)
            nc.vector.wait_ge(mm.h, mm.n)
            dve.inc(v.tensor_add(x_tb[:, m * 512:(m + 1) * 512],
                                 psx[gm % 2][:, :], x_tb[:, m * 512:(m + 1) * 512]))
            add_cnt[gm] = dve.n
            gm += 1
        nc.sync.wait_ge(dve.h, dve.n)
        nc.sync.dma_start(out=x1v[:, :, tsl],
                          in_=x_tb[:, :].rearrange("p (f t) -> p f t", f=FT)).then_inc(st.h, 16)
        st.n += 16
        emit_ln(P, lnscr,
                lambda ft: x_tb[:, ft * 512:(ft + 1) * 512],
                lambda ft: ln_tb[:, ft * 512:(ft + 1) * 512],
                512, (ones_bf, ones_row, cst))
        nc.sync.wait_ge(dve.h, dve.n)
        nc.sync.dma_start(out=lnv[:, :, tsl],
                          in_=ln_tb[:, :].rearrange("p (f t) -> p f t", f=FT)).then_inc(st.h, 16)
        st.n += 16
        for e in (nc.vector, nc.tensor, nc.scalar):
            e.wait_ge(st.h, st.n)
    bar()
    a_scope.__exit__(None, None, None)

    P.mark("f0B")
    b_scope = P.scope(); b_scope.__enter__()
    down_sb = P.sbuf(pfx + "down_sb", [128, FT * 8 * 128], BF16)
    g_sb = P.sbuf(pfx + "g_sb", [128, 8 * T], BF16)
    load_weight_fm(P, down_sb, down_w, 8, FT)
    nc.tensor.wait_ge(lg.h, lg.n)
    for tb in range(NTB):
        tsl = slice(tb * TBB, (tb + 1) * TBB)
        nc.sync.wait_ge(mm.h, mm.n)
        nc.sync.dma_start(out=ln_tb[:, 0:FT * TBB].rearrange("p (f t) -> p f t", f=FT),
                          in_=lnv[:, :, tsl]).then_inc(ld.h, 16); ld.n += 16
        nc.tensor.wait_ge(ld.h, ld.n)
        for m in range(8):
            nc.tensor.wait_ge(act.h, act.n)
            for ft in range(FT):
                ins = nc.tensor.matmul(ps_x[:, 0:TBB], w_tile(down_sb, m, ft, 8),
                                       ln_tb[:, ft * TBB:(ft + 1) * TBB],
                                       start=(ft == 0), stop=(ft == FT - 1))
            mm.inc(ins)
            nc.scalar.wait_ge(mm.h, mm.n)
            act.inc(nc.scalar.activation(g_sb[:, m * T + tb * TBB: m * T + (tb + 1) * TBB],
                                         ps_x[:, 0:TBB], GELU, bias=db_c[:, m:m + 1]))
    bar()

    P.mark("f0C")
    for tt in range(T // 128):
        tbb = (tt * 128) // TBB
        o = (tt * 128) % TBB
        if o == 0:
            nc.sync.wait_ge(mm.h, mm.n)
            nc.sync.dma_start(out=ln_tb[:, 0:FT * TBB].rearrange("p (f t) -> p f t", f=FT),
                              in_=lnv[:, :, tbb * TBB:(tbb + 1) * TBB]).then_inc(ld.h, 16)
            ld.n += 16
        nc.tensor.wait_ge(ld.h, ld.n)
        nc.tensor.wait_ge(act.h, act.n)
        for ft in range(FT):
            nc.tensor.matmul(ps_m[:, 0:8],
                             ln_tb[:, ft * TBB + o: ft * TBB + o + 128],
                             rw_sb[:, ft * 8:(ft + 1) * 8],
                             start=(ft == 0), stop=False)
        mm.inc(nc.tensor.matmul(ps_m[:, 0:8], ones_row[0:1, :], rbadj,
                                start=False, stop=True))
        nc.vector.wait_ge(mm.h, mm.n)
        mx = coef[:, 8:9]; sm = coef[:, 9:10]; rr = coef[:, 10:11]
        m1 = coef[:, 11:12]; m2 = coef[:, 12:13]; dd = coef[:, 13:14]; g1 = coef[:, 14:15]
        e_r = coef[:, 16:24]; sel1 = coef[:, 24:32]; msk = coef[:, 32:40]
        sel2 = coef[:, 40:48]; cc = coef[:, 48:56]
        dve.inc(v.reduce_max(mx, ps_m[:, 0:8], mybir.AxisListType.X, negate=True))
        nc.scalar.wait_ge(dve.h, dve.n)
        act.inc(nc.scalar.activation(e_r, ps_m[:, 0:8], AF.Exp, bias=mx))
        nc.vector.wait_ge(act.h, act.n)
        dve.inc(v.reduce_sum(sm, e_r, mybir.AxisListType.X)); v.wait_ge(dve.h, dve.n)
        dve.inc(v.reciprocal(rr, sm))
        dve.inc(v.reduce_max(m1, e_r, mybir.AxisListType.X)); v.wait_ge(dve.h, dve.n)
        dve.inc(v.tensor_scalar(sel1, e_r, m1, None, ALU.is_equal)); v.wait_ge(dve.h, dve.n)
        dve.inc(v.scalar_tensor_tensor(msk, sel1, -1e30, e_r, ALU.mult, ALU.add)); v.wait_ge(dve.h, dve.n)
        dve.inc(v.reduce_max(m2, msk, mybir.AxisListType.X)); v.wait_ge(dve.h, dve.n)
        dve.inc(v.tensor_scalar(sel2, msk, m2, None, ALU.is_equal))
        dve.inc(v.tensor_sub(dd, m1, m2)); v.wait_ge(dve.h, dve.n)
        dve.inc(v.tensor_tensor(dd, dd, rr, ALU.mult)); nc.scalar.wait_ge(dve.h, dve.n)
        act.inc(nc.scalar.activation(g1, dd, AF.Sigmoid))
        nc.vector.wait_ge(act.h, act.n)
        dve.inc(v.tensor_scalar(cc, sel1, g1, None, ALU.mult))
        dve.inc(v.tensor_scalar(msk, sel2, g1, None, ALU.mult)); v.wait_ge(dve.h, dve.n)
        dve.inc(v.tensor_sub(msk, sel2, msk)); v.wait_ge(dve.h, dve.n)
        dve.inc(v.tensor_add(cc, cc, msk)); nc.tensor.wait_ge(dve.h, dve.n)
        mm.inc(nc.tensor.transpose(ps_m[0:8, 128:256], cc, idn[:, :]))
        nc.scalar.wait_ge(mm.h, mm.n)
        act.inc(nc.scalar.copy(ct_bf[:, tt * 128:(tt + 1) * 128], ps_m[0:8, 128:256]))
        nc.vector.wait_ge(act.h, act.n)
    bar()

    P.mark("f0D")
    act.inc(nc.scalar.activation(gb1_sb[:, 0:160], b1_c, GELU))
    bar()
    hofs = [0]
    for e in range(NE):
        hofs.append(hofs[-1] + HID[e] // 128)
    with P.scope():
        w2big = P.sbuf(pfx + "w2big", [128, 32 * 1024], BF16)
        for e in range(NE):
            nh = HID[e] // 128
            nc.gpsimd.wait_ge(lg.h, lg.n)
            nc.gpsimd.wait_ge(mm.h, mm.n)
            nc.gpsimd.dma_start(
                out=w2big[:, 0:nh * 1024].rearrange("p (k c) -> p k c", k=nh),
                in_=ew2[e].rearrange("(k p) c -> p k c", p=128)).then_inc(lg.h, 16)
            lg.n += 16
            nc.tensor.wait_ge(lg.h, lg.n)
            nc.tensor.wait_ge(act.h, act.n)
            for m2 in range(8):
                for hk in range(nh):
                    ins = nc.tensor.matmul(ps_m[:, 256 + m2:257 + m2],
                                           w2big[:, hk * 1024 + m2 * 128: hk * 1024 + (m2 + 1) * 128],
                                           gb1_sb[:, hofs[e] + hk: hofs[e] + hk + 1],
                                           start=(hk == 0), stop=(hk == nh - 1))
                mm.inc(ins)
            nc.vector.wait_ge(mm.h, mm.n)
            dve.inc(v.tensor_add(ce_sb[:, e:e + 57:8], ps_m[:, 256:264], b2_c[:, e:e + 57:8]))
            nc.tensor.wait_ge(dve.h, dve.n)
    bar()
    for m2 in range(8):
        dve.inc(v.reduce_sum(ctot_sb[:, m2:m2 + 1], ce_sb[:, m2 * 8:(m2 + 1) * 8],
                             mybir.AxisListType.X))
    v.wait_ge(dve.h, dve.n)
    dve.inc(v.scalar_tensor_tensor(ctot_sb[:, :], sb_c[:, 8:16], 0.1, ctot_sb[:, :],
                                   ALU.mult, ALU.add))
    dve.inc(v.tensor_sub(b2m[:, :], b2_c, ce_sb[:, :]))
    nc.tensor.wait_ge(dve.h, dve.n)
    for m2 in range(8):
        colap = b2m[:, m2 * 8:(m2 + 1) * 8]
        nc.tensor.wait_ge(act.h, act.n)
        mm.inc(nc.tensor.transpose(ps_m[0:8, 128:256], colap, idn[:, :]))
        nc.scalar.wait_ge(mm.h, mm.n)
        act.inc(nc.scalar.copy(b2c_sb[:, m2 * 128:(m2 + 1) * 128], ps_m[0:8, 128:256]))
    bar()

    P.mark("f0E1")
    with P.scope():
        TBE = 512
        NTE = T // TBE
        wsB2 = P.sbuf(pfx + "wsB2", [128, 2 * 8 * 128], BF16)
        gpw = P.sbuf(pfx + "gpw", [128, 2 * T], BF16)
        a_ps = [pool[:, 0:512], pool[:, 512:1024]]
        w1sem = [Sem(nc, pfx + f"w1sem{i}") for i in range(2)]
        gpsem = [Sem(nc, pfx + f"gpsem{i}") for i in range(2)]
        exphist = []
        mulhist = []
        wload = {}
        ehm = [(e, hm) for e in range(9) for hm in range((HID + [1024])[e] // 128)]

        def load_w1(idx):
            e, hm = ehm[idx]
            sl = idx % 2
            w1src = ew1[e] if e < NE else shared_w1
            nc.gpsimd.wait_ge(lg.h, lg.n)
            if idx >= 2:
                nc.gpsimd.wait_ge(mm.h, wuse[idx - 2])
            d = nc.gpsimd.dma_start(
                out=wsB2[:, sl * 1024:(sl + 1) * 1024].rearrange("p (k c) -> p k c", k=8),
                in_=w1src.rearrange("(k p) c -> p k c", p=128)[:, :, hm * 128:(hm + 1) * 128])
            w1sem[sl].inc(d, 16)
            wload[idx] = (sl, w1sem[sl].n)

        wuse = {}
        load_w1(0)
        load_w1(1)
        eprev = -1
        for idx, (e, hm) in enumerate(ehm):
            if e != eprev:
                eprev = e
                if e < NE:
                    for tb4 in range(NTA):
                        nc.tensor.wait_ge(act.h, act.n)
                        nc.tensor.wait_ge(dve.h, dve.n)
                        mm.inc(nc.tensor.matmul(ps_m[:, 0:512], sel_sb[:, e * 128:(e + 1) * 128],
                                                ct_bf[:, tb4 * 512:(tb4 + 1) * 512],
                                                start=True, stop=True))
                        nc.scalar.wait_ge(mm.h, mm.n)
                        act.inc(nc.scalar.copy(cb_sb[:, tb4 * 512:(tb4 + 1) * 512], ps_m[:, 0:512]))
                    nc.vector.wait_ge(act.h, act.n)
            sl, wtgt = wload[idx]
            gsl = idx % 2
            bias = (b1_c[:, hofs[e] + hm: hofs[e] + hm + 1] if e < NE
                    else sb_c[:, hm:hm + 1])
            nc.tensor.wait_ge(w1sem[sl].h, wtgt)
            for tb in range(NTE):
                bank = tb % 2
                if len(exphist) >= 2:
                    nc.tensor.wait_ge(act.h, exphist[-2])
                for kk in range(8):
                    ins = nc.tensor.matmul(a_ps[bank][:, :],
                                           wsB2[:, sl * 1024 + kk * 128: sl * 1024 + (kk + 1) * 128],
                                           g_sb[:, kk * T + tb * TBE: kk * T + (tb + 1) * TBE],
                                           start=(kk == 0), stop=(kk == 7))
                mm.inc(ins)
                nc.scalar.wait_ge(mm.h, mm.n)
                if tb == 0 and idx >= 2:
                    nc.scalar.wait_ge(gpsem[gsl].h, gpsem[gsl].n)
                act.inc(nc.scalar.activation(gpw[:, gsl * T + tb * TBE: gsl * T + (tb + 1) * TBE],
                                             a_ps[bank][:, :], GELU, bias=bias))
                exphist.append(act.n)
                nc.vector.wait_ge(act.h, act.n)
                if e < NE:
                    dve.inc(v.tensor_tensor(gpw[:, gsl * T + tb * TBE: gsl * T + (tb + 1) * TBE],
                                            gpw[:, gsl * T + tb * TBE: gsl * T + (tb + 1) * TBE],
                                            cb_sb[:, tb * TBE:(tb + 1) * TBE], ALU.mult))
                else:
                    dve.inc(v.tensor_scalar_mul(gpw[:, gsl * T + tb * TBE: gsl * T + (tb + 1) * TBE],
                                                gpw[:, gsl * T + tb * TBE: gsl * T + (tb + 1) * TBE], 0.1))
            wuse[idx] = mm.n
            if idx + 2 < len(ehm):
                load_w1(idx + 2)
            nc.sync.wait_ge(dve.h, dve.n)
            d = nc.sync.dma_start(out=gph[e][hm * 128:(hm + 1) * 128, :],
                                  in_=gpw[:, gsl * T:(gsl + 1) * T])
            gpsem[gsl].inc(d, 16)
        P.barrier_engines(extra=gpsem + w1sem)
    b_scope.__exit__(None, None, None)

    P.mark("f0E2")
    with P.scope():
        TBS = 512
        NTS = T // TBS
        GRP = 6
        S_sb = P.sbuf(pfx + "S_sb", [128, 8 * T], F32)
        e2_scope = P.scope(); e2_scope.__enter__()
        w2s = P.sbuf(pfx + "w2s", [128, 2 * GRP * 1024], BF16)
        gprd = P.sbuf(pfx + "gprd", [128, 2 * GRP * T], BF16)
        w2sem = [Sem(nc, pfx + f"w2sem{i}") for i in range(2)]
        gdsem = [Sem(nc, pfx + f"gdsem{i}") for i in range(2)]
        psets = [pool[:, 0:2048], pool[:, 2048:4096]]

        allhk = [(e, hk) for e in range(9) for hk in range((HID + [1024])[e] // 128)]
        groups = [allhk[i:i + GRP] for i in range(0, len(allhk), GRP)]
        grpuse = {}
        evhist = {}

        def load_grp(gi):
            sl = gi % 2
            if gi >= 2:
                nc.gpsimd.wait_ge(mm.h, grpuse[gi - 2])
                nc.sync.wait_ge(mm.h, grpuse[gi - 2])
            for j, (e, hk) in enumerate(groups[gi]):
                w2src = ew2[e] if e < NE else shared_w2
                nc.gpsimd.wait_ge(lg.h, lg.n)
                d = nc.gpsimd.dma_start(
                    out=w2s[:, (sl * GRP + j) * 1024:(sl * GRP + j + 1) * 1024],
                    in_=w2src[hk * 128:(hk + 1) * 128, :])
                w2sem[sl].inc(d, 16)
                d = nc.sync.dma_start(
                    out=gprd[:, (sl * GRP + j) * T:(sl * GRP + j + 1) * T],
                    in_=gph[e][hk * 128:(hk + 1) * 128, :])
                gdsem[sl].inc(d, 16)
            gload[gi] = (sl, w2sem[sl].n, gdsem[sl].n)

        gload = {}
        load_grp(0)
        hidx = 0
        for gi, grp in enumerate(groups):
            if gi + 1 < len(groups):
                load_grp(gi + 1)
            sl, wt, gt = gload[gi]
            nc.tensor.wait_ge(w2sem[sl].h, wt)
            nc.tensor.wait_ge(gdsem[sl].h, gt)
            for tb in range(NTS):
                for hf in range(2):
                    pset = psets[hf]
                    nc.tensor.wait_ge(dve.h, evhist.get(hidx - 2, 0))
                    for j in range(len(grp)):
                        for mi in range(4):
                            m2 = hf * 4 + mi
                            ins = nc.tensor.matmul(
                                pset[:, mi * 512:(mi + 1) * 512],
                                w2s[:, (sl * GRP + j) * 1024 + m2 * 128:
                                    (sl * GRP + j) * 1024 + (m2 + 1) * 128],
                                gprd[:, (sl * GRP + j) * T + tb * TBS:
                                     (sl * GRP + j) * T + (tb + 1) * TBS],
                                start=(j == 0), stop=(j == len(grp) - 1))
                    mm.inc(ins)
                    nc.vector.wait_ge(mm.h, mm.n)
                    for mi in range(4):
                        m2 = hf * 4 + mi
                        dst = S_sb[:, m2 * T + tb * TBS: m2 * T + (tb + 1) * TBS]
                        if gi == 0:
                            ins = v.tensor_copy(dst, pset[:, mi * 512:(mi + 1) * 512])
                        else:
                            ins = v.tensor_tensor(dst, pset[:, mi * 512:(mi + 1) * 512],
                                                  dst, ALU.add)
                    dve.inc(ins)
                    evhist[hidx] = dve.n
                    hidx += 1
            grpuse[gi] = mm.n
        P.barrier_engines(extra=w2sem + gdsem)
        for tb in range(NTA):
            tsl = slice(tb * TBA, (tb + 1) * TBA)
            for m2 in range(8):
                ins = nc.tensor.matmul(pool[:, (m2 % 4) * 512:(m2 % 4 + 1) * 512],
                                       b2c_sb[:, m2 * 128:(m2 + 1) * 128],
                                       ct_bf[:, tsl], start=True, stop=True)
                mm.inc(ins)
                nc.vector.wait_ge(mm.h, mm.n)
                ins2 = v.scalar_tensor_tensor(
                    S_sb[:, m2 * T + tb * TBA: m2 * T + (tb + 1) * TBA],
                    pool[:, (m2 % 4) * 512:(m2 % 4 + 1) * 512],
                    ctot_sb[:, m2:m2 + 1],
                    S_sb[:, m2 * T + tb * TBA: m2 * T + (tb + 1) * TBA],
                    ALU.add, ALU.add)
                dve.inc(ins2)
                nc.tensor.wait_ge(dve.h, dve.n)
        bar()

        e2_scope.__exit__(None, None, None)
        P.mark("f0F")
        fevac = {}
        gln_tb = P.sbuf(pfx + "gln_tb", [128, FT * 512], BF16)
        smid_bf = P.sbuf(pfx + "smid", [128, 8 * 512], BF16)
        wsA2 = P.sbuf(pfx + "wsA2", [128, 2 * FT * 128], BF16)
        wsU2 = P.sbuf(pfx + "wsU2", [128, 2 * 8 * 128], BF16)
        casem = [Sem(nc, pfx + f"casem{i}") for i in range(2)]
        cusem = [Sem(nc, pfx + f"cusem{i}") for i in range(2)]
        fuse = {}
        fload = {}

        def load_f(fi):
            tb, m = divmod(fi, FT)
            sl = fi % 2
            nc.gpsimd.wait_ge(lg.h, lg.n)
            if fi >= 2:
                nc.gpsimd.wait_ge(mm.h, fuse[fi - 2])
            d = nc.gpsimd.dma_start(
                out=wsA2[:, sl * 2048:(sl + 1) * 2048].rearrange("p (f c) -> p f c", f=FT),
                in_=hbm_fm_view(core_w)[:, :, m * 128:(m + 1) * 128])
            casem[sl].inc(d, 16)
            nc.gpsimd.wait_ge(lg.h, lg.n)
            d = nc.gpsimd.dma_start(
                out=wsU2[:, sl * 1024:(sl + 1) * 1024].rearrange("p (k c) -> p k c", k=8),
                in_=up_w.rearrange("(k p) c -> p k c", p=128)[:, :, m * 128:(m + 1) * 128])
            cusem[sl].inc(d, 16)
            fload[fi] = (sl, casem[sl].n, cusem[sl].n)

        load_f(0)
        load_f(1)
        for tb in range(NTA):
            tsl = slice(tb * TBA, (tb + 1) * TBA)
            nc.sync.wait_ge(st.h, st.n)
            nc.sync.wait_ge(dve.h, dve.n)
            nc.sync.dma_start(out=ln_tb[:, :].rearrange("p (f t) -> p f t", f=FT),
                              in_=lnv[:, :, tsl]).then_inc(ld.h, 16); ld.n += 16
            nc.sync.dma_start(out=x_tb[:, :].rearrange("p (f t) -> p f t", f=FT),
                              in_=x1v[:, :, tsl]).then_inc(ld.h, 16); ld.n += 16
            nc.scalar.wait_ge(ld.h, ld.n)
            nc.scalar.wait_ge(dve.h, dve.n)
            for ft in range(FT):
                ins = nc.scalar.activation(gln_tb[:, ft * 512:(ft + 1) * 512],
                                           ln_tb[:, ft * 512:(ft + 1) * 512], GELU)
            act.inc(ins)
            nc.vector.wait_ge(act.h, act.n)
            for m2 in range(8):
                ins = v.tensor_copy(smid_bf[:, m2 * 512:(m2 + 1) * 512],
                                    S_sb[:, m2 * T + tb * TBA: m2 * T + (tb + 1) * TBA])
            dve.inc(ins)
            for e_ in (nc.tensor,):
                e_.wait_ge(act.h, act.n); e_.wait_ge(dve.h, dve.n)
            for m in range(FT):
                fi = tb * FT + m
                sl, ca, cu = fload[fi]
                nc.tensor.wait_ge(casem[sl].h, ca)
                nc.tensor.wait_ge(cusem[sl].h, cu)
                bank = fi % 2
                pf = pool[:, (2 + bank) * 512:(3 + bank) * 512]
                if fi >= 2:
                    nc.tensor.wait_ge(dve.h, fevac[fi - 2])
                for kk in range(8):
                    nc.tensor.matmul(pf, wsU2[:, sl * 1024 + kk * 128: sl * 1024 + (kk + 1) * 128],
                                     smid_bf[:, kk * 512:(kk + 1) * 512],
                                     start=(kk == 0), stop=False)
                for ft in range(FT):
                    ins = nc.tensor.matmul(pf, wsA2[:, sl * 2048 + ft * 128: sl * 2048 + (ft + 1) * 128],
                                           gln_tb[:, ft * 512:(ft + 1) * 512],
                                           start=False, stop=(ft == FT - 1))
                mm.inc(ins)
                fuse[fi] = mm.n
                if fi + 2 < NTA * FT:
                    load_f(fi + 2)
                nc.vector.wait_ge(mm.h, mm.n)
                dve.inc(v.scalar_tensor_tensor(x_tb[:, m * 512:(m + 1) * 512],
                                               pf, fb_c[:, m:m + 1],
                                               x_tb[:, m * 512:(m + 1) * 512],
                                               ALU.add, ALU.add))
                fevac[fi] = dve.n
            nc.sync.wait_ge(dve.h, dve.n)
            nc.sync.dma_start(out=offap(hbm_fm_view(x2T)[:, :, tsl], toff),
                              in_=x_tb[:, :].rearrange("p (f t) -> p f t", f=FT)).then_inc(st.h, 16)
            st.n += 16
        P.barrier_engines(extra=casem + cusem)
    nc.sync.wait_ge(st.h, st.n)


def build_ffn0():
    P = Prog()
    nc = P.nc
    NE = 8
    io = {
        "xT": nc.dram_tensor("xT", [D, TOK], F32, kind="ExternalInput").ap(),
        "attnT": nc.dram_tensor("attnT", [D, TOK], BF16, kind="ExternalInput").ap(),
        "woT": nc.dram_tensor("woT", [D, D], F32, kind="ExternalInput").ap(),
        "down_w": nc.dram_tensor("down_w", [D, 1024], F32, kind="ExternalInput").ap(),
        "down_b": nc.dram_tensor("down_b", [128, 8], F32, kind="ExternalInput").ap(),
        "up_w": nc.dram_tensor("up_w", [1024, D], F32, kind="ExternalInput").ap(),
        "upcore_b": nc.dram_tensor("upcore_b", [128, 32], F32, kind="ExternalInput").ap(),
        "router_w": nc.dram_tensor("router_w", [D, 8], F32, kind="ExternalInput").ap(),
        "router_b": nc.dram_tensor("router_b", [1, 8], F32, kind="ExternalInput").ap(),
        "cost_t": nc.dram_tensor("cost_t", [1, 8], F32, kind="ExternalInput").ap(),
        "shared_w1": nc.dram_tensor("shared_w1", [1024, 1024], F32, kind="ExternalInput").ap(),
        "shared_w2": nc.dram_tensor("shared_w2", [1024, 1024], F32, kind="ExternalInput").ap(),
        "shared_b": nc.dram_tensor("shared_b", [128, 16], F32, kind="ExternalInput").ap(),
        "core_w": nc.dram_tensor("core_w", [D, D], F32, kind="ExternalInput").ap(),
        "ew1": [nc.dram_tensor(f"e{e}_w1", [1024, HID[e]], F32, kind="ExternalInput").ap() for e in range(NE)],
        "ew2": [nc.dram_tensor(f"e{e}_w2", [HID[e], 1024], F32, kind="ExternalInput").ap() for e in range(NE)],
        "eb1": nc.dram_tensor("eb1", [128, 160], F32, kind="ExternalInput").ap(),
        "eb2": nc.dram_tensor("eb2", [128, 64], F32, kind="ExternalInput").ap(),
        "idn_h": nc.dram_tensor("idn_h", [128, 128], F32, kind="ExternalInput").ap(),
        "sel_h": nc.dram_tensor("sel_h", [8, 1024], BF16, kind="ExternalInput").ap(),
        "x2T": nc.dram_tensor("x2T", [D, TOK], F32, kind="ExternalOutput").ap(),
        "x1h": nc.dram_tensor("x1h", [D, TOK], F32).ap(),
        "ln2h": nc.dram_tensor("ln2h", [D, TOK], BF16).ap(),
        "gph": [nc.dram_tensor(f"gph{e}", [(HID + [1024])[e], TOK], BF16).ap() for e in range(9)],
    }
    with P.scope():
        emit_ffn0(P, io)
    return P



def emit_ffn1(P, io, pfx="g_"):
    nc = P.nc
    v = nc.vector
    mm, act, dve, ld, st, lg = P.mm, P.act, P.dve, P.ld, P.st, P.lg
    T = TOK
    TBA = 512
    NTA = T // TBA
    NH = 44
    toff = io.get("toff", 0)
    x2T = io["x2T"]; attnT = io["attnT"]; wo = io["wo"]
    wqgT = io["wqgT"]; wquT = io["wquT"]; wqdT = io["wqdT"]
    sg_c = io["sg_c"]; su_c = io["su_c"]; sd_c = io["sd_c"]
    outT = io["outT"]; x3h = io["x3h"]; ln3h = io["ln3h"]; prodh = io["prodh"]

    ones_bf = P.sbuf(pfx + "ones_bf", [128, 1], BF16)
    ones_row = P.sbuf(pfx + "ones_row", [1, 128], F32)
    cst = P.sbuf(pfx + "cst", [128, 3], F32)
    nc.gpsimd.memset(ones_bf[:, :], 1.0)
    nc.gpsimd.memset(ones_row[:, :], 1.0)
    nc.gpsimd.memset(cst[:, 0:1], EPS)
    P.gp.inc(nc.gpsimd.memset(cst[:, 1:2], -0.5))

    pool = P.psum(pfx + "pool", [128, 4096], F32)
    ln_bc = pool[:, 1024:1536]
    ps_o = pool[:, 1536:2048]
    ps_u = pool[:, 1024:1536]

    x_tb = P.sbuf(pfx + "x_tb", [128, FT * 512], F32)
    ln3_sb = P.sbuf(pfx + "ln3_sb", [128, FT * T], BF16)
    scl = P.sbuf(pfx + "scl", [128, 2 * NH + FT], F32)
    sg_s = scl[:, 0:NH]
    su_s = scl[:, NH:2 * NH]
    sd_s = scl[:, 2 * NH:2 * NH + FT]

    def bar():
        P.barrier_engines()

    nc.sync.dma_start(out=sg_s, in_=sg_c[:, :]).then_inc(ld.h, 16); ld.n += 16
    nc.sync.dma_start(out=su_s, in_=su_c[:, :]).then_inc(ld.h, 16); ld.n += 16
    nc.sync.dma_start(out=sd_s, in_=sd_c[:, :]).then_inc(ld.h, 16); ld.n += 16
    bar()

    xv = hbm_fm_view(x2T)
    av = hbm_fm_view(attnT)
    x3v = hbm_fm_view(x3h)
    lnv = hbm_fm_view(ln3h)

    P.mark("g1A")
    aevac = {}
    bact = {}
    pmul = {}
    with P.scope():
        big = P.sbuf(pfx + "big", [128, 3 * FT * 512], BF16)
        attn_tb = big[:, 0:FT * 512]
        lnscr = LnScratch(P, 512, name=pfx + "lns", xb=big[:, FT * 512:2 * FT * 512],
                          sq=big[:, 2 * FT * 512:3 * FT * 512],
                          st_ps=pool[0:1, 0:1024], bc_ps=ln_bc)
        wsA = P.sbuf(pfx + "wsA", [128, 2 * FT * 128], BF16)
        wosem = [Sem(nc, pfx + f"wosem{i}") for i in range(2)]
        wouse = {}
        woload = {}

        def load_wo(fi):
            sl = fi % 2
            m = fi % FT
            nc.gpsimd.wait_ge(lg.h, lg.n)
            if fi >= 2:
                nc.gpsimd.wait_ge(mm.h, wouse[fi - 2])
            d = nc.gpsimd.dma_start(
                out=wsA[:, sl * 2048:(sl + 1) * 2048].rearrange("p (f c) -> p f c", f=FT),
                in_=hbm_fm_view(wo)[:, :, m * 128:(m + 1) * 128])
            wosem[sl].inc(d, 16)
            woload[fi] = (sl, wosem[sl].n)

        load_wo(0)
        load_wo(1)
        for tb in range(NTA):
            tsl = slice(tb * TBA, (tb + 1) * TBA)
            nc.sync.wait_ge(st.h, st.n)
            nc.sync.wait_ge(mm.h, mm.n)
            nc.sync.wait_ge(dve.h, dve.n)
            nc.sync.wait_ge(act.h, act.n)
            nc.sync.dma_start(out=x_tb[:, :].rearrange("p (f t) -> p f t", f=FT),
                              in_=offap(xv[:, :, tsl], toff)).then_inc(ld.h, 16); ld.n += 16
            nc.sync.dma_start(out=attn_tb.rearrange("p (f t) -> p f t", f=FT),
                              in_=offap(av[:, :, tsl], toff)).then_inc(ld.h, 16); ld.n += 16
            for m in range(FT):
                fi = tb * FT + m
                sl, wt = woload[fi]
                bank = fi % 2
                ps_b = pool[:, (2 + bank) * 512:(3 + bank) * 512]
                for e in (nc.tensor,):
                    e.wait_ge(wosem[sl].h, wt); e.wait_ge(ld.h, ld.n)
                    e.wait_ge(dve.h, aevac.get(fi - 2, 0))
                for ft in range(FT):
                    ins = nc.tensor.matmul(ps_b, wsA[:, sl * 2048 + ft * 128: sl * 2048 + (ft + 1) * 128],
                                           attn_tb[:, ft * 512:(ft + 1) * 512],
                                           start=(ft == 0), stop=(ft == FT - 1))
                mm.inc(ins)
                wouse[fi] = mm.n
                if fi + 2 < NTA * FT:
                    load_wo(fi + 2)
                nc.vector.wait_ge(mm.h, mm.n)
                dve.inc(v.tensor_add(x_tb[:, m * 512:(m + 1) * 512],
                                     ps_b, x_tb[:, m * 512:(m + 1) * 512]))
                aevac[fi] = dve.n
            nc.sync.wait_ge(dve.h, dve.n)
            nc.sync.dma_start(out=x3v[:, :, tsl],
                              in_=x_tb[:, :].rearrange("p (f t) -> p f t", f=FT)).then_inc(st.h, 16)
            st.n += 16
            emit_ln(P, lnscr,
                    lambda ft: x_tb[:, ft * 512:(ft + 1) * 512],
                    lambda ft: ln3_sb[:, ft * T + tb * TBA: ft * T + (tb + 1) * TBA],
                    512, (ones_bf, ones_row, cst))
            nc.sync.wait_ge(dve.h, dve.n)
        P.barrier_engines(extra=wosem)

    P.mark("g1B")
    with P.scope():
        wgb = P.sbuf(pfx + "wgb", [128, 2 * FT * 128], BF16)
        wub = P.sbuf(pfx + "wub", [128, 2 * FT * 128], BF16)
        prw = P.sbuf(pfx + "prw", [128, 2 * T], BF16)
        pra = P.sbuf(pfx + "pra", [128, 2 * 512], BF16)
        wgsem = [Sem(nc, pfx + f"wgsem{i}") for i in range(2)]
        prsem = [Sem(nc, pfx + f"prsem{i}") for i in range(2)]
        guse = {}
        gldd = {}

        def load_gu(hm):
            sl = hm % 2
            nc.gpsimd.wait_ge(lg.h, lg.n)
            if hm >= 2:
                nc.gpsimd.wait_ge(mm.h, guse[hm - 2])
                nc.gpsimd.wait_ge(dve.h, gsub[hm - 2])
            d = nc.gpsimd.dma_start(
                out=wgb[:, sl * 2048:(sl + 1) * 2048].rearrange("p (f c) -> p f c", f=FT),
                in_=wqgT.rearrange("(k p) c -> p k c", p=128)[:, :, hm * 128:(hm + 1) * 128])
            wgsem[sl].inc(d, 16)
            nc.gpsimd.wait_ge(lg.h, lg.n)
            d = nc.gpsimd.dma_start(
                out=wub[:, sl * 2048:(sl + 1) * 2048].rearrange("p (f c) -> p f c", f=FT),
                in_=wquT.rearrange("(k p) c -> p k c", p=128)[:, :, hm * 128:(hm + 1) * 128])
            wgsem[sl].inc(d, 16)
            gldd[hm] = (sl, wgsem[sl].n)

        gsub = {}
        load_gu(0)
        load_gu(1)
        for hm in range(NH):
            sl, wt = gldd[hm]
            nc.vector.wait_ge(wgsem[sl].h, wt)
            nc.vector.wait_ge(mm.h, guse.get(hm - 2, 0))
            dve.inc(v.tensor_scalar_add(wgb[:, sl * 2048:(sl + 1) * 2048],
                                        wgb[:, sl * 2048:(sl + 1) * 2048], -128.0))
            dve.inc(v.tensor_scalar_add(wub[:, sl * 2048:(sl + 1) * 2048],
                                        wub[:, sl * 2048:(sl + 1) * 2048], -128.0))
            gsub[hm] = dve.n
            nc.tensor.wait_ge(dve.h, gsub[hm])
            for tb in range(NTA):
                bank = tb % 2
                ps_g = pool[:, bank * 512:(bank + 1) * 512]
                ps_u = pool[:, (2 + bank) * 512:(3 + bank) * 512]
                nc.tensor.wait_ge(act.h, bact.get(('g', bank), 0))
                for ft in range(FT):
                    ins = nc.tensor.matmul(ps_g, wgb[:, sl * 2048 + ft * 128: sl * 2048 + (ft + 1) * 128],
                                           ln3_sb[:, ft * T + tb * TBA: ft * T + (tb + 1) * TBA],
                                           start=(ft == 0), stop=(ft == FT - 1))
                mm.inc(ins)
                gmm = mm.n
                nc.tensor.wait_ge(act.h, bact.get(('u', bank), 0))
                for ft in range(FT):
                    ins = nc.tensor.matmul(ps_u, wub[:, sl * 2048 + ft * 128: sl * 2048 + (ft + 1) * 128],
                                           ln3_sb[:, ft * T + tb * TBA: ft * T + (tb + 1) * TBA],
                                           start=(ft == 0), stop=(ft == FT - 1))
                mm.inc(ins)
                nc.scalar.wait_ge(mm.h, gmm)
                nc.scalar.wait_ge(dve.h, pmul.get(('pra', bank), 0))
                act.inc(nc.scalar.activation(pra[:, bank * 512:(bank + 1) * 512], ps_g, AF.Silu,
                                             scale=sg_s[:, hm:hm + 1]))
                bact[('g', bank)] = act.n
                nc.scalar.wait_ge(mm.h, mm.n)
                if tb == 0:
                    nc.scalar.wait_ge(prsem[hm % 2].h, prsem[hm % 2].n)
                act.inc(nc.scalar.activation(prw[:, (hm % 2) * T + tb * TBA:(hm % 2) * T + (tb + 1) * TBA],
                                             ps_u, AF.Copy, scale=su_s[:, hm:hm + 1]))
                bact[('u', bank)] = act.n
                nc.vector.wait_ge(act.h, act.n)
                dve.inc(v.tensor_tensor(prw[:, (hm % 2) * T + tb * TBA:(hm % 2) * T + (tb + 1) * TBA],
                                        pra[:, bank * 512:(bank + 1) * 512],
                                        prw[:, (hm % 2) * T + tb * TBA:(hm % 2) * T + (tb + 1) * TBA],
                                        ALU.mult))
                pmul[('pra', bank)] = dve.n
            guse[hm] = mm.n
            if hm + 2 < NH:
                load_gu(hm + 2)
            nc.sync.wait_ge(dve.h, dve.n)
            d = nc.sync.dma_start(out=prodh[hm * 128:(hm + 1) * 128, :],
                                  in_=prw[:, (hm % 2) * T:(hm % 2 + 1) * T])
            prsem[hm % 2].inc(d, 16)
        P.barrier_engines(extra=wgsem + prsem)

    P.mark("g1C")
    with P.scope():
        CDEP = 4
        wdb = P.sbuf(pfx + "wdb", [128, CDEP * 512], BF16)
        prb = P.sbuf(pfx + "prb", [128, CDEP * 1024], BF16)
        wdsem = [Sem(nc, pfx + f"wdsem{i}") for i in range(CDEP)]
        pbsem = [Sem(nc, pfx + f"pbsem{i}") for i in range(CDEP)]
        cuse = {}
        cload = {}
        cidx = [(mq, tb2, hk) for mq in range(4) for tb2 in range(2) for hk in range(NH)]

        def load_c(ci):
            mq, tb2, hk = cidx[ci]
            sl = ci % CDEP
            nc.gpsimd.wait_ge(lg.h, lg.n)
            if ci >= CDEP:
                nc.gpsimd.wait_ge(mm.h, cuse[ci - CDEP])
                nc.gpsimd.wait_ge(dve.h, csub.get(ci - CDEP, 0))
            d = nc.gpsimd.dma_start(out=wdb[:, sl * 512:(sl + 1) * 512],
                                    in_=wqdT[hk * 128:(hk + 1) * 128, mq * 512:(mq + 1) * 512])
            wdsem[sl].inc(d, 16)
            nc.scalar.wait_ge(mm.h, cuse.get(ci - CDEP, 0))
            d = nc.scalar.dma_start(out=prb[:, sl * 1024:(sl + 1) * 1024],
                                    in_=prodh[hk * 128:(hk + 1) * 128, tb2 * 1024:(tb2 + 1) * 1024])
            pbsem[sl].inc(d, 16)
            cload[ci] = (sl, wdsem[sl].n, pbsem[sl].n)

        csub = {}
        for _ci0 in range(CDEP):
            load_c(_ci0)
        ci = 0
        for mq in range(4):
            for tb2 in range(2):
                t0 = tb2 * 1024
                nc.sync.wait_ge(dve.h, dve.n)
                nc.sync.wait_ge(st.h, st.n)
                nc.sync.dma_start(out=x_tb[:, 0:4 * 1024].rearrange("p (f t) -> p f t", f=4),
                                  in_=x3v[:, mq * 4:(mq + 1) * 4, t0:t0 + 1024]).then_inc(ld.h, 16)
                ld.n += 16
                for hk in range(NH):
                    sl, wt, pt = cload[ci]
                    nc.vector.wait_ge(wdsem[sl].h, wt)
                    nc.vector.wait_ge(mm.h, cuse.get(ci - CDEP, 0))
                    dve.inc(v.tensor_scalar_add(wdb[:, sl * 512:(sl + 1) * 512],
                                                wdb[:, sl * 512:(sl + 1) * 512], -128.0))
                    csub[ci] = dve.n
                    for e in (nc.tensor,):
                        e.wait_ge(dve.h, csub[ci])
                        e.wait_ge(pbsem[sl].h, pt)
                    for mi in range(4):
                        for half in range(2):
                            ins = nc.tensor.matmul(
                                pool[:, (mi * 2 + half) * 512:(mi * 2 + half + 1) * 512],
                                wdb[:, sl * 512 + mi * 128: sl * 512 + (mi + 1) * 128],
                                prb[:, sl * 1024 + half * 512: sl * 1024 + (half + 1) * 512],
                                start=(hk == 0), stop=(hk == NH - 1))
                    mm.inc(ins)
                    cuse[ci] = mm.n
                    if ci + CDEP < len(cidx):
                        load_c(ci + CDEP)
                    ci += 1
                nc.vector.wait_ge(mm.h, mm.n)
                nc.vector.wait_ge(ld.h, ld.n)
                for mi in range(4):
                    m = mq * 4 + mi
                    for half in range(2):
                        ins = v.scalar_tensor_tensor(
                            x_tb[:, mi * 1024 + half * 512: mi * 1024 + (half + 1) * 512],
                            pool[:, (mi * 2 + half) * 512:(mi * 2 + half + 1) * 512],
                            sd_s[:, m:m + 1],
                            x_tb[:, mi * 1024 + half * 512: mi * 1024 + (half + 1) * 512],
                            ALU.mult, ALU.add)
                dve.inc(ins)
                nc.sync.wait_ge(dve.h, dve.n)
                nc.sync.dma_start(out=hbm_fm_view(outT)[:, mq * 4:(mq + 1) * 4, t0:t0 + 1024],
                                  in_=x_tb[:, 0:4 * 1024].rearrange("p (f t) -> p f t", f=4)).then_inc(st.h, 16)
                st.n += 16
        P.barrier_engines(extra=wdsem + pbsem)
    nc.sync.wait_ge(st.h, st.n)


def build_ffn1():
    P = Prog()
    nc = P.nc
    NH = 44
    io = {
        "x2T": nc.dram_tensor("x2T", [D, TOK], F32, kind="ExternalInput").ap(),
        "attnT": nc.dram_tensor("attnT", [D, TOK], BF16, kind="ExternalInput").ap(),
        "wo": nc.dram_tensor("wo", [D, D], F32, kind="ExternalInput").ap(),
        "wqgT": nc.dram_tensor("wqgT", [D, 5632], I32, kind="ExternalInput").ap(),
        "wquT": nc.dram_tensor("wquT", [D, 5632], I32, kind="ExternalInput").ap(),
        "wqdT": nc.dram_tensor("wqdT", [5632, D], I32, kind="ExternalInput").ap(),
        "sg_c": nc.dram_tensor("sg_c", [128, NH], F32, kind="ExternalInput").ap(),
        "su_c": nc.dram_tensor("su_c", [128, NH], F32, kind="ExternalInput").ap(),
        "sd_c": nc.dram_tensor("sd_c", [128, FT], F32, kind="ExternalInput").ap(),
        "outT": nc.dram_tensor("outT", [D, TOK], F32, kind="ExternalOutput").ap(),
        "x3h": nc.dram_tensor("x3h", [D, TOK], F32).ap(),
        "ln3h": nc.dram_tensor("ln3h", [D, TOK], BF16).ap(),
        "prodh": nc.dram_tensor("prodh", [5632, TOK], BF16).ap(),
    }
    with P.scope():
        emit_ffn1(P, io)
    return P


def ffn1_in_maps(x2_fm, attn_fm, w):
    wqgT = np.ascontiguousarray(w["wq_gate"].T)
    wquT = np.ascontiguousarray(w["wq_up"].T)
    wqdT = np.ascontiguousarray(w["wq_down"].T)
    shared_m = {
        "wo": w["wo"].astype(np.float32),
        "wqgT": wqgT, "wquT": wquT, "wqdT": wqdT,
        "sg_c": pack_bias(w["s_gate"]), "su_c": pack_bias(w["s_up"]),
        "sd_c": pack_bias(w["s_down"]),
    }
    in_maps = []
    for core in range(8):
        b, hf = core // 2, core % 2
        tsl = slice(hf * TOK, (hf + 1) * TOK)
        m = dict(shared_m)
        m["x2T"] = np.ascontiguousarray(x2_fm[b][:, tsl])
        m["attnT"] = np.ascontiguousarray(attn_fm[b][:, tsl])
        in_maps.append(m)
    return in_maps


def run_ffn1(x2_fm, attn_fm, w):
    P = get_prog("ffn1", build_ffn1)
    in_maps = ffn1_in_maps(x2_fm, attn_fm, w)
    res = run_bass_kernel_spmd(P.nc, in_maps, core_ids=list(range(8)))
    out = np.empty((4, 2048, 4096), dtype=np.float32)
    for core in range(8):
        b, hf = core // 2, core % 2
        out[b][:, hf * TOK:(hf + 1) * TOK] = res.results[core]["outT"]
    return out



def emit_rendezvous(P, rv, idx):
    nc = P.nc
    st = P.st
    nc.sync.wait_ge(st.h, st.n)
    nc.sync.dma_start(out=rv["myslot"][idx], in_=rv["nsb"][0:1, idx * 16:(idx + 1) * 16]
                      ).then_inc(st.h, 16)
    st.n += 16
    nc.sync.wait_ge(st.h, st.n)
    fv, cnt, tgt, psem = rv["fv"], rv["cnt"], rv["tgt"], rv["psem"]
    nc.sync.reg_load(tgt, rv["nsb"][0:1, idx * 16:idx * 16 + 1])
    nc.sync.reg_mov(fv, 0)

    def cond():
        a = nc.sync.snap(fv) != nc.sync.snap(tgt)
        b = nc.sync.snap(cnt) < 2_000_000_000
        return a & b

    with nc.sync.While(cond):
        nc.sync.reg_add(cnt, cnt, 16)
        nc.sync.dma_start(out=rv["fsb"][0:1, :], in_=rv["otherslot"][idx]).then_inc(psem.h, 16)
        nc.sync.wait_ge(psem.h, nc.sync.snap(cnt))
        nc.sync.reg_load(fv, rv["fsb"][0:1, 0:1])
    P.gp.inc(nc.sync.nop())
    for e in (nc.tensor, nc.scalar, nc.vector, nc.gpsimd, nc.sync):
        e.wait_ge(P.gp.h, P.gp.n)


def build_fused():
    P = Prog()
    nc = P.nc
    NE = 8

    def din(name, shape, dtype=F32):
        return nc.dram_tensor(name, shape, dtype, kind="ExternalInput").ap()

    xb = din("xb", [D, S])
    attn0_sh = nc.dram_tensor("attn0_sh", [D, S], BF16, addr_space="Shared").ap()
    attn1_sh = nc.dram_tensor("attn1_sh", [D, S], BF16, addr_space="Shared").ap()
    x2_sh = nc.dram_tensor("x2_sh", [D, S], F32, addr_space="Shared").ap()
    attn0_loc = nc.dram_tensor("attn0_loc", [1024, S], BF16).ap()
    attn1_loc = nc.dram_tensor("attn1_loc", [1024, S], BF16).ap()
    x_loc = nc.dram_tensor("x_loc", [D, TOK], F32).ap()
    attnT_loc = nc.dram_tensor("attnT_loc", [D, TOK], BF16).ap()
    attn1T_loc = nc.dram_tensor("attn1T_loc", [D, TOK], BF16).ap()
    x2_loc = nc.dram_tensor("x2_loc", [D, TOK], F32).ap()

    a0 = {
        "xT": xb,
        "wqT": din("a0_wqT", [D, 1024]), "wkT": din("a0_wkT", [D, 1024]),
        "wvT": din("a0_wvT", [D, 1024]),
        "maskm": din("a0_maskm", [128, 1024], BF16),
        "qh": nc.dram_tensor("a0_qh", [1024, S], BF16).ap(),
        "kh": nc.dram_tensor("a0_kh", [1024, S], BF16).ap(),
        "vh": nc.dram_tensor("a0_vh", [S, 1024], BF16).ap(),
        "attnout": attn0_loc,
    }
    f0 = {
        "xT": x_loc,
        "attnT": attnT_loc,
        "x2T": x2_loc,
        "woT": din("f0_woT", [D, D]),
        "down_w": din("f0_down_w", [D, 1024]), "down_b": din("f0_down_b", [128, 8]),
        "up_w": din("f0_up_w", [1024, D]), "upcore_b": din("f0_upcore_b", [128, 32]),
        "router_w": din("f0_router_w", [D, 8]), "router_b": din("f0_router_b", [1, 8]),
        "cost_t": din("f0_cost_t", [1, 8]),
        "shared_w1": din("f0_shared_w1", [1024, 1024]),
        "shared_w2": din("f0_shared_w2", [1024, 1024]),
        "shared_b": din("f0_shared_b", [128, 16]),
        "core_w": din("f0_core_w", [D, D]),
        "ew1": [din(f"f0_e{e}_w1", [1024, HID[e]]) for e in range(NE)],
        "ew2": [din(f"f0_e{e}_w2", [HID[e], 1024]) for e in range(NE)],
        "eb1": din("f0_eb1", [128, 160]), "eb2": din("f0_eb2", [128, 64]),
        "idn_h": din("f0_idn_h", [128, 128]), "sel_h": din("f0_sel_h", [8, 1024], BF16),
        "x1h": nc.dram_tensor("f0_x1h", [D, TOK], F32).ap(),
        "ln2h": nc.dram_tensor("f0_ln2h", [D, TOK], BF16).ap(),
        "gph": [nc.dram_tensor(f"f0_gph{e}", [(HID + [1024])[e], TOK], BF16).ap() for e in range(9)],
    }
    a1 = {
        "xT": x2_sh,
        "wqT": din("a1_wqT", [D, 1024]), "wkT": din("a1_wkT", [D, 256]),
        "wvT": din("a1_wvT", [D, 256]),
        "maskm": din("a1_maskm", [128, 2688], BF16),
        "qh": nc.dram_tensor("a1_qh", [1024, S], BF16).ap(),
        "kh": nc.dram_tensor("a1_kh", [256, S], BF16).ap(),
        "vh": nc.dram_tensor("a1_vh", [S, 260], BF16).ap(),
        "attnout": attn1_loc,
    }
    g = {
        "x2T": x2_loc,
        "attnT": attn1T_loc,
        "wo": din("g_wo", [D, D]),
        "wqgT": din("g_wqgT", [D, 5632], I32), "wquT": din("g_wquT", [D, 5632], I32),
        "wqdT": din("g_wqdT", [5632, D], I32),
        "sg_c": din("g_sg_c", [128, 44]), "su_c": din("g_su_c", [128, 44]),
        "sd_c": din("g_sd_c", [128, FT]),
        "outT": nc.dram_tensor("outT", [D, TOK], F32, kind="ExternalOutput").ap(),
        "x3h": nc.dram_tensor("g_x3h", [D, TOK], F32).ap(),
        "ln3h": nc.dram_tensor("g_ln3h", [D, TOK], BF16).ap(),
        "prodh": nc.dram_tensor("g_prodh", [5632, TOK], BF16).ap(),
    }
    nonces = din("nonces", [1, 64], I32)
    fls = [nc.dram_tensor(f"fl{i}", [2, 16], I32, addr_space="Shared").ap() for i in range(4)]

    pid = nc.partition_id([mybir.EngineType.SP])
    parity = pid % 2
    toff = nc.sync.snap(parity * TOK)
    roff = nc.sync.snap(parity * (1024 * S))
    flmy = nc.sync.snap(parity * 16)
    flot = nc.sync.snap((1 - parity) * 16)

    rv = {}
    rv["nsb"] = P.sbuf("rv_nsb", [1, 64], I32)
    rv["fsb"] = P.sbuf("rv_fsb", [1, 16], I32)
    rv["fv"] = nc.alloc_register(mybir.EngineType.SP, "rv_fv")
    rv["cnt"] = nc.alloc_register(mybir.EngineType.SP, "rv_cnt")
    rv["tgt"] = nc.alloc_register(mybir.EngineType.SP, "rv_tgt")
    rv["psem"] = Sem(nc, "rv_psem")
    rv["myslot"] = [dataclasses.replace(f[0:1, 0:16], offset=flmy) for f in fls]
    rv["otherslot"] = [dataclasses.replace(f[0:1, 0:16], offset=flot) for f in fls]
    nc.sync.dma_start(out=rv["nsb"][:, :], in_=nonces[:, :]).then_inc(P.ld.h, 16)
    P.ld.n += 16
    nc.sync.reg_mov(rv["cnt"], 0)
    nc.sync.wait_ge(P.ld.h, P.ld.n)

    def bulk(dst_ap, src_ap):
        nc.sync.dma_start(out=dst_ap, in_=src_ap).then_inc(P.st.h, 16)
        P.st.n += 16
        nc.sync.wait_ge(P.st.h, P.st.n)

    bulk(x_loc[:, :], offap(xb[:, 0:TOK], toff))

    with nc.named_scope("attn0"):
        with P.scope():
            emit_attn(P, 0, a0, pfx="a0_")
    with nc.named_scope("rdv0"):
        bulk(offap(attn0_sh[0:1024, :], roff), attn0_loc[:, :])
        emit_rendezvous(P, rv, 0)
        bulk(attnT_loc[:, :], offap(attn0_sh[:, 0:TOK], toff))
    with nc.named_scope("ffn0"):
        with P.scope():
            emit_ffn0(P, f0, pfx="f0_")
    with nc.named_scope("rdv1"):
        bulk(offap(x2_sh[:, 0:TOK], toff), x2_loc[:, :])
        emit_rendezvous(P, rv, 1)
    with nc.named_scope("attn1"):
        with P.scope():
            emit_attn(P, 1, a1, pfx="a1_")
    with nc.named_scope("rdv2"):
        bulk(offap(attn1_sh[0:1024, :], roff), attn1_loc[:, :])
        emit_rendezvous(P, rv, 2)
        bulk(attn1T_loc[:, :], offap(attn1_sh[:, 0:TOK], toff))
    with nc.named_scope("ffn1"):
        with P.scope():
            emit_ffn1(P, g, pfx="g_")
    nc.sync.wait_ge(P.st.h, P.st.n)
    return P


def fused_in_maps(ins):
    x = ins["L0_attn_hidden_states"]
    xT_all = np.ascontiguousarray(x.transpose(0, 2, 1)).astype(np.float32)
    w_qkv = ins["L0_attn_w_qkv"]
    wq0, wk0, wv0 = w_qkv[0:2048], w_qkv[2048:4096], w_qkv[4096:6144]
    mask0 = make_mask_master(0)
    mask1 = make_mask_master(1)
    shared_m = {
        "a0_maskm": mask0, "a1_maskm": mask1,
        "f0_woT": np.ascontiguousarray(ins["L0_attn_w_o"].T.astype(np.float32)),
        "f0_down_w": ins["L0_ffn_down_w"], "f0_down_b": pack_bias(ins["L0_ffn_down_b"]),
        "f0_up_w": ins["L0_ffn_up_w"],
        "f0_upcore_b": np.concatenate([pack_bias(ins["L0_ffn_up_b"]),
                                       pack_bias(ins["L0_ffn_core_b"])], axis=1),
        "f0_router_w": ins["L0_ffn_router_w"],
        "f0_router_b": ins["L0_ffn_router_b"].reshape(1, 8).astype(np.float32),
        "f0_cost_t": _l0_cost_consts(),
        "f0_shared_w1": ins["L0_ffn_shared_w1"], "f0_shared_w2": ins["L0_ffn_shared_w2"],
        "f0_shared_b": np.concatenate([pack_bias(ins["L0_ffn_shared_b1"]),
                                       pack_bias(ins["L0_ffn_shared_b2"])], axis=1),
        "f0_core_w": ins["L0_ffn_core_w"],
        "f0_eb1": np.concatenate([pack_bias(ins[f"L0_ffn_e{e}_b1"]) for e in range(8)], axis=1),
        "f0_eb2": pack_b2([ins[f"L0_ffn_e{e}_b2"] for e in range(8)]),
        "f0_idn_h": np.eye(128, dtype=np.float32),
        "f0_sel_h": _sel_mat().astype(BF),
        "g_wo": ins["L1_attn_wo"].astype(np.float32),
        "g_wqgT": np.ascontiguousarray(ins["L1_ffn_wq_gate"].T),
        "g_wquT": np.ascontiguousarray(ins["L1_ffn_wq_up"].T),
        "g_wqdT": np.ascontiguousarray(ins["L1_ffn_wq_down"].T),
        "g_sg_c": pack_bias(ins["L1_ffn_s_gate"]), "g_su_c": pack_bias(ins["L1_ffn_s_up"]),
        "g_sd_c": pack_bias(ins["L1_ffn_s_down"]),
        "nonces": np.random.randint(1, 2 ** 30, size=(1, 64)).astype(np.int32),
    }
    for e in range(8):
        shared_m[f"f0_e{e}_w1"] = ins[f"L0_ffn_e{e}_w1"]
        shared_m[f"f0_e{e}_w2"] = ins[f"L0_ffn_e{e}_w2"]
    in_maps = []
    for core in range(8):
        b, hg = core // 2, core % 2
        m = dict(shared_m)
        m["xb"] = xT_all[b]
        r0, r1 = hg * 1024, hg * 1024 + 1024
        m["a0_wqT"] = np.ascontiguousarray(wq0[r0:r1].T)
        m["a0_wkT"] = np.ascontiguousarray(wk0[r0:r1].T)
        m["a0_wvT"] = np.ascontiguousarray(wv0[r0:r1].T)
        m["a1_wqT"] = np.ascontiguousarray(ins["L1_attn_wq"][:, r0:r1])
        kc0, kc1 = hg * 256, hg * 256 + 256
        m["a1_wkT"] = np.ascontiguousarray(ins["L1_attn_wk"][:, kc0:kc1])
        m["a1_wvT"] = np.ascontiguousarray(ins["L1_attn_wv"][:, kc0:kc1])
        in_maps.append(m)
    return in_maps


def kernel(**inputs):
    ins = {k: np.asarray(v) for k, v in inputs.items()}
    P = get_prog("fused", build_fused)
    in_maps = fused_in_maps(ins)
    res = run_bass_kernel_spmd(P.nc, in_maps, core_ids=list(range(8)))
    out = np.empty((4, 2048, 4096), dtype=np.float32)
    for core in range(8):
        b, hf = core // 2, core % 2
        out[b][:, hf * TOK:(hf + 1) * TOK] = res.results[core]["outT"]
    return np.ascontiguousarray(out.transpose(0, 2, 1))



def make_mask_master(layer):
    if layer == 0:
        p = np.arange(128)[:, None]
        c = np.arange(1024)[None, :]
        return (c >= p + 512).astype(BF)
    else:
        p = np.arange(128)[:, None]
        c = np.arange(2688)[None, :]
        d = c - p - 640
        return (~((d >= 1) & (d <= 1024))).astype(BF)


_PROG_CACHE = {}


def get_prog(key, builder):
    if key not in _PROG_CACHE:
        _PROG_CACHE[key] = builder()
    return _PROG_CACHE[key]


def run_l0_attn(x, w_qkv):
    P = get_prog("attn0", lambda: build_attn(0))
    mask = make_mask_master(0)
    xT = np.ascontiguousarray(x.transpose(0, 2, 1))
    wq = w_qkv[0:2048]
    wk = w_qkv[2048:4096]
    wv = w_qkv[4096:6144]
    in_maps = []
    for core in range(8):
        b, hg = core // 2, core % 2
        r0, r1 = hg * 1024, hg * 1024 + 1024
        in_maps.append({
            "xT": xT[b],
            "wqT": np.ascontiguousarray(wq[r0:r1].T),
            "wkT": np.ascontiguousarray(wk[r0:r1].T),
            "wvT": np.ascontiguousarray(wv[r0:r1].T),
            "maskm": mask,
        })
    res = run_bass_kernel_spmd(P.nc, in_maps, core_ids=list(range(8)))
    out = np.empty((4, 2048, 4096), dtype=BF)
    for core in range(8):
        b, hg = core // 2, core % 2
        out[b, hg * 1024:(hg + 1) * 1024] = res.results[core]["attnout"]
    return out


def run_l1_attn(x_fm, wq, wk, wv):
    P = get_prog("attn1", lambda: build_attn(1))
    mask = make_mask_master(1)
    in_maps = []
    for core in range(8):
        b, hg = core // 2, core % 2
        qc0, qc1 = hg * 1024, hg * 1024 + 1024
        kc0, kc1 = hg * 256, hg * 256 + 256
        in_maps.append({
            "xT": x_fm[b],
            "wqT": np.ascontiguousarray(wq[:, qc0:qc1]),
            "wkT": np.ascontiguousarray(wk[:, kc0:kc1]),
            "wvT": np.ascontiguousarray(wv[:, kc0:kc1]),
            "maskm": mask,
        })
    res = run_bass_kernel_spmd(P.nc, in_maps, core_ids=list(range(8)))
    out = np.empty((4, 2048, 4096), dtype=BF)
    for core in range(8):
        b, hg = core // 2, core % 2
        out[b, hg * 1024:(hg + 1) * 1024] = res.results[core]["attnout"]
    return out



def _sel_mat():
    s = np.zeros((8, 1024), np.float32)
    for e in range(8):
        s[e, e * 128:(e + 1) * 128] = 1.0
    return s


def _l0_cost_consts():
    return np.array([[COST_LAMBDA * 2048.0 * h for h in HID]], dtype=np.float32)


def pack_b2(b2_list):
    out = np.zeros((128, 64), np.float32)
    for e, b2 in enumerate(b2_list):
        out[:, e::8] = b2.reshape(8, 128).T
    return out


def ffn0_in_maps(xT_all, attn_fm, w):
    woT = np.ascontiguousarray(w["w_o"].T.astype(np.float32))
    shared_m = {
        "woT": woT,
        "down_w": w["down_w"], "down_b": pack_bias(w["down_b"]),
        "up_w": w["up_w"],
        "upcore_b": np.concatenate([pack_bias(w["up_b"]), pack_bias(w["core_b"])], axis=1),
        "router_w": w["router_w"], "router_b": w["router_b"].reshape(1, 8),
        "cost_t": _l0_cost_consts(),
        "shared_w1": w["shared_w1"], "shared_w2": w["shared_w2"],
        "shared_b": np.concatenate([pack_bias(w["shared_b1"]), pack_bias(w["shared_b2"])], axis=1),
        "core_w": w["core_w"],
        "eb1": np.concatenate([pack_bias(w[f"e{e}_b1"]) for e in range(8)], axis=1),
        "eb2": pack_b2([w[f"e{e}_b2"] for e in range(8)]),
        "idn_h": np.eye(128, dtype=np.float32),
        "sel_h": _sel_mat().astype(BF),
    }
    for e in range(8):
        shared_m[f"e{e}_w1"] = w[f"e{e}_w1"]
        shared_m[f"e{e}_w2"] = w[f"e{e}_w2"]
    in_maps = []
    for core in range(8):
        b, hf = core // 2, core % 2
        tsl = slice(hf * TOK, (hf + 1) * TOK)
        m = dict(shared_m)
        m["xT"] = np.ascontiguousarray(xT_all[b][:, tsl])
        m["attnT"] = np.ascontiguousarray(attn_fm[b][:, tsl])
        in_maps.append(m)
    return in_maps


def run_ffn0(xT_all, attn_fm, w):
    P = get_prog("ffn0", build_ffn0)
    in_maps = ffn0_in_maps(xT_all, attn_fm, w)
    res = run_bass_kernel_spmd(P.nc, in_maps, core_ids=list(range(8)))
    out = np.empty((4, 2048, 4096), dtype=np.float32)
    for core in range(8):
        b, hf = core // 2, core % 2
        out[b][:, hf * TOK:(hf + 1) * TOK] = res.results[core]["x2T"]
    return out


def kernel_unfused(**inputs):
    ins = {k: np.asarray(v) for k, v in inputs.items()}
    x = ins["L0_attn_hidden_states"]
    xT_all = np.ascontiguousarray(x.transpose(0, 2, 1))

    attn0 = run_l0_attn(x, ins["L0_attn_w_qkv"])

    w0 = {"w_o": ins["L0_attn_w_o"]}
    for k in ("down_w", "down_b", "up_w", "up_b", "router_w", "router_b",
              "shared_w1", "shared_b1", "shared_w2", "shared_b2",
              "core_w", "core_b"):
        w0[k] = ins["L0_ffn_" + k]
    for e in range(8):
        for p in ("w1", "b1", "w2", "b2"):
            w0[f"e{e}_{p}"] = ins[f"L0_ffn_e{e}_{p}"]
    x2_fm = run_ffn0(xT_all, attn0, w0)

    attn1 = run_l1_attn(x2_fm, ins["L1_attn_wq"], ins["L1_attn_wk"], ins["L1_attn_wv"])

    w1 = {"wo": ins["L1_attn_wo"]}
    for k in ("wq_gate", "s_gate", "wq_up", "s_up", "wq_down", "s_down"):
        w1[k] = ins["L1_ffn_" + k]
    out_fm = run_ffn1(x2_fm, attn1, w1)
    return np.ascontiguousarray(out_fm.transpose(0, 2, 1))


# version 11
# speedup vs baseline: 1.3306x; 1.3306x over previous
import math
import dataclasses
from contextlib import ExitStack, contextmanager
import numpy as np
import ml_dtypes

import concourse.bass as bass
import concourse.mybir as mybir
from concourse.bass_utils import run_bass_kernel_spmd

F32 = mybir.dt.float32
BF16 = mybir.dt.bfloat16
I32 = mybir.dt.int32
AF = mybir.ActivationFunctionType
ALU = mybir.AluOpType

D = 2048
B = 4
S = 4096
TOK = 2048
FT = D // 128

EPS = 1e-5
COST_LAMBDA = 1e-07
HID = [1024, 2048, 3072, 4096, 1024, 2048, 3072, 4096]
GELU = AF.Gelu

BF = ml_dtypes.bfloat16



class Sem:
    def __init__(self, nc, name):
        self.h = nc.alloc_semaphore(name=name)
        self.n = 0

    def inc(self, instr, k=1):
        instr.then_inc(self.h, k)
        self.n += k
        return self.n


class Prog:
    def __init__(self):
        self.nc = bass.Bass("TRN2", target_bir_lowering=False)
        nc = self.nc
        self.mm = Sem(nc, "mm")
        self.act = Sem(nc, "act")
        self.dve = Sem(nc, "dve")
        self.gp = Sem(nc, "gp")
        self.ld = Sem(nc, "ld")
        self.lg = Sem(nc, "lg")
        self.st = Sem(nc, "st")

    def sbuf(self, name, shape, dtype):
        if getattr(self, "_scopes", None):
            return self._scopes[-1].enter_context(self.nc.sbuf_tensor(name, shape, dtype))
        return self.nc.alloc_sbuf_tensor(name, shape, dtype)

    def psum(self, name, shape, dtype=F32):
        if getattr(self, "_scopes", None):
            return self._scopes[-1].enter_context(self.nc.psum_tensor(name, shape, dtype))
        return self.nc.alloc_psum_tensor(name, shape, dtype)

    def mark(self, name):
        try:
            self.nc.enter_named_scope(name, False)
            self.nc.leave_named_scope(name, 0, False)
        except Exception:
            pass

    @contextmanager
    def scope(self):
        if not hasattr(self, "_scopes"):
            self._scopes = []
        es = ExitStack()
        self._scopes.append(es)
        try:
            yield
        finally:
            self._scopes.pop()
            es.close()

    def barrier_engines(self, engines=None, extra=()):
        nc = self.nc
        engines = engines or [nc.tensor, nc.scalar, nc.vector, nc.sync, nc.gpsimd]
        for s in (self.mm, self.act, self.dve, self.gp, self.ld, self.st, *extra):
            for e in engines:
                e.wait_ge(s.h, s.n)


def hbm_fm_view(ap, p=128):
    return ap.rearrange("(f p) t -> p f t", p=p)


def offap(ap, toff):
    if isinstance(toff, int) and toff == 0:
        return ap
    return dataclasses.replace(ap, offset=toff + ap.offset)


def load_weight_fm(P, dst_sb, src_hbm, nM, nK, eng=None):
    nc = P.nc
    eng = eng or nc.gpsimd
    src = src_hbm.rearrange("(k p) mc -> p k mc", p=128)
    eng.wait_ge(P.lg.h, P.lg.n)
    d = eng.dma_start(out=dst_sb[:, :].rearrange("p (k mc) -> p k mc", k=nK), in_=src)
    P.lg.inc(d, 16)


def w_tile(w_sb, m, k, nM):
    c0 = (k * nM + m) * 128
    return w_sb[:, c0:c0 + 128]


def pack_bias(b):
    n = b.shape[0]
    return np.ascontiguousarray(b.reshape(n // 128, 128).T.astype(np.float32))



class LnScratch:
    def __init__(self, P, T, name="lns", xb=None, sq=None, st_ps=None, bc_ps=None):
        self.T = T
        self.xb = xb if xb is not None else P.sbuf(name + "_xb", [128, FT * T], BF16)
        self.sq = sq if sq is not None else P.sbuf(name + "_sq", [128, FT * T], BF16)
        self.st_ps = st_ps if st_ps is not None else P.psum(name + "_stps", [1, 2 * T], F32)
        self.bc_ps = bc_ps if bc_ps is not None else P.psum(name + "_bcps", [128, T], F32)
        self.stats = P.sbuf(name + "_stats", [1, 6 * T], F32)
        self.ab = P.sbuf(name + "_ab", [128, 2 * T], BF16)


def emit_ln(P, scr, src_slices, dst_slices, ntok, consts):
    nc = P.nc
    ones_bf, ones_row, cst = consts
    mm, act, dve = P.mm, P.act, P.dve
    T = scr.T
    v = nc.vector
    v.wait_ge(dve.h, dve.n)
    nc.scalar.wait_ge(dve.h, dve.n)
    cast = src_slices(0).dtype == F32
    if cast:
        for ft in range(FT):
            ins = v.tensor_copy(scr.xb[:, ft * ntok:(ft + 1) * ntok], src_slices(ft))
        dve.inc(ins)
        for e in (nc.scalar, nc.tensor):
            e.wait_ge(dve.h, dve.n)
        ssl = lambda ft: scr.xb[:, ft * ntok:(ft + 1) * ntok]
    else:
        ssl = src_slices
    for ft in range(FT):
        ins = nc.scalar.activation(scr.sq[:, ft * ntok:(ft + 1) * ntok], ssl(ft), AF.Square)
    act.inc(ins)
    nc.tensor.wait_ge(act.h, act.n)
    for ft in range(FT):
        nc.tensor.matmul(scr.st_ps[0:1, 0:ntok], ones_bf[:, 0:1], ssl(ft),
                         start=(ft == 0), stop=(ft == FT - 1))
    for ft in range(FT):
        ins = nc.tensor.matmul(scr.st_ps[0:1, T:T + ntok], ones_bf[:, 0:1],
                               scr.sq[:, ft * ntok:(ft + 1) * ntok],
                               start=(ft == 0), stop=(ft == FT - 1))
    mm.inc(ins)
    nc.vector.wait_ge(mm.h, mm.n)
    mean = scr.stats[0:1, 0:ntok]
    aa = scr.stats[0:1, T:T + ntok]
    bb = scr.stats[0:1, 2 * T:2 * T + ntok]
    ex2 = scr.stats[0:1, 3 * T:3 * T + ntok]
    var = scr.stats[0:1, 4 * T:4 * T + ntok]
    lnv = scr.stats[0:1, 5 * T:5 * T + ntok]
    dve.inc(v.tensor_scalar_mul(mean, scr.st_ps[0:1, 0:ntok], 1.0 / D))
    dve.inc(v.tensor_scalar_mul(ex2, scr.st_ps[0:1, T:T + ntok], 1.0 / D))
    v.wait_ge(dve.h, dve.n)
    dve.inc(v.tensor_tensor(var, mean, mean, ALU.mult))
    v.wait_ge(dve.h, dve.n)
    dve.inc(v.tensor_sub(var, ex2, var))
    nc.scalar.wait_ge(dve.h, dve.n)
    act.inc(nc.scalar.activation(lnv, var, AF.Ln, bias=cst[0:1, 0:1]))
    nc.scalar.wait_ge(act.h, act.n)
    act.inc(nc.scalar.activation(aa, lnv, AF.Exp, scale=cst[0:1, 1:2]))
    nc.vector.wait_ge(act.h, act.n)
    dve.inc(v.tensor_tensor(bb, mean, aa, ALU.mult))
    v.wait_ge(dve.h, dve.n)
    dve.inc(v.tensor_scalar_mul(bb, bb, -1.0))
    nc.tensor.wait_ge(dve.h, dve.n)
    mm.inc(nc.tensor.matmul(scr.bc_ps[:, 0:ntok], ones_row[0:1, :], aa, start=True, stop=True))
    nc.scalar.wait_ge(mm.h, mm.n)
    act.inc(nc.scalar.copy(scr.ab[:, 0:ntok], scr.bc_ps[:, 0:ntok]))
    nc.tensor.wait_ge(act.h, act.n)
    mm.inc(nc.tensor.matmul(scr.bc_ps[:, 0:ntok], ones_row[0:1, :], bb, start=True, stop=True))
    nc.scalar.wait_ge(mm.h, mm.n)
    act.inc(nc.scalar.copy(scr.ab[:, T:T + ntok], scr.bc_ps[:, 0:ntok]))
    nc.vector.wait_ge(act.h, act.n)
    for ft in range(FT):
        ins = v.tensor_tensor(dst_slices(ft), ssl(ft), scr.ab[:, 0:ntok], ALU.mult)
    dve.inc(ins)
    v.wait_ge(dve.h, dve.n)
    for ft in range(FT):
        ins = v.tensor_tensor(dst_slices(ft), dst_slices(ft), scr.ab[:, T:T + ntok], ALU.add)
    dve.inc(ins)
    for e in (nc.tensor, nc.scalar, nc.sync, nc.gpsimd):
        e.wait_ge(dve.h, dve.n)



def emit_attn(P, layer, io, pfx="a_"):
    nc = P.nc
    v = nc.vector
    if layer == 0:
        NQ, NKV, HD = 8, 8, 128
        scale = 1.0 / math.sqrt(128.0)
        MW = 1024
        VW = KW0 = 1024
        VST = HD
    else:
        NQ, NKV, HD = 16, 4, 64
        scale = 1.0 / math.sqrt(64.0)
        MW = 2688
        VST = 65
    QW = NQ * HD
    KW = NKV * HD
    VWID = NKV * VST
    QB = 512
    NQB = S // QB
    NKT = S // 128
    xT = io["xT"]; wqT = io["wqT"]; wkT = io["wkT"]; wvT = io["wvT"]
    maskm = io["maskm"]; attnout = io["attnout"]; roff = io.get("roff", 0)
    qh = io["qh"]; kh = io["kh"]; vh = io["vh"]
    mm, act, dve, ld, st = P.mm, P.act, P.dve, P.ld, P.st

    ones_bf = P.sbuf(pfx + "ones_bf", [128, 128], BF16)
    ones_row = P.sbuf(pfx + "ones_row", [1, 128], F32)
    cst = P.sbuf(pfx + "cst", [128, 3], F32)
    nc.gpsimd.memset(ones_bf[:, :], 1.0)
    nc.gpsimd.memset(ones_row[:, :], 1.0)
    nc.gpsimd.memset(cst[:, 0:1], EPS)
    nc.gpsimd.memset(cst[:, 1:2], -0.5)
    P.gp.inc(nc.gpsimd.memset(cst[:, 2:3], scale))
    for e in (nc.tensor, nc.vector, nc.scalar, nc.sync):
        e.wait_ge(P.gp.h, P.gp.n)

    mask_sb = P.sbuf(pfx + "mask_sb", [128, MW], BF16)
    nc.sync.dma_start(out=mask_sb[:, :], in_=maskm[:, :]).then_inc(ld.h, 16)
    ld.n += 16

    P.mark(pfx + "qkv")
    with P.scope():
        wq_sb = P.sbuf(pfx + "wq_sb", [128, (QW // 128) * FT * 128], BF16)
        wk_sb = P.sbuf(pfx + "wk_sb", [128, (KW // 128) * FT * 128], BF16)
        load_weight_fm(P, wq_sb, wqT, QW // 128, FT)
        load_weight_fm(P, wk_sb, wkT, KW // 128, FT)
        wv_sb = P.sbuf(pfx + "wv_sb", [128, FT * KW], BF16)
        nc.gpsimd.wait_ge(P.lg.h, P.lg.n)
        d = nc.gpsimd.dma_start(out=wv_sb[:, :].rearrange("p (f c) -> p f c", f=FT),
                                in_=hbm_fm_view(wvT))
        P.lg.inc(d, 16)

        TB = 512
        x_sb = P.sbuf(pfx + "x_sb", [128, FT * TB], BF16)
        ln_sb = P.sbuf(pfx + "ln_sb", [128, FT * TB], BF16)
        lnscr = LnScratch(P, TB, name=pfx + "lns")
        qk_ps = [P.psum(pfx + f"qk_ps{i}", [128, TB], F32) for i in range(2)]
        qko_sb = P.sbuf(pfx + "qko_sb", [128, 2 * TB], BF16)
        qsem = [Sem(nc, pfx + f"qsem{i}") for i in range(2)]
        vsem = [Sem(nc, pfx + f"vsem{i}") for i in range(2)]
        vo_sb = P.sbuf(pfx + "vo_sb", [128, 2 * VWID], BF16)
        if layer == 1:
            P.gp.inc(nc.gpsimd.memset(vo_sb[:, :], 1.0))
            nc.scalar.wait_ge(P.gp.h, P.gp.n)

        xv = hbm_fm_view(xT)
        nc.sync.wait_ge(ld.h, ld.n)
        nc.sync.wait_ge(P.lg.h, P.lg.n)
        gslot = [0]
        ev_cnt = {}
        for tb in range(S // TB):
            nc.gpsimd.wait_ge(P.lg.h, P.lg.n)
            nc.gpsimd.wait_ge(dve.h, dve.n)
            nc.gpsimd.wait_ge(mm.h, mm.n)
            nc.gpsimd.dma_start(out=x_sb[:, :].rearrange("p (f t) -> p f t", f=FT),
                                in_=xv[:, :, tb * TB:(tb + 1) * TB]).then_inc(P.lg.h, 16)
            P.lg.n += 16
            for e in (nc.vector, nc.scalar, nc.tensor):
                e.wait_ge(P.lg.h, P.lg.n)
                e.wait_ge(ld.h, ld.n)
            emit_ln(P, lnscr,
                    lambda ft: x_sb[:, ft * TB:(ft + 1) * TB],
                    lambda ft: ln_sb[:, ft * TB:(ft + 1) * TB],
                    TB, (ones_bf, ones_row, cst))
            for which, wsb, nm, dsth in ((0, wq_sb, QW // 128, qh), (1, wk_sb, KW // 128, kh)):
                for m in range(nm):
                    bank = gslot[0] % 2
                    if gslot[0] >= 2:
                        nc.tensor.wait_ge(act.h, ev_cnt[gslot[0] - 2])
                    for ft in range(FT):
                        ins = nc.tensor.matmul(qk_ps[bank][:, :], w_tile(wsb, m, ft, nm),
                                               ln_sb[:, ft * TB:(ft + 1) * TB],
                                               start=(ft == 0), stop=(ft == FT - 1))
                    mm.inc(ins)
                    nc.scalar.wait_ge(mm.h, mm.n)
                    nc.scalar.wait_ge(qsem[bank].h, qsem[bank].n)
                    act.inc(nc.scalar.copy(qko_sb[:, bank * TB:(bank + 1) * TB], qk_ps[bank][:, :]))
                    ev_cnt[gslot[0]] = act.n
                    gslot[0] += 1
                    nc.sync.wait_ge(act.h, act.n)
                    d = nc.sync.dma_start(out=dsth[m * 128:(m + 1) * 128, tb * TB:(tb + 1) * TB],
                                          in_=qko_sb[:, bank * TB:(bank + 1) * TB])
                    qsem[bank].inc(d, 16)
            for tt in range(TB // 128):
                bank = tt % 2
                nw = (KW + 511) // 512
                for nb in range(nw):
                    ncol = min(512, KW - nb * 512)
                    pbank = gslot[0] % 2
                    if gslot[0] >= 2:
                        nc.tensor.wait_ge(act.h, ev_cnt[gslot[0] - 2])
                    for ft in range(FT):
                        ins = nc.tensor.matmul(
                            qk_ps[pbank][:, 0:ncol],
                            ln_sb[:, ft * TB + tt * 128: ft * TB + tt * 128 + 128],
                            wv_sb[:, ft * KW + nb * 512: ft * KW + nb * 512 + ncol],
                            start=(ft == 0), stop=(ft == FT - 1))
                    mm.inc(ins)
                    nc.scalar.wait_ge(mm.h, mm.n)
                    nc.scalar.wait_ge(vsem[bank].h, vsem[bank].n)
                    if layer == 0:
                        dstv = vo_sb[:, bank * VWID + nb * 512: bank * VWID + nb * 512 + ncol]
                    else:
                        dstv = vo_sb[:, bank * VWID:(bank + 1) * VWID].rearrange(
                            "p (g c) -> p g c", c=VST)[:, :, 0:64]
                    act.inc(nc.scalar.copy(dstv, qk_ps[pbank][:, 0:ncol].rearrange(
                        "p (g c) -> p g c", c=64) if layer == 1 else qk_ps[pbank][:, 0:ncol]))
                    ev_cnt[gslot[0]] = act.n
                    gslot[0] += 1
                nc.sync.wait_ge(act.h, act.n)
                t0 = tb * TB + tt * 128
                d = nc.sync.dma_start(out=vh[t0:t0 + 128, :],
                                      in_=vo_sb[:, bank * VWID: bank * VWID + VWID])
                vsem[bank].inc(d, 16)

        P.barrier_engines(extra=qsem + vsem)

    P.mark(pfx + "core")
    def kt_list(qb):
        out = []
        if layer == 0:
            for kt in range(4 * qb + 4):
                if kt >= 4 * qb:
                    delta = kt * 128 - qb * QB
                    out.append((kt, 512 - delta))
                else:
                    out.append((kt, None))
        else:
            for kt in range(NKT):
                delta = kt * 128 - qb * QB
                if -513 <= delta <= -128:
                    continue
                if -1151 <= delta <= 510:
                    out.append((kt, 640 - delta))
                else:
                    out.append((kt, None))
        return out

    with P.scope():
        L1 = layer == 1
        k_sb = [P.sbuf(pfx + f"k_sb{i}", [128, S], BF16) for i in range(2)]
        v_sb = [P.sbuf(pfx + f"v_sb{i}", [128, NKT * VST], BF16) for i in range(2)]
        q_sb = [P.sbuf(pfx + f"q_sb{i}", [128, QB], BF16) for i in range(2)]
        e_pair = [P.sbuf(pfx + f"e_pair{i}", [128, 2 * QB], BF16) for i in range(3)]
        s_pair = [P.psum(pfx + f"s_pair{i}", [128, 2 * QB], F32) for i in range(2)]
        if L1:
            o_psA = P.psum(pfx + "o_psA", [65, QB], F32)
            o_psB = P.psum(pfx + "o_psB", [65, QB], F32)
            nm_ps = P.psum(pfx + "nm_ps", [128, QB], F32)
            t_sb = P.sbuf(pfx + "t_sb", [65, QB], F32)
        else:
            o_ps = [P.psum(pfx + f"o_ps{i}", [128, QB], F32) for i in range(2)]
            sum_ps = [P.psum(pfx + f"sum_ps{i}", [128, QB], F32) for i in range(2)]
        rsum = P.sbuf(pfx + "rsum", [1, 2 * QB], BF16)
        rsum32 = P.sbuf(pfx + "rsum32", [1, 2 * QB], F32)
        rb_sb = P.sbuf(pfx + "rb_sb", [128, 2 * QB], BF16)
        ao_sb = P.sbuf(pfx + "ao_sb", [HD, 2 * QB], BF16)
        ones_rbf = P.sbuf(pfx + "ones_rbf", [1, 128], BF16)
        P.gp.inc(nc.gpsimd.memset(ones_rbf[:, :], 1.0))
        for e in (nc.tensor, nc.vector, nc.scalar, nc.sync):
            e.wait_ge(P.gp.h, P.gp.n)
        asem = [Sem(nc, pfx + f"asem{i}") for i in range(2)]

        ghead = lambda h: h // (NQ // NKV)
        kvsem = [Sem(nc, pfx + f"kvsem{i}") for i in range(2)]
        q2sem = [Sem(nc, pfx + f"q2sem{i}") for i in range(2)]
        kvld = {}
        qld = {}
        last_head_mm = [0]
        qs_hist = {}
        sp_cnt_hist = []
        exp_free_hist = []
        av_free_hist = []
        norm_hist = {}
        pend = {}
        pend_rbfree = [0]

        def load_kv(h):
            hb = h % 2
            g = ghead(h)
            nc.sync.wait_ge(mm.h, last_head_mm[0])
            if L1:
                nc.sync.dma_start(out=k_sb[hb][0:64, :],
                                  in_=kh[g * 64:(g + 1) * 64, :]).then_inc(kvsem[hb].h, 16)
                kvsem[hb].n += 16
                nc.sync.dma_start(out=k_sb[hb][64:128, :],
                                  in_=kh[g * 64:(g + 1) * 64, :]).then_inc(kvsem[hb].h, 16)
                kvsem[hb].n += 16
            else:
                nc.sync.dma_start(out=k_sb[hb][:, :],
                                  in_=kh[g * HD:(g + 1) * HD, :]).then_inc(kvsem[hb].h, 16)
                kvsem[hb].n += 16
            vv = vh.rearrange("(tt p) c -> p tt c", p=128)[:, :, g * VST:(g + 1) * VST]
            nc.sync.dma_start(out=v_sb[hb][:, :].rearrange("p (tt c) -> p tt c", c=VST),
                              in_=vv).then_inc(kvsem[hb].h, 16)
            kvsem[hb].n += 16
            kvld[h] = (hb, kvsem[hb].n)

        def load_q(h, qb, qi):
            qbuf = qi % 2
            if qi >= 2:
                nc.sync.wait_ge(mm.h, qs_hist[qi - 2])
            src = qh[h * HD:(h + 1) * HD, qb * QB:(qb + 1) * QB]
            nc.sync.dma_start(out=q_sb[qbuf][0:HD, :], in_=src).then_inc(q2sem[qbuf].h, 16)
            q2sem[qbuf].n += 16
            if L1:
                nc.sync.dma_start(out=q_sb[qbuf][64:128, :], in_=src).then_inc(q2sem[qbuf].h, 16)
                q2sem[qbuf].n += 16
            qld[qi] = (qbuf, q2sem[qbuf].n)

        def norm_top():
            if not pend or not L1:
                return
            ob = pend["obank"]
            nc.vector.wait_ge(mm.h, pend["mm_last"])
            nc.vector.wait_ge(asem[ob].h, asem[ob].n)
            v.wait_ge(dve.h, dve.n)
            dve.inc(v.tensor_scalar_mul(t_sb[:, :], o_psA[:, :], 1.0))
            v.wait_ge(dve.h, dve.n)
            dve.inc(v.tensor_tensor(t_sb[:, :], o_psB[:, :], t_sb[:, :], ALU.add))
            pend["addB"] = dve.n

        def norm_recip():
            if not pend:
                return
            ob = pend["obank"]
            if L1:
                v.wait_ge(dve.h, pend["addB"])
                dve.inc(v.reciprocal(rsum32[0:1, ob * QB:(ob + 1) * QB], t_sb[64:65, :]))
            else:
                nc.vector.wait_ge(mm.h, pend["mm_last"])
                nc.vector.wait_ge(asem[ob].h, asem[ob].n)
                dve.inc(v.reciprocal(rsum32[0:1, ob * QB:(ob + 1) * QB], sum_ps[ob][0:1, :]))
            v.wait_ge(dve.h, dve.n)
            dve.inc(v.tensor_scalar_mul(rsum[0:1, ob * QB:(ob + 1) * QB],
                                        rsum32[0:1, ob * QB:(ob + 1) * QB], 1.0))
            pend["recip"] = dve.n

        def norm_rb():
            if not pend:
                return
            ob = pend["obank"]
            nc.tensor.wait_ge(dve.h, pend["recip"])
            if L1 and pend_rbfree[0]:
                nc.tensor.wait_ge(dve.h, pend_rbfree[0])
            tgt = nm_ps if L1 else sum_ps[ob]
            mm.inc(nc.tensor.matmul(tgt[:, :], ones_rbf[0:1, :],
                                    rsum[0:1, ob * QB:(ob + 1) * QB],
                                    start=True, stop=True))
            pend["rb"] = mm.n

        def norm_tail():
            if not pend:
                return
            ob = pend["obank"]
            nc.vector.wait_ge(mm.h, pend["rb"])
            if L1:
                dve.inc(v.tensor_tensor(ao_sb[0:HD, ob * QB:(ob + 1) * QB],
                                        nm_ps[0:HD, :], t_sb[0:HD, :], ALU.mult))
                rbfree = dve.n
            else:
                dve.inc(v.tensor_scalar_mul(rb_sb[:, ob * QB:(ob + 1) * QB],
                                            sum_ps[ob][:, :], 1.0))
                rbfree = dve.n
                v.wait_ge(dve.h, dve.n)
                dve.inc(v.tensor_tensor(ao_sb[0:HD, ob * QB:(ob + 1) * QB],
                                        o_ps[ob][0:HD, :],
                                        rb_sb[0:HD, ob * QB:(ob + 1) * QB], ALU.mult))
            nc.sync.wait_ge(dve.h, dve.n)
            d = nc.sync.dma_start(
                out=offap(attnout[pend["h"] * HD:(pend["h"] + 1) * HD,
                                  pend["qb"] * QB:(pend["qb"] + 1) * QB], roff),
                in_=ao_sb[0:HD, ob * QB:(ob + 1) * QB])
            asem[ob].inc(d, 16)
            norm_hist[pend["qi"]] = {"tt": dve.n, "rbfree": rbfree,
                                     "addB": pend.get("addB", 0)}
            pend_rbfree[0] = rbfree if L1 else 0
            pend.clear()

        load_kv(0)
        qi_total = 0
        for h in range(NQ):
            if h + 1 < NQ:
                load_kv(h + 1)
            hb = h % 2
            for qb in range(NQB):
                qi = qi_total
                if qb == 0:
                    load_q(h, 0, qi)
                if qb + 1 < NQB:
                    load_q(h, qb + 1, qi + 1)
                qbuf = qi % 2
                kts = kt_list(qb)
                n = len(kts)
                pairs = [kts[2 * j:2 * j + 2] for j in range((n + 1) // 2)]
                npairs = len(pairs)
                obank = qi % 2
                gp_base = len(sp_cnt_hist)
                j_recip = min(1, npairs - 1)
                j_rb = min(3, npairs - 1)
                j_tail = min(4, npairs - 1)

                def emit_SP(j):
                    gpj = gp_base + j
                    buf = gpj % 2
                    if gpj >= 2:
                        nc.tensor.wait_ge(act.h, exp_free_hist[gpj - 2])
                    for idx, (kt, moff) in enumerate(pairs[j]):
                        if L1:
                            ins = nc.tensor.matmul(
                                s_pair[buf][:, idx * QB:(idx + 1) * QB],
                                k_sb[hb][64 * idx:64 * idx + 64, kt * 128:(kt + 1) * 128],
                                q_sb[qbuf][64 * idx:64 * idx + 64, :],
                                start=True, stop=True, tile_position=(64 * idx, 0))
                        else:
                            ins = nc.tensor.matmul(
                                s_pair[buf][:, idx * QB:(idx + 1) * QB],
                                k_sb[hb][:, kt * 128:(kt + 1) * 128],
                                q_sb[qbuf][:, :], start=True, stop=True)
                        mm.inc(ins)
                    sp_cnt_hist.append(mm.n)

                norm_top()
                nc.tensor.wait_ge(kvsem[kvld[h][0]].h, kvld[h][1])
                nc.tensor.wait_ge(q2sem[qld[qi][0]].h, qld[qi][1])
                emit_SP(0)
                if npairs > 1:
                    emit_SP(1)
                for j in range(npairs):
                    gpj = gp_base + j
                    buf = gpj % 2
                    ebuf = gpj % 3
                    pr = pairs[j]
                    W = len(pr) * QB
                    nc.scalar.wait_ge(mm.h, sp_cnt_hist[gpj])
                    if gpj >= 3:
                        nc.scalar.wait_ge(mm.h, av_free_hist[gpj - 3])
                    act.inc(nc.scalar.activation(e_pair[ebuf][:, 0:W], s_pair[buf][:, 0:W],
                                                 AF.Exp, scale=cst[:, 2:3]))
                    exp_free_hist.append(act.n)
                    rdy = []
                    for idx, (kt, moff) in enumerate(pr):
                        if moff is not None:
                            nc.vector.wait_ge(act.h, act.n)
                            dve.inc(v.tensor_tensor(
                                e_pair[ebuf][:, idx * QB:(idx + 1) * QB],
                                e_pair[ebuf][:, idx * QB:(idx + 1) * QB],
                                mask_sb[:, moff:moff + QB], ALU.mult))
                            rdy.append((dve, dve.n))
                        else:
                            rdy.append((act, act.n))
                    if j == j_recip:
                        norm_recip()
                    if j == j_rb:
                        norm_rb()
                    if j == j_tail:
                        norm_tail()
                    if j + 2 < npairs:
                        emit_SP(j + 2)
                    first = j == 0
                    lastp = j == npairs - 1
                    for idx, (kt, moff) in enumerate(pr):
                        sem_c, cnt = rdy[idx]
                        nc.tensor.wait_ge(sem_c.h, cnt)
                        kfirst = first and idx == 0
                        klast = lastp and idx == len(pr) - 1
                        if L1:
                            if kfirst and pend and pend.get("addB"):
                                nc.tensor.wait_ge(dve.h, pend["addB"])
                            for ph in range(2):
                                ins = nc.tensor.matmul(
                                    (o_psA if ph == 0 else o_psB)[:, :],
                                    v_sb[hb][64 * ph:64 * ph + 64, kt * VST:kt * VST + VST],
                                    e_pair[ebuf][64 * ph:64 * ph + 64, idx * QB:(idx + 1) * QB],
                                    start=kfirst, stop=klast, tile_position=(64 * ph, 0))
                                mm.inc(ins)
                        else:
                            if kfirst and qi >= 2:
                                nc.tensor.wait_ge(dve.h, norm_hist[qi - 2]["tt"])
                                nc.tensor.wait_ge(dve.h, norm_hist[qi - 2]["rbfree"])
                            ins = nc.tensor.matmul(
                                o_ps[obank][:, :], v_sb[hb][:, kt * VST:(kt + 1) * VST],
                                e_pair[ebuf][:, idx * QB:(idx + 1) * QB],
                                start=kfirst, stop=klast)
                            mm.inc(ins)
                            ins = nc.tensor.matmul(
                                sum_ps[obank][0:1, :], ones_bf[:, 0:1],
                                e_pair[ebuf][:, idx * QB:(idx + 1) * QB],
                                start=kfirst, stop=klast)
                            mm.inc(ins)
                    av_free_hist.append(mm.n)
                qs_hist[qi] = mm.n
                pend.update(qi=qi, obank=obank, h=h, qb=qb, mm_last=mm.n)
                qi_total += 1
            last_head_mm[0] = qs_hist[qi_total - 1]
        norm_top()
        norm_recip()
        norm_rb()
        norm_tail()
        P.barrier_engines(extra=asem + kvsem + q2sem)
    nc.sync.wait_ge(st.h, st.n)


def build_attn(layer):
    P = Prog()
    nc = P.nc
    if layer == 0:
        QW, KW, MW, VW = 1024, 1024, 1024, 1024
    else:
        QW, KW, MW, VW = 1024, 256, 2688, 260
    io = {
        "xT": nc.dram_tensor("xT", [D, S], F32, kind="ExternalInput").ap(),
        "wqT": nc.dram_tensor("wqT", [D, QW], F32, kind="ExternalInput").ap(),
        "wkT": nc.dram_tensor("wkT", [D, KW], F32, kind="ExternalInput").ap(),
        "wvT": nc.dram_tensor("wvT", [D, KW], F32, kind="ExternalInput").ap(),
        "maskm": nc.dram_tensor("maskm", [128, MW], BF16, kind="ExternalInput").ap(),
        "attnout": nc.dram_tensor("attnout", [QW, S], BF16, kind="ExternalOutput").ap(),
        "qh": nc.dram_tensor("qh", [QW, S], BF16).ap(),
        "kh": nc.dram_tensor("kh", [KW, S], BF16).ap(),
        "vh": nc.dram_tensor("vh", [S, VW], BF16).ap(),
    }
    with P.scope():
        emit_attn(P, layer, io)
    return P




def emit_ffn0(P, io, pfx="f0_"):
    nc = P.nc
    v = nc.vector
    mm, act, dve, ld, st, lg = P.mm, P.act, P.dve, P.ld, P.st, P.lg
    T = TOK
    TBA = 512
    NTA = T // TBA
    TBB = 256
    NTB = T // TBB
    NE = 8
    toff = io.get("toff", 0)
    xT = io["xT"]; attnT = io["attnT"]; x2T = io["x2T"]
    woT = io["woT"]; down_w = io["down_w"]; down_b = io["down_b"]
    up_w = io["up_w"]; upcore_b = io["upcore_b"]
    router_w = io["router_w"]; router_b = io["router_b"]; cost_t = io["cost_t"]
    shared_w1 = io["shared_w1"]; shared_w2 = io["shared_w2"]; shared_b = io["shared_b"]
    core_w = io["core_w"]; ew1 = io["ew1"]; ew2 = io["ew2"]
    eb1 = io["eb1"]; eb2 = io["eb2"]; idn_h = io["idn_h"]; sel_h = io["sel_h"]
    x1h = io["x1h"]; ln2h = io["ln2h"]; gph = io["gph"]
    EHID = HID + [1024]

    ones_bf = P.sbuf(pfx + "ones_bf", [128, 1], BF16)
    ones_row = P.sbuf(pfx + "ones_row", [1, 128], F32)
    cst = P.sbuf(pfx + "cst", [128, 3], F32)
    idn = P.sbuf(pfx + "idn", [128, 128], F32)
    nc.gpsimd.memset(ones_bf[:, :], 1.0)
    nc.gpsimd.memset(ones_row[:, :], 1.0)
    nc.gpsimd.memset(cst[:, 0:1], EPS)
    P.gp.inc(nc.gpsimd.memset(cst[:, 1:2], -0.5))
    nc.sync.dma_start(out=idn[:, :], in_=idn_h[:, :]).then_inc(ld.h, 16); ld.n += 16
    sel_sb = P.sbuf(pfx + "sel_sb", [8, 1024], BF16)
    nc.sync.dma_start(out=sel_sb[:, :], in_=sel_h[:, :]).then_inc(ld.h, 16); ld.n += 16

    pool = P.psum(pfx + "pool", [128, 4096], F32)
    ln_st = pool[0:1, 0:1024]
    ln_bc = pool[:, 1024:1536]
    ps_m = pool[:, 1024:1536]
    ps_x = pool[:, 1536:2048]

    x_tb = P.sbuf(pfx + "x_tb", [128, FT * 512], F32)
    ln_tb = P.sbuf(pfx + "ln_tb", [128, FT * 512], BF16)
    rw_sb = P.sbuf(pfx + "rw_sb", [128, FT * 8], BF16)
    cb_sb = P.sbuf(pfx + "cb_sb", [128, T], BF16)
    ct_bf = P.sbuf(pfx + "ct_bf", [8, T], BF16)
    b2c_sb = P.sbuf(pfx + "b2c_sb", [8, 1024], BF16)
    ctot_sb = P.sbuf(pfx + "ctot_sb", [128, 8], F32)
    biases = P.sbuf(pfx + "biases", [128, 304], F32)
    db_c = biases[:, 0:8]
    ucb_c = biases[:, 8:40]
    fb_c = biases[:, 40:56]
    sb_c = biases[:, 56:72]
    b1_c = biases[:, 72:232]
    b2_c = biases[:, 232:296]
    small = P.sbuf(pfx + "small", [128, 64], F32)
    rbadj = small[0:1, 0:8]
    coef = P.sbuf(pfx + "coef", [128, 64], F32)
    gb1_sb = P.sbuf(pfx + "gb1_sb", [128, 160], BF16)
    ce_sb = P.sbuf(pfx + "ce_sb", [128, 64], F32)
    b2m = P.sbuf(pfx + "b2m", [128, 64], F32)

    def bar():
        P.barrier_engines()

    nc.gpsimd.wait_ge(lg.h, lg.n)
    nc.gpsimd.dma_start(out=rw_sb[:, :].rearrange("p (f c) -> p f c", f=FT),
                        in_=hbm_fm_view(router_w)).then_inc(lg.h, 16); lg.n += 16
    for dst, srcap in ((db_c, down_b), (ucb_c, upcore_b), (sb_c, shared_b),
                       (b1_c, eb1), (b2_c, eb2)):
        nc.sync.dma_start(out=dst, in_=srcap[:, :]).then_inc(ld.h, 16); ld.n += 16
    nc.sync.dma_start(out=small[0:1, 8:16], in_=router_b[:, :]).then_inc(ld.h, 16); ld.n += 16
    nc.sync.dma_start(out=small[0:1, 16:24], in_=cost_t[:, :]).then_inc(ld.h, 16); ld.n += 16
    bar()
    dve.inc(v.tensor_sub(rbadj, small[0:1, 8:16], small[0:1, 16:24]))
    dve.inc(v.tensor_add(fb_c, ucb_c[:, 0:16], ucb_c[:, 16:32]))
    bar()

    xv = hbm_fm_view(xT)
    av = hbm_fm_view(attnT)
    x1v = hbm_fm_view(x1h)
    lnv = hbm_fm_view(ln2h)

    P.mark("f0A")
    a_scope = P.scope(); a_scope.__enter__()
    big = P.sbuf(pfx + "big", [128, 3 * FT * 512], BF16)
    attn_tb = big[:, 0:FT * 512]
    ln_xb = big[:, FT * 512:2 * FT * 512]
    ln_sq = big[:, 2 * FT * 512:3 * FT * 512]
    lnscr = LnScratch(P, 512, name=pfx + "lns", xb=ln_xb, sq=ln_sq,
                      st_ps=pool[0:1, 0:1024], bc_ps=ln_bc)
    wo_sb = P.sbuf(pfx + "wo_sb", [128, FT * FT * 128], BF16)
    load_weight_fm(P, wo_sb, woT, FT, FT)
    psx = [pool[:, 1536:2048], pool[:, 2048:2560]]
    nc.tensor.wait_ge(lg.h, lg.n)
    add_cnt = {}
    gm = 0
    for tb in range(NTA):
        tsl = slice(tb * TBA, (tb + 1) * TBA)
        nc.sync.wait_ge(st.h, st.n)
        nc.sync.wait_ge(mm.h, mm.n)
        nc.sync.wait_ge(dve.h, dve.n)
        nc.sync.dma_start(out=x_tb[:, :].rearrange("p (f t) -> p f t", f=FT),
                          in_=offap(xv[:, :, tsl], toff)).then_inc(ld.h, 16); ld.n += 16
        nc.sync.dma_start(out=attn_tb.rearrange("p (f t) -> p f t", f=FT),
                          in_=offap(av[:, :, tsl], toff)).then_inc(ld.h, 16); ld.n += 16
        nc.tensor.wait_ge(ld.h, ld.n)
        for m in range(FT):
            if gm >= 2:
                nc.tensor.wait_ge(dve.h, add_cnt[gm - 2])
            for ft in range(FT):
                ins = nc.tensor.matmul(psx[gm % 2][:, :], w_tile(wo_sb, m, ft, FT),
                                       attn_tb[:, ft * 512:(ft + 1) * 512],
                                       start=(ft == 0), stop=(ft == FT - 1))
            mm.inc(ins)
            nc.vector.wait_ge(mm.h, mm.n)
            dve.inc(v.tensor_add(x_tb[:, m * 512:(m + 1) * 512],
                                 psx[gm % 2][:, :], x_tb[:, m * 512:(m + 1) * 512]))
            add_cnt[gm] = dve.n
            gm += 1
        nc.sync.wait_ge(dve.h, dve.n)
        nc.sync.dma_start(out=x1v[:, :, tsl],
                          in_=x_tb[:, :].rearrange("p (f t) -> p f t", f=FT)).then_inc(st.h, 16)
        st.n += 16
        emit_ln(P, lnscr,
                lambda ft: x_tb[:, ft * 512:(ft + 1) * 512],
                lambda ft: ln_tb[:, ft * 512:(ft + 1) * 512],
                512, (ones_bf, ones_row, cst))
        nc.sync.wait_ge(dve.h, dve.n)
        nc.sync.dma_start(out=lnv[:, :, tsl],
                          in_=ln_tb[:, :].rearrange("p (f t) -> p f t", f=FT)).then_inc(st.h, 16)
        st.n += 16
        for e in (nc.vector, nc.tensor, nc.scalar):
            e.wait_ge(st.h, st.n)
    bar()
    a_scope.__exit__(None, None, None)

    P.mark("f0B")
    b_scope = P.scope(); b_scope.__enter__()
    down_sb = P.sbuf(pfx + "down_sb", [128, FT * 8 * 128], BF16)
    g_sb = P.sbuf(pfx + "g_sb", [128, 8 * T], BF16)
    load_weight_fm(P, down_sb, down_w, 8, FT)
    nc.tensor.wait_ge(lg.h, lg.n)
    for tb in range(NTB):
        tsl = slice(tb * TBB, (tb + 1) * TBB)
        nc.sync.wait_ge(mm.h, mm.n)
        nc.sync.dma_start(out=ln_tb[:, 0:FT * TBB].rearrange("p (f t) -> p f t", f=FT),
                          in_=lnv[:, :, tsl]).then_inc(ld.h, 16); ld.n += 16
        nc.tensor.wait_ge(ld.h, ld.n)
        for m in range(8):
            nc.tensor.wait_ge(act.h, act.n)
            for ft in range(FT):
                ins = nc.tensor.matmul(ps_x[:, 0:TBB], w_tile(down_sb, m, ft, 8),
                                       ln_tb[:, ft * TBB:(ft + 1) * TBB],
                                       start=(ft == 0), stop=(ft == FT - 1))
            mm.inc(ins)
            nc.scalar.wait_ge(mm.h, mm.n)
            act.inc(nc.scalar.activation(g_sb[:, m * T + tb * TBB: m * T + (tb + 1) * TBB],
                                         ps_x[:, 0:TBB], GELU, bias=db_c[:, m:m + 1]))
    bar()

    P.mark("f0C")
    for tt in range(T // 128):
        tbb = (tt * 128) // TBB
        o = (tt * 128) % TBB
        if o == 0:
            nc.sync.wait_ge(mm.h, mm.n)
            nc.sync.dma_start(out=ln_tb[:, 0:FT * TBB].rearrange("p (f t) -> p f t", f=FT),
                              in_=lnv[:, :, tbb * TBB:(tbb + 1) * TBB]).then_inc(ld.h, 16)
            ld.n += 16
        nc.tensor.wait_ge(ld.h, ld.n)
        nc.tensor.wait_ge(act.h, act.n)
        for ft in range(FT):
            nc.tensor.matmul(ps_m[:, 0:8],
                             ln_tb[:, ft * TBB + o: ft * TBB + o + 128],
                             rw_sb[:, ft * 8:(ft + 1) * 8],
                             start=(ft == 0), stop=False)
        mm.inc(nc.tensor.matmul(ps_m[:, 0:8], ones_row[0:1, :], rbadj,
                                start=False, stop=True))
        nc.vector.wait_ge(mm.h, mm.n)
        mx = coef[:, 8:9]; sm = coef[:, 9:10]; rr = coef[:, 10:11]
        m1 = coef[:, 11:12]; m2 = coef[:, 12:13]; dd = coef[:, 13:14]; g1 = coef[:, 14:15]
        e_r = coef[:, 16:24]; sel1 = coef[:, 24:32]; msk = coef[:, 32:40]
        sel2 = coef[:, 40:48]; cc = coef[:, 48:56]
        dve.inc(v.reduce_max(mx, ps_m[:, 0:8], mybir.AxisListType.X, negate=True))
        nc.scalar.wait_ge(dve.h, dve.n)
        act.inc(nc.scalar.activation(e_r, ps_m[:, 0:8], AF.Exp, bias=mx))
        nc.vector.wait_ge(act.h, act.n)
        dve.inc(v.reduce_sum(sm, e_r, mybir.AxisListType.X)); v.wait_ge(dve.h, dve.n)
        dve.inc(v.reciprocal(rr, sm))
        dve.inc(v.reduce_max(m1, e_r, mybir.AxisListType.X)); v.wait_ge(dve.h, dve.n)
        dve.inc(v.tensor_scalar(sel1, e_r, m1, None, ALU.is_equal)); v.wait_ge(dve.h, dve.n)
        dve.inc(v.scalar_tensor_tensor(msk, sel1, -1e30, e_r, ALU.mult, ALU.add)); v.wait_ge(dve.h, dve.n)
        dve.inc(v.reduce_max(m2, msk, mybir.AxisListType.X)); v.wait_ge(dve.h, dve.n)
        dve.inc(v.tensor_scalar(sel2, msk, m2, None, ALU.is_equal))
        dve.inc(v.tensor_sub(dd, m1, m2)); v.wait_ge(dve.h, dve.n)
        dve.inc(v.tensor_tensor(dd, dd, rr, ALU.mult)); nc.scalar.wait_ge(dve.h, dve.n)
        act.inc(nc.scalar.activation(g1, dd, AF.Sigmoid))
        nc.vector.wait_ge(act.h, act.n)
        dve.inc(v.tensor_scalar(cc, sel1, g1, None, ALU.mult))
        dve.inc(v.tensor_scalar(msk, sel2, g1, None, ALU.mult)); v.wait_ge(dve.h, dve.n)
        dve.inc(v.tensor_sub(msk, sel2, msk)); v.wait_ge(dve.h, dve.n)
        dve.inc(v.tensor_add(cc, cc, msk)); nc.tensor.wait_ge(dve.h, dve.n)
        mm.inc(nc.tensor.transpose(ps_m[0:8, 128:256], cc, idn[:, :]))
        nc.scalar.wait_ge(mm.h, mm.n)
        act.inc(nc.scalar.copy(ct_bf[:, tt * 128:(tt + 1) * 128], ps_m[0:8, 128:256]))
        nc.vector.wait_ge(act.h, act.n)
    bar()

    P.mark("f0D")
    act.inc(nc.scalar.activation(gb1_sb[:, 0:160], b1_c, GELU))
    bar()
    hofs = [0]
    for e in range(NE):
        hofs.append(hofs[-1] + HID[e] // 128)
    with P.scope():
        w2big = P.sbuf(pfx + "w2big", [128, 32 * 1024], BF16)
        for e in range(NE):
            nh = HID[e] // 128
            nc.gpsimd.wait_ge(lg.h, lg.n)
            nc.gpsimd.wait_ge(mm.h, mm.n)
            nc.gpsimd.dma_start(
                out=w2big[:, 0:nh * 1024].rearrange("p (k c) -> p k c", k=nh),
                in_=ew2[e].rearrange("(k p) c -> p k c", p=128)).then_inc(lg.h, 16)
            lg.n += 16
            nc.tensor.wait_ge(lg.h, lg.n)
            nc.tensor.wait_ge(act.h, act.n)
            for m2 in range(8):
                for hk in range(nh):
                    ins = nc.tensor.matmul(ps_m[:, 256 + m2:257 + m2],
                                           w2big[:, hk * 1024 + m2 * 128: hk * 1024 + (m2 + 1) * 128],
                                           gb1_sb[:, hofs[e] + hk: hofs[e] + hk + 1],
                                           start=(hk == 0), stop=(hk == nh - 1))
                mm.inc(ins)
            nc.vector.wait_ge(mm.h, mm.n)
            dve.inc(v.tensor_add(ce_sb[:, e:e + 57:8], ps_m[:, 256:264], b2_c[:, e:e + 57:8]))
            nc.tensor.wait_ge(dve.h, dve.n)
    bar()
    for m2 in range(8):
        dve.inc(v.reduce_sum(ctot_sb[:, m2:m2 + 1], ce_sb[:, m2 * 8:(m2 + 1) * 8],
                             mybir.AxisListType.X))
    v.wait_ge(dve.h, dve.n)
    dve.inc(v.scalar_tensor_tensor(ctot_sb[:, :], sb_c[:, 8:16], 0.1, ctot_sb[:, :],
                                   ALU.mult, ALU.add))
    dve.inc(v.tensor_sub(b2m[:, :], b2_c, ce_sb[:, :]))
    nc.tensor.wait_ge(dve.h, dve.n)
    for m2 in range(8):
        colap = b2m[:, m2 * 8:(m2 + 1) * 8]
        nc.tensor.wait_ge(act.h, act.n)
        mm.inc(nc.tensor.transpose(ps_m[0:8, 128:256], colap, idn[:, :]))
        nc.scalar.wait_ge(mm.h, mm.n)
        act.inc(nc.scalar.copy(b2c_sb[:, m2 * 128:(m2 + 1) * 128], ps_m[0:8, 128:256]))
    bar()

    P.mark("f0E1")
    with P.scope():
        TBE = 512
        NTE = T // TBE
        wsB2 = P.sbuf(pfx + "wsB2", [128, 2 * 8 * 128], BF16)
        gpw = P.sbuf(pfx + "gpw", [128, 2 * T], BF16)
        a_ps = [pool[:, 0:512], pool[:, 512:1024]]
        w1sem = [Sem(nc, pfx + f"w1sem{i}") for i in range(2)]
        gpsem = [Sem(nc, pfx + f"gpsem{i}") for i in range(2)]
        exphist = []
        mulhist = []
        wload = {}
        ehm = [(e, hm) for e in range(9) for hm in range((HID + [1024])[e] // 128)]

        def load_w1(idx):
            e, hm = ehm[idx]
            sl = idx % 2
            w1src = ew1[e] if e < NE else shared_w1
            nc.gpsimd.wait_ge(lg.h, lg.n)
            if idx >= 2:
                nc.gpsimd.wait_ge(mm.h, wuse[idx - 2])
            d = nc.gpsimd.dma_start(
                out=wsB2[:, sl * 1024:(sl + 1) * 1024].rearrange("p (k c) -> p k c", k=8),
                in_=w1src.rearrange("(k p) c -> p k c", p=128)[:, :, hm * 128:(hm + 1) * 128])
            w1sem[sl].inc(d, 16)
            wload[idx] = (sl, w1sem[sl].n)

        wuse = {}
        load_w1(0)
        load_w1(1)
        eprev = -1
        for idx, (e, hm) in enumerate(ehm):
            if e != eprev:
                eprev = e
                if e < NE:
                    for tb4 in range(NTA):
                        nc.tensor.wait_ge(act.h, act.n)
                        nc.tensor.wait_ge(dve.h, dve.n)
                        mm.inc(nc.tensor.matmul(ps_m[:, 0:512], sel_sb[:, e * 128:(e + 1) * 128],
                                                ct_bf[:, tb4 * 512:(tb4 + 1) * 512],
                                                start=True, stop=True))
                        nc.scalar.wait_ge(mm.h, mm.n)
                        act.inc(nc.scalar.copy(cb_sb[:, tb4 * 512:(tb4 + 1) * 512], ps_m[:, 0:512]))
                    nc.vector.wait_ge(act.h, act.n)
            sl, wtgt = wload[idx]
            gsl = idx % 2
            bias = (b1_c[:, hofs[e] + hm: hofs[e] + hm + 1] if e < NE
                    else sb_c[:, hm:hm + 1])
            nc.tensor.wait_ge(w1sem[sl].h, wtgt)
            for tb in range(NTE):
                bank = tb % 2
                if len(exphist) >= 2:
                    nc.tensor.wait_ge(act.h, exphist[-2])
                for kk in range(8):
                    ins = nc.tensor.matmul(a_ps[bank][:, :],
                                           wsB2[:, sl * 1024 + kk * 128: sl * 1024 + (kk + 1) * 128],
                                           g_sb[:, kk * T + tb * TBE: kk * T + (tb + 1) * TBE],
                                           start=(kk == 0), stop=(kk == 7))
                mm.inc(ins)
                nc.scalar.wait_ge(mm.h, mm.n)
                if tb == 0 and idx >= 2:
                    nc.scalar.wait_ge(gpsem[gsl].h, gpsem[gsl].n)
                act.inc(nc.scalar.activation(gpw[:, gsl * T + tb * TBE: gsl * T + (tb + 1) * TBE],
                                             a_ps[bank][:, :], GELU, bias=bias))
                exphist.append(act.n)
                nc.vector.wait_ge(act.h, act.n)
                if e < NE:
                    dve.inc(v.tensor_tensor(gpw[:, gsl * T + tb * TBE: gsl * T + (tb + 1) * TBE],
                                            gpw[:, gsl * T + tb * TBE: gsl * T + (tb + 1) * TBE],
                                            cb_sb[:, tb * TBE:(tb + 1) * TBE], ALU.mult))
                else:
                    dve.inc(v.tensor_scalar_mul(gpw[:, gsl * T + tb * TBE: gsl * T + (tb + 1) * TBE],
                                                gpw[:, gsl * T + tb * TBE: gsl * T + (tb + 1) * TBE], 0.1))
            wuse[idx] = mm.n
            if idx + 2 < len(ehm):
                load_w1(idx + 2)
            nc.sync.wait_ge(dve.h, dve.n)
            d = nc.sync.dma_start(out=gph[e][hm * 128:(hm + 1) * 128, :],
                                  in_=gpw[:, gsl * T:(gsl + 1) * T])
            gpsem[gsl].inc(d, 16)
        P.barrier_engines(extra=gpsem + w1sem)
    b_scope.__exit__(None, None, None)

    P.mark("f0E2")
    with P.scope():
        TBS = 512
        NTS = T // TBS
        GRP = 6
        S_sb = P.sbuf(pfx + "S_sb", [128, 8 * T], F32)
        e2_scope = P.scope(); e2_scope.__enter__()
        w2s = P.sbuf(pfx + "w2s", [128, 2 * GRP * 1024], BF16)
        gprd = P.sbuf(pfx + "gprd", [128, 2 * GRP * T], BF16)
        w2sem = [Sem(nc, pfx + f"w2sem{i}") for i in range(2)]
        gdsem = [Sem(nc, pfx + f"gdsem{i}") for i in range(2)]
        psets = [pool[:, 0:2048], pool[:, 2048:4096]]

        allhk = [(e, hk) for e in range(9) for hk in range((HID + [1024])[e] // 128)]
        groups = [allhk[i:i + GRP] for i in range(0, len(allhk), GRP)]
        grpuse = {}
        evhist = {}

        def load_grp(gi):
            sl = gi % 2
            if gi >= 2:
                nc.gpsimd.wait_ge(mm.h, grpuse[gi - 2])
                nc.sync.wait_ge(mm.h, grpuse[gi - 2])
            for j, (e, hk) in enumerate(groups[gi]):
                w2src = ew2[e] if e < NE else shared_w2
                nc.gpsimd.wait_ge(lg.h, lg.n)
                d = nc.gpsimd.dma_start(
                    out=w2s[:, (sl * GRP + j) * 1024:(sl * GRP + j + 1) * 1024],
                    in_=w2src[hk * 128:(hk + 1) * 128, :])
                w2sem[sl].inc(d, 16)
                d = nc.sync.dma_start(
                    out=gprd[:, (sl * GRP + j) * T:(sl * GRP + j + 1) * T],
                    in_=gph[e][hk * 128:(hk + 1) * 128, :])
                gdsem[sl].inc(d, 16)
            gload[gi] = (sl, w2sem[sl].n, gdsem[sl].n)

        gload = {}
        load_grp(0)
        hidx = 0
        for gi, grp in enumerate(groups):
            if gi + 1 < len(groups):
                load_grp(gi + 1)
            sl, wt, gt = gload[gi]
            nc.tensor.wait_ge(w2sem[sl].h, wt)
            nc.tensor.wait_ge(gdsem[sl].h, gt)
            for tb in range(NTS):
                for hf in range(2):
                    pset = psets[hf]
                    nc.tensor.wait_ge(dve.h, evhist.get(hidx - 2, 0))
                    for j in range(len(grp)):
                        for mi in range(4):
                            m2 = hf * 4 + mi
                            ins = nc.tensor.matmul(
                                pset[:, mi * 512:(mi + 1) * 512],
                                w2s[:, (sl * GRP + j) * 1024 + m2 * 128:
                                    (sl * GRP + j) * 1024 + (m2 + 1) * 128],
                                gprd[:, (sl * GRP + j) * T + tb * TBS:
                                     (sl * GRP + j) * T + (tb + 1) * TBS],
                                start=(j == 0), stop=(j == len(grp) - 1))
                    mm.inc(ins)
                    nc.vector.wait_ge(mm.h, mm.n)
                    for mi in range(4):
                        m2 = hf * 4 + mi
                        dst = S_sb[:, m2 * T + tb * TBS: m2 * T + (tb + 1) * TBS]
                        if gi == 0:
                            ins = v.tensor_copy(dst, pset[:, mi * 512:(mi + 1) * 512])
                        else:
                            ins = v.tensor_tensor(dst, pset[:, mi * 512:(mi + 1) * 512],
                                                  dst, ALU.add)
                    dve.inc(ins)
                    evhist[hidx] = dve.n
                    hidx += 1
            grpuse[gi] = mm.n
        P.barrier_engines(extra=w2sem + gdsem)
        for tb in range(NTA):
            tsl = slice(tb * TBA, (tb + 1) * TBA)
            for m2 in range(8):
                ins = nc.tensor.matmul(pool[:, (m2 % 4) * 512:(m2 % 4 + 1) * 512],
                                       b2c_sb[:, m2 * 128:(m2 + 1) * 128],
                                       ct_bf[:, tsl], start=True, stop=True)
                mm.inc(ins)
                nc.vector.wait_ge(mm.h, mm.n)
                ins2 = v.scalar_tensor_tensor(
                    S_sb[:, m2 * T + tb * TBA: m2 * T + (tb + 1) * TBA],
                    pool[:, (m2 % 4) * 512:(m2 % 4 + 1) * 512],
                    ctot_sb[:, m2:m2 + 1],
                    S_sb[:, m2 * T + tb * TBA: m2 * T + (tb + 1) * TBA],
                    ALU.add, ALU.add)
                dve.inc(ins2)
                nc.tensor.wait_ge(dve.h, dve.n)
        bar()

        e2_scope.__exit__(None, None, None)
        P.mark("f0F")
        fevac = {}
        gln_tb = P.sbuf(pfx + "gln_tb", [128, FT * 512], BF16)
        smid_bf = P.sbuf(pfx + "smid", [128, 8 * 512], BF16)
        wsA2 = P.sbuf(pfx + "wsA2", [128, 2 * FT * 128], BF16)
        wsU2 = P.sbuf(pfx + "wsU2", [128, 2 * 8 * 128], BF16)
        casem = [Sem(nc, pfx + f"casem{i}") for i in range(2)]
        cusem = [Sem(nc, pfx + f"cusem{i}") for i in range(2)]
        fuse = {}
        fload = {}

        def load_f(fi):
            tb, m = divmod(fi, FT)
            sl = fi % 2
            nc.gpsimd.wait_ge(lg.h, lg.n)
            if fi >= 2:
                nc.gpsimd.wait_ge(mm.h, fuse[fi - 2])
            d = nc.gpsimd.dma_start(
                out=wsA2[:, sl * 2048:(sl + 1) * 2048].rearrange("p (f c) -> p f c", f=FT),
                in_=hbm_fm_view(core_w)[:, :, m * 128:(m + 1) * 128])
            casem[sl].inc(d, 16)
            nc.gpsimd.wait_ge(lg.h, lg.n)
            d = nc.gpsimd.dma_start(
                out=wsU2[:, sl * 1024:(sl + 1) * 1024].rearrange("p (k c) -> p k c", k=8),
                in_=up_w.rearrange("(k p) c -> p k c", p=128)[:, :, m * 128:(m + 1) * 128])
            cusem[sl].inc(d, 16)
            fload[fi] = (sl, casem[sl].n, cusem[sl].n)

        load_f(0)
        load_f(1)
        for tb in range(NTA):
            tsl = slice(tb * TBA, (tb + 1) * TBA)
            nc.sync.wait_ge(st.h, st.n)
            nc.sync.wait_ge(dve.h, dve.n)
            nc.sync.dma_start(out=ln_tb[:, :].rearrange("p (f t) -> p f t", f=FT),
                              in_=lnv[:, :, tsl]).then_inc(ld.h, 16); ld.n += 16
            nc.sync.dma_start(out=x_tb[:, :].rearrange("p (f t) -> p f t", f=FT),
                              in_=x1v[:, :, tsl]).then_inc(ld.h, 16); ld.n += 16
            nc.scalar.wait_ge(ld.h, ld.n)
            nc.scalar.wait_ge(dve.h, dve.n)
            for ft in range(FT):
                ins = nc.scalar.activation(gln_tb[:, ft * 512:(ft + 1) * 512],
                                           ln_tb[:, ft * 512:(ft + 1) * 512], GELU)
            act.inc(ins)
            nc.vector.wait_ge(act.h, act.n)
            for m2 in range(8):
                ins = v.tensor_copy(smid_bf[:, m2 * 512:(m2 + 1) * 512],
                                    S_sb[:, m2 * T + tb * TBA: m2 * T + (tb + 1) * TBA])
            dve.inc(ins)
            for e_ in (nc.tensor,):
                e_.wait_ge(act.h, act.n); e_.wait_ge(dve.h, dve.n)
            for m in range(FT):
                fi = tb * FT + m
                sl, ca, cu = fload[fi]
                nc.tensor.wait_ge(casem[sl].h, ca)
                nc.tensor.wait_ge(cusem[sl].h, cu)
                bank = fi % 2
                pf = pool[:, (2 + bank) * 512:(3 + bank) * 512]
                if fi >= 2:
                    nc.tensor.wait_ge(dve.h, fevac[fi - 2])
                for kk in range(8):
                    nc.tensor.matmul(pf, wsU2[:, sl * 1024 + kk * 128: sl * 1024 + (kk + 1) * 128],
                                     smid_bf[:, kk * 512:(kk + 1) * 512],
                                     start=(kk == 0), stop=False)
                for ft in range(FT):
                    ins = nc.tensor.matmul(pf, wsA2[:, sl * 2048 + ft * 128: sl * 2048 + (ft + 1) * 128],
                                           gln_tb[:, ft * 512:(ft + 1) * 512],
                                           start=False, stop=(ft == FT - 1))
                mm.inc(ins)
                fuse[fi] = mm.n
                if fi + 2 < NTA * FT:
                    load_f(fi + 2)
                nc.vector.wait_ge(mm.h, mm.n)
                dve.inc(v.scalar_tensor_tensor(x_tb[:, m * 512:(m + 1) * 512],
                                               pf, fb_c[:, m:m + 1],
                                               x_tb[:, m * 512:(m + 1) * 512],
                                               ALU.add, ALU.add))
                fevac[fi] = dve.n
            nc.sync.wait_ge(dve.h, dve.n)
            nc.sync.dma_start(out=offap(hbm_fm_view(x2T)[:, :, tsl], toff),
                              in_=x_tb[:, :].rearrange("p (f t) -> p f t", f=FT)).then_inc(st.h, 16)
            st.n += 16
        P.barrier_engines(extra=casem + cusem)
    nc.sync.wait_ge(st.h, st.n)


def build_ffn0():
    P = Prog()
    nc = P.nc
    NE = 8
    io = {
        "xT": nc.dram_tensor("xT", [D, TOK], F32, kind="ExternalInput").ap(),
        "attnT": nc.dram_tensor("attnT", [D, TOK], BF16, kind="ExternalInput").ap(),
        "woT": nc.dram_tensor("woT", [D, D], F32, kind="ExternalInput").ap(),
        "down_w": nc.dram_tensor("down_w", [D, 1024], F32, kind="ExternalInput").ap(),
        "down_b": nc.dram_tensor("down_b", [128, 8], F32, kind="ExternalInput").ap(),
        "up_w": nc.dram_tensor("up_w", [1024, D], F32, kind="ExternalInput").ap(),
        "upcore_b": nc.dram_tensor("upcore_b", [128, 32], F32, kind="ExternalInput").ap(),
        "router_w": nc.dram_tensor("router_w", [D, 8], F32, kind="ExternalInput").ap(),
        "router_b": nc.dram_tensor("router_b", [1, 8], F32, kind="ExternalInput").ap(),
        "cost_t": nc.dram_tensor("cost_t", [1, 8], F32, kind="ExternalInput").ap(),
        "shared_w1": nc.dram_tensor("shared_w1", [1024, 1024], F32, kind="ExternalInput").ap(),
        "shared_w2": nc.dram_tensor("shared_w2", [1024, 1024], F32, kind="ExternalInput").ap(),
        "shared_b": nc.dram_tensor("shared_b", [128, 16], F32, kind="ExternalInput").ap(),
        "core_w": nc.dram_tensor("core_w", [D, D], F32, kind="ExternalInput").ap(),
        "ew1": [nc.dram_tensor(f"e{e}_w1", [1024, HID[e]], F32, kind="ExternalInput").ap() for e in range(NE)],
        "ew2": [nc.dram_tensor(f"e{e}_w2", [HID[e], 1024], F32, kind="ExternalInput").ap() for e in range(NE)],
        "eb1": nc.dram_tensor("eb1", [128, 160], F32, kind="ExternalInput").ap(),
        "eb2": nc.dram_tensor("eb2", [128, 64], F32, kind="ExternalInput").ap(),
        "idn_h": nc.dram_tensor("idn_h", [128, 128], F32, kind="ExternalInput").ap(),
        "sel_h": nc.dram_tensor("sel_h", [8, 1024], BF16, kind="ExternalInput").ap(),
        "x2T": nc.dram_tensor("x2T", [D, TOK], F32, kind="ExternalOutput").ap(),
        "x1h": nc.dram_tensor("x1h", [D, TOK], F32).ap(),
        "ln2h": nc.dram_tensor("ln2h", [D, TOK], BF16).ap(),
        "gph": [nc.dram_tensor(f"gph{e}", [(HID + [1024])[e], TOK], BF16).ap() for e in range(9)],
    }
    with P.scope():
        emit_ffn0(P, io)
    return P



def emit_ffn1(P, io, pfx="g_"):
    nc = P.nc
    v = nc.vector
    mm, act, dve, ld, st, lg = P.mm, P.act, P.dve, P.ld, P.st, P.lg
    T = TOK
    TBA = 512
    NTA = T // TBA
    NH = 44
    toff = io.get("toff", 0)
    x2T = io["x2T"]; attnT = io["attnT"]; wo = io["wo"]
    wqgT = io["wqgT"]; wquT = io["wquT"]; wqdT = io["wqdT"]
    sg_c = io["sg_c"]; su_c = io["su_c"]; sd_c = io["sd_c"]
    outT = io["outT"]; x3h = io["x3h"]; ln3h = io["ln3h"]; prodh = io["prodh"]

    ones_bf = P.sbuf(pfx + "ones_bf", [128, 1], BF16)
    ones_row = P.sbuf(pfx + "ones_row", [1, 128], F32)
    cst = P.sbuf(pfx + "cst", [128, 3], F32)
    nc.gpsimd.memset(ones_bf[:, :], 1.0)
    nc.gpsimd.memset(ones_row[:, :], 1.0)
    nc.gpsimd.memset(cst[:, 0:1], EPS)
    P.gp.inc(nc.gpsimd.memset(cst[:, 1:2], -0.5))

    pool = P.psum(pfx + "pool", [128, 4096], F32)
    ln_bc = pool[:, 1024:1536]
    ps_o = pool[:, 1536:2048]
    ps_u = pool[:, 1024:1536]

    x_tb = P.sbuf(pfx + "x_tb", [128, FT * 512], F32)
    ln3_sb = P.sbuf(pfx + "ln3_sb", [128, FT * T], BF16)
    scl = P.sbuf(pfx + "scl", [128, 2 * NH + FT], F32)
    sg_s = scl[:, 0:NH]
    su_s = scl[:, NH:2 * NH]
    sd_s = scl[:, 2 * NH:2 * NH + FT]

    def bar():
        P.barrier_engines()

    nc.sync.dma_start(out=sg_s, in_=sg_c[:, :]).then_inc(ld.h, 16); ld.n += 16
    nc.sync.dma_start(out=su_s, in_=su_c[:, :]).then_inc(ld.h, 16); ld.n += 16
    nc.sync.dma_start(out=sd_s, in_=sd_c[:, :]).then_inc(ld.h, 16); ld.n += 16
    bar()

    xv = hbm_fm_view(x2T)
    av = hbm_fm_view(attnT)
    x3v = hbm_fm_view(x3h)
    lnv = hbm_fm_view(ln3h)

    P.mark("g1A")
    aevac = {}
    bact = {}
    pmul = {}
    with P.scope():
        big = P.sbuf(pfx + "big", [128, 3 * FT * 512], BF16)
        attn_tb = big[:, 0:FT * 512]
        lnscr = LnScratch(P, 512, name=pfx + "lns", xb=big[:, FT * 512:2 * FT * 512],
                          sq=big[:, 2 * FT * 512:3 * FT * 512],
                          st_ps=pool[0:1, 0:1024], bc_ps=ln_bc)
        wsA = P.sbuf(pfx + "wsA", [128, 2 * FT * 128], BF16)
        wosem = [Sem(nc, pfx + f"wosem{i}") for i in range(2)]
        wouse = {}
        woload = {}

        def load_wo(fi):
            sl = fi % 2
            m = fi % FT
            nc.gpsimd.wait_ge(lg.h, lg.n)
            if fi >= 2:
                nc.gpsimd.wait_ge(mm.h, wouse[fi - 2])
            d = nc.gpsimd.dma_start(
                out=wsA[:, sl * 2048:(sl + 1) * 2048].rearrange("p (f c) -> p f c", f=FT),
                in_=hbm_fm_view(wo)[:, :, m * 128:(m + 1) * 128])
            wosem[sl].inc(d, 16)
            woload[fi] = (sl, wosem[sl].n)

        load_wo(0)
        load_wo(1)
        for tb in range(NTA):
            tsl = slice(tb * TBA, (tb + 1) * TBA)
            nc.sync.wait_ge(st.h, st.n)
            nc.sync.wait_ge(mm.h, mm.n)
            nc.sync.wait_ge(dve.h, dve.n)
            nc.sync.wait_ge(act.h, act.n)
            nc.sync.dma_start(out=x_tb[:, :].rearrange("p (f t) -> p f t", f=FT),
                              in_=offap(xv[:, :, tsl], toff)).then_inc(ld.h, 16); ld.n += 16
            nc.sync.dma_start(out=attn_tb.rearrange("p (f t) -> p f t", f=FT),
                              in_=offap(av[:, :, tsl], toff)).then_inc(ld.h, 16); ld.n += 16
            for m in range(FT):
                fi = tb * FT + m
                sl, wt = woload[fi]
                bank = fi % 2
                ps_b = pool[:, (2 + bank) * 512:(3 + bank) * 512]
                for e in (nc.tensor,):
                    e.wait_ge(wosem[sl].h, wt); e.wait_ge(ld.h, ld.n)
                    e.wait_ge(dve.h, aevac.get(fi - 2, 0))
                for ft in range(FT):
                    ins = nc.tensor.matmul(ps_b, wsA[:, sl * 2048 + ft * 128: sl * 2048 + (ft + 1) * 128],
                                           attn_tb[:, ft * 512:(ft + 1) * 512],
                                           start=(ft == 0), stop=(ft == FT - 1))
                mm.inc(ins)
                wouse[fi] = mm.n
                if fi + 2 < NTA * FT:
                    load_wo(fi + 2)
                nc.vector.wait_ge(mm.h, mm.n)
                dve.inc(v.tensor_add(x_tb[:, m * 512:(m + 1) * 512],
                                     ps_b, x_tb[:, m * 512:(m + 1) * 512]))
                aevac[fi] = dve.n
            nc.sync.wait_ge(dve.h, dve.n)
            nc.sync.dma_start(out=x3v[:, :, tsl],
                              in_=x_tb[:, :].rearrange("p (f t) -> p f t", f=FT)).then_inc(st.h, 16)
            st.n += 16
            emit_ln(P, lnscr,
                    lambda ft: x_tb[:, ft * 512:(ft + 1) * 512],
                    lambda ft: ln3_sb[:, ft * T + tb * TBA: ft * T + (tb + 1) * TBA],
                    512, (ones_bf, ones_row, cst))
            nc.sync.wait_ge(dve.h, dve.n)
        P.barrier_engines(extra=wosem)

    P.mark("g1B")
    with P.scope():
        wgb = P.sbuf(pfx + "wgb", [128, 2 * FT * 128], BF16)
        wub = P.sbuf(pfx + "wub", [128, 2 * FT * 128], BF16)
        prw = P.sbuf(pfx + "prw", [128, 2 * T], BF16)
        pra = P.sbuf(pfx + "pra", [128, 2 * 512], BF16)
        wgsem = [Sem(nc, pfx + f"wgsem{i}") for i in range(2)]
        prsem = [Sem(nc, pfx + f"prsem{i}") for i in range(2)]
        guse = {}
        gldd = {}

        def load_gu(hm):
            sl = hm % 2
            nc.gpsimd.wait_ge(lg.h, lg.n)
            if hm >= 2:
                nc.gpsimd.wait_ge(mm.h, guse[hm - 2])
                nc.gpsimd.wait_ge(dve.h, gsub[hm - 2])
            d = nc.gpsimd.dma_start(
                out=wgb[:, sl * 2048:(sl + 1) * 2048].rearrange("p (f c) -> p f c", f=FT),
                in_=wqgT.rearrange("(k p) c -> p k c", p=128)[:, :, hm * 128:(hm + 1) * 128])
            wgsem[sl].inc(d, 16)
            nc.gpsimd.wait_ge(lg.h, lg.n)
            d = nc.gpsimd.dma_start(
                out=wub[:, sl * 2048:(sl + 1) * 2048].rearrange("p (f c) -> p f c", f=FT),
                in_=wquT.rearrange("(k p) c -> p k c", p=128)[:, :, hm * 128:(hm + 1) * 128])
            wgsem[sl].inc(d, 16)
            gldd[hm] = (sl, wgsem[sl].n)

        gsub = {}
        load_gu(0)
        load_gu(1)
        for hm in range(NH):
            sl, wt = gldd[hm]
            nc.vector.wait_ge(wgsem[sl].h, wt)
            nc.vector.wait_ge(mm.h, guse.get(hm - 2, 0))
            dve.inc(v.tensor_scalar_add(wgb[:, sl * 2048:(sl + 1) * 2048],
                                        wgb[:, sl * 2048:(sl + 1) * 2048], -128.0))
            dve.inc(v.tensor_scalar_add(wub[:, sl * 2048:(sl + 1) * 2048],
                                        wub[:, sl * 2048:(sl + 1) * 2048], -128.0))
            gsub[hm] = dve.n
            nc.tensor.wait_ge(dve.h, gsub[hm])
            for tb in range(NTA):
                bank = tb % 2
                ps_g = pool[:, bank * 512:(bank + 1) * 512]
                ps_u = pool[:, (2 + bank) * 512:(3 + bank) * 512]
                nc.tensor.wait_ge(act.h, bact.get(('g', bank), 0))
                for ft in range(FT):
                    ins = nc.tensor.matmul(ps_g, wgb[:, sl * 2048 + ft * 128: sl * 2048 + (ft + 1) * 128],
                                           ln3_sb[:, ft * T + tb * TBA: ft * T + (tb + 1) * TBA],
                                           start=(ft == 0), stop=(ft == FT - 1))
                mm.inc(ins)
                gmm = mm.n
                nc.tensor.wait_ge(act.h, bact.get(('u', bank), 0))
                for ft in range(FT):
                    ins = nc.tensor.matmul(ps_u, wub[:, sl * 2048 + ft * 128: sl * 2048 + (ft + 1) * 128],
                                           ln3_sb[:, ft * T + tb * TBA: ft * T + (tb + 1) * TBA],
                                           start=(ft == 0), stop=(ft == FT - 1))
                mm.inc(ins)
                nc.scalar.wait_ge(mm.h, gmm)
                nc.scalar.wait_ge(dve.h, pmul.get(('pra', bank), 0))
                act.inc(nc.scalar.activation(pra[:, bank * 512:(bank + 1) * 512], ps_g, AF.Silu,
                                             scale=sg_s[:, hm:hm + 1]))
                bact[('g', bank)] = act.n
                nc.scalar.wait_ge(mm.h, mm.n)
                if tb == 0:
                    nc.scalar.wait_ge(prsem[hm % 2].h, prsem[hm % 2].n)
                act.inc(nc.scalar.activation(prw[:, (hm % 2) * T + tb * TBA:(hm % 2) * T + (tb + 1) * TBA],
                                             ps_u, AF.Copy, scale=su_s[:, hm:hm + 1]))
                bact[('u', bank)] = act.n
                nc.vector.wait_ge(act.h, act.n)
                dve.inc(v.tensor_tensor(prw[:, (hm % 2) * T + tb * TBA:(hm % 2) * T + (tb + 1) * TBA],
                                        pra[:, bank * 512:(bank + 1) * 512],
                                        prw[:, (hm % 2) * T + tb * TBA:(hm % 2) * T + (tb + 1) * TBA],
                                        ALU.mult))
                pmul[('pra', bank)] = dve.n
            guse[hm] = mm.n
            if hm + 2 < NH:
                load_gu(hm + 2)
            nc.sync.wait_ge(dve.h, dve.n)
            d = nc.sync.dma_start(out=prodh[hm * 128:(hm + 1) * 128, :],
                                  in_=prw[:, (hm % 2) * T:(hm % 2 + 1) * T])
            prsem[hm % 2].inc(d, 16)
        P.barrier_engines(extra=wgsem + prsem)

    P.mark("g1C")
    with P.scope():
        CDEP = 4
        wdb = P.sbuf(pfx + "wdb", [128, CDEP * 512], BF16)
        prb = P.sbuf(pfx + "prb", [128, CDEP * 1024], BF16)
        wdsem = [Sem(nc, pfx + f"wdsem{i}") for i in range(CDEP)]
        pbsem = [Sem(nc, pfx + f"pbsem{i}") for i in range(CDEP)]
        cuse = {}
        cload = {}
        cidx = [(mq, tb2, hk) for mq in range(4) for tb2 in range(2) for hk in range(NH)]

        def load_c(ci):
            mq, tb2, hk = cidx[ci]
            sl = ci % CDEP
            nc.gpsimd.wait_ge(lg.h, lg.n)
            if ci >= CDEP:
                nc.gpsimd.wait_ge(mm.h, cuse[ci - CDEP])
                nc.gpsimd.wait_ge(dve.h, csub.get(ci - CDEP, 0))
            d = nc.gpsimd.dma_start(out=wdb[:, sl * 512:(sl + 1) * 512],
                                    in_=wqdT[hk * 128:(hk + 1) * 128, mq * 512:(mq + 1) * 512])
            wdsem[sl].inc(d, 16)
            nc.scalar.wait_ge(mm.h, cuse.get(ci - CDEP, 0))
            d = nc.scalar.dma_start(out=prb[:, sl * 1024:(sl + 1) * 1024],
                                    in_=prodh[hk * 128:(hk + 1) * 128, tb2 * 1024:(tb2 + 1) * 1024])
            pbsem[sl].inc(d, 16)
            cload[ci] = (sl, wdsem[sl].n, pbsem[sl].n)

        csub = {}
        for _ci0 in range(CDEP):
            load_c(_ci0)
        ci = 0
        for mq in range(4):
            for tb2 in range(2):
                t0 = tb2 * 1024
                nc.sync.wait_ge(dve.h, dve.n)
                nc.sync.wait_ge(st.h, st.n)
                nc.sync.dma_start(out=x_tb[:, 0:4 * 1024].rearrange("p (f t) -> p f t", f=4),
                                  in_=x3v[:, mq * 4:(mq + 1) * 4, t0:t0 + 1024]).then_inc(ld.h, 16)
                ld.n += 16
                for hk in range(NH):
                    sl, wt, pt = cload[ci]
                    nc.vector.wait_ge(wdsem[sl].h, wt)
                    nc.vector.wait_ge(mm.h, cuse.get(ci - CDEP, 0))
                    dve.inc(v.tensor_scalar_add(wdb[:, sl * 512:(sl + 1) * 512],
                                                wdb[:, sl * 512:(sl + 1) * 512], -128.0))
                    csub[ci] = dve.n
                    for e in (nc.tensor,):
                        e.wait_ge(dve.h, csub[ci])
                        e.wait_ge(pbsem[sl].h, pt)
                    for mi in range(4):
                        for half in range(2):
                            ins = nc.tensor.matmul(
                                pool[:, (mi * 2 + half) * 512:(mi * 2 + half + 1) * 512],
                                wdb[:, sl * 512 + mi * 128: sl * 512 + (mi + 1) * 128],
                                prb[:, sl * 1024 + half * 512: sl * 1024 + (half + 1) * 512],
                                start=(hk == 0), stop=(hk == NH - 1))
                    mm.inc(ins)
                    cuse[ci] = mm.n
                    if ci + CDEP < len(cidx):
                        load_c(ci + CDEP)
                    ci += 1
                nc.vector.wait_ge(mm.h, mm.n)
                nc.vector.wait_ge(ld.h, ld.n)
                for mi in range(4):
                    m = mq * 4 + mi
                    for half in range(2):
                        ins = v.scalar_tensor_tensor(
                            x_tb[:, mi * 1024 + half * 512: mi * 1024 + (half + 1) * 512],
                            pool[:, (mi * 2 + half) * 512:(mi * 2 + half + 1) * 512],
                            sd_s[:, m:m + 1],
                            x_tb[:, mi * 1024 + half * 512: mi * 1024 + (half + 1) * 512],
                            ALU.mult, ALU.add)
                dve.inc(ins)
                nc.sync.wait_ge(dve.h, dve.n)
                nc.sync.dma_start(out=hbm_fm_view(outT)[:, mq * 4:(mq + 1) * 4, t0:t0 + 1024],
                                  in_=x_tb[:, 0:4 * 1024].rearrange("p (f t) -> p f t", f=4)).then_inc(st.h, 16)
                st.n += 16
        P.barrier_engines(extra=wdsem + pbsem)
    nc.sync.wait_ge(st.h, st.n)


def build_ffn1():
    P = Prog()
    nc = P.nc
    NH = 44
    io = {
        "x2T": nc.dram_tensor("x2T", [D, TOK], F32, kind="ExternalInput").ap(),
        "attnT": nc.dram_tensor("attnT", [D, TOK], BF16, kind="ExternalInput").ap(),
        "wo": nc.dram_tensor("wo", [D, D], F32, kind="ExternalInput").ap(),
        "wqgT": nc.dram_tensor("wqgT", [D, 5632], I32, kind="ExternalInput").ap(),
        "wquT": nc.dram_tensor("wquT", [D, 5632], I32, kind="ExternalInput").ap(),
        "wqdT": nc.dram_tensor("wqdT", [5632, D], I32, kind="ExternalInput").ap(),
        "sg_c": nc.dram_tensor("sg_c", [128, NH], F32, kind="ExternalInput").ap(),
        "su_c": nc.dram_tensor("su_c", [128, NH], F32, kind="ExternalInput").ap(),
        "sd_c": nc.dram_tensor("sd_c", [128, FT], F32, kind="ExternalInput").ap(),
        "outT": nc.dram_tensor("outT", [D, TOK], F32, kind="ExternalOutput").ap(),
        "x3h": nc.dram_tensor("x3h", [D, TOK], F32).ap(),
        "ln3h": nc.dram_tensor("ln3h", [D, TOK], BF16).ap(),
        "prodh": nc.dram_tensor("prodh", [5632, TOK], BF16).ap(),
    }
    with P.scope():
        emit_ffn1(P, io)
    return P


def ffn1_in_maps(x2_fm, attn_fm, w):
    wqgT = np.ascontiguousarray(w["wq_gate"].T)
    wquT = np.ascontiguousarray(w["wq_up"].T)
    wqdT = np.ascontiguousarray(w["wq_down"].T)
    shared_m = {
        "wo": w["wo"].astype(np.float32),
        "wqgT": wqgT, "wquT": wquT, "wqdT": wqdT,
        "sg_c": pack_bias(w["s_gate"]), "su_c": pack_bias(w["s_up"]),
        "sd_c": pack_bias(w["s_down"]),
    }
    in_maps = []
    for core in range(8):
        b, hf = core // 2, core % 2
        tsl = slice(hf * TOK, (hf + 1) * TOK)
        m = dict(shared_m)
        m["x2T"] = np.ascontiguousarray(x2_fm[b][:, tsl])
        m["attnT"] = np.ascontiguousarray(attn_fm[b][:, tsl])
        in_maps.append(m)
    return in_maps


def run_ffn1(x2_fm, attn_fm, w):
    P = get_prog("ffn1", build_ffn1)
    in_maps = ffn1_in_maps(x2_fm, attn_fm, w)
    res = run_bass_kernel_spmd(P.nc, in_maps, core_ids=list(range(8)))
    out = np.empty((4, 2048, 4096), dtype=np.float32)
    for core in range(8):
        b, hf = core // 2, core % 2
        out[b][:, hf * TOK:(hf + 1) * TOK] = res.results[core]["outT"]
    return out



def emit_rendezvous(P, rv, idx):
    nc = P.nc
    st = P.st
    nc.sync.wait_ge(st.h, st.n)
    nc.sync.dma_start(out=rv["myslot"][idx], in_=rv["nsb"][0:1, idx * 16:(idx + 1) * 16]
                      ).then_inc(st.h, 16)
    st.n += 16
    nc.sync.wait_ge(st.h, st.n)
    fv, cnt, tgt, psem = rv["fv"], rv["cnt"], rv["tgt"], rv["psem"]
    nc.sync.reg_load(tgt, rv["nsb"][0:1, idx * 16:idx * 16 + 1])
    nc.sync.reg_mov(fv, 0)

    def cond():
        a = nc.sync.snap(fv) != nc.sync.snap(tgt)
        b = nc.sync.snap(cnt) < 2_000_000_000
        return a & b

    with nc.sync.While(cond):
        nc.sync.reg_add(cnt, cnt, 16)
        nc.sync.dma_start(out=rv["fsb"][0:1, :], in_=rv["otherslot"][idx]).then_inc(psem.h, 16)
        nc.sync.wait_ge(psem.h, nc.sync.snap(cnt))
        nc.sync.reg_load(fv, rv["fsb"][0:1, 0:1])
    P.gp.inc(nc.sync.nop())
    for e in (nc.tensor, nc.scalar, nc.vector, nc.gpsimd, nc.sync):
        e.wait_ge(P.gp.h, P.gp.n)


def build_fused():
    P = Prog()
    nc = P.nc
    NE = 8

    def din(name, shape, dtype=F32):
        return nc.dram_tensor(name, shape, dtype, kind="ExternalInput").ap()

    xb = din("xb", [D, S])
    attn0_sh = nc.dram_tensor("attn0_sh", [D, S], BF16, addr_space="Shared").ap()
    attn1_sh = nc.dram_tensor("attn1_sh", [D, S], BF16, addr_space="Shared").ap()
    x2_sh = nc.dram_tensor("x2_sh", [D, S], F32, addr_space="Shared").ap()
    attn0_loc = nc.dram_tensor("attn0_loc", [1024, S], BF16).ap()
    attn1_loc = nc.dram_tensor("attn1_loc", [1024, S], BF16).ap()
    x_loc = nc.dram_tensor("x_loc", [D, TOK], F32).ap()
    attnT_loc = nc.dram_tensor("attnT_loc", [D, TOK], BF16).ap()
    attn1T_loc = nc.dram_tensor("attn1T_loc", [D, TOK], BF16).ap()
    x2_loc = nc.dram_tensor("x2_loc", [D, TOK], F32).ap()

    a0 = {
        "xT": xb,
        "wqT": din("a0_wqT", [D, 1024]), "wkT": din("a0_wkT", [D, 1024]),
        "wvT": din("a0_wvT", [D, 1024]),
        "maskm": din("a0_maskm", [128, 1024], BF16),
        "qh": nc.dram_tensor("a0_qh", [1024, S], BF16).ap(),
        "kh": nc.dram_tensor("a0_kh", [1024, S], BF16).ap(),
        "vh": nc.dram_tensor("a0_vh", [S, 1024], BF16).ap(),
        "attnout": attn0_loc,
    }
    f0 = {
        "xT": x_loc,
        "attnT": attnT_loc,
        "x2T": x2_loc,
        "woT": din("f0_woT", [D, D]),
        "down_w": din("f0_down_w", [D, 1024]), "down_b": din("f0_down_b", [128, 8]),
        "up_w": din("f0_up_w", [1024, D]), "upcore_b": din("f0_upcore_b", [128, 32]),
        "router_w": din("f0_router_w", [D, 8]), "router_b": din("f0_router_b", [1, 8]),
        "cost_t": din("f0_cost_t", [1, 8]),
        "shared_w1": din("f0_shared_w1", [1024, 1024]),
        "shared_w2": din("f0_shared_w2", [1024, 1024]),
        "shared_b": din("f0_shared_b", [128, 16]),
        "core_w": din("f0_core_w", [D, D]),
        "ew1": [din(f"f0_e{e}_w1", [1024, HID[e]]) for e in range(NE)],
        "ew2": [din(f"f0_e{e}_w2", [HID[e], 1024]) for e in range(NE)],
        "eb1": din("f0_eb1", [128, 160]), "eb2": din("f0_eb2", [128, 64]),
        "idn_h": din("f0_idn_h", [128, 128]), "sel_h": din("f0_sel_h", [8, 1024], BF16),
        "x1h": nc.dram_tensor("f0_x1h", [D, TOK], F32).ap(),
        "ln2h": nc.dram_tensor("f0_ln2h", [D, TOK], BF16).ap(),
        "gph": [nc.dram_tensor(f"f0_gph{e}", [(HID + [1024])[e], TOK], BF16).ap() for e in range(9)],
    }
    a1 = {
        "xT": x2_sh,
        "wqT": din("a1_wqT", [D, 1024]), "wkT": din("a1_wkT", [D, 256]),
        "wvT": din("a1_wvT", [D, 256]),
        "maskm": din("a1_maskm", [128, 2688], BF16),
        "qh": nc.dram_tensor("a1_qh", [1024, S], BF16).ap(),
        "kh": nc.dram_tensor("a1_kh", [256, S], BF16).ap(),
        "vh": nc.dram_tensor("a1_vh", [S, 260], BF16).ap(),
        "attnout": attn1_loc,
    }
    g = {
        "x2T": x2_loc,
        "attnT": attn1T_loc,
        "wo": din("g_wo", [D, D]),
        "wqgT": din("g_wqgT", [D, 5632], I32), "wquT": din("g_wquT", [D, 5632], I32),
        "wqdT": din("g_wqdT", [5632, D], I32),
        "sg_c": din("g_sg_c", [128, 44]), "su_c": din("g_su_c", [128, 44]),
        "sd_c": din("g_sd_c", [128, FT]),
        "outT": nc.dram_tensor("outT", [D, TOK], F32, kind="ExternalOutput").ap(),
        "x3h": nc.dram_tensor("g_x3h", [D, TOK], F32).ap(),
        "ln3h": nc.dram_tensor("g_ln3h", [D, TOK], BF16).ap(),
        "prodh": nc.dram_tensor("g_prodh", [5632, TOK], BF16).ap(),
    }
    nonces = din("nonces", [1, 64], I32)
    fls = [nc.dram_tensor(f"fl{i}", [2, 16], I32, addr_space="Shared").ap() for i in range(4)]

    pid = nc.partition_id([mybir.EngineType.SP])
    parity = pid % 2
    toff = nc.sync.snap(parity * TOK)
    roff = nc.sync.snap(parity * (1024 * S))
    flmy = nc.sync.snap(parity * 16)
    flot = nc.sync.snap((1 - parity) * 16)

    rv = {}
    rv["nsb"] = P.sbuf("rv_nsb", [1, 64], I32)
    rv["fsb"] = P.sbuf("rv_fsb", [1, 16], I32)
    rv["fv"] = nc.alloc_register(mybir.EngineType.SP, "rv_fv")
    rv["cnt"] = nc.alloc_register(mybir.EngineType.SP, "rv_cnt")
    rv["tgt"] = nc.alloc_register(mybir.EngineType.SP, "rv_tgt")
    rv["psem"] = Sem(nc, "rv_psem")
    rv["myslot"] = [dataclasses.replace(f[0:1, 0:16], offset=flmy) for f in fls]
    rv["otherslot"] = [dataclasses.replace(f[0:1, 0:16], offset=flot) for f in fls]
    nc.sync.dma_start(out=rv["nsb"][:, :], in_=nonces[:, :]).then_inc(P.ld.h, 16)
    P.ld.n += 16
    nc.sync.reg_mov(rv["cnt"], 0)
    nc.sync.wait_ge(P.ld.h, P.ld.n)

    def bulk(dst_ap, src_ap):
        nc.sync.dma_start(out=dst_ap, in_=src_ap).then_inc(P.st.h, 16)
        P.st.n += 16
        nc.sync.wait_ge(P.st.h, P.st.n)

    bulk(x_loc[:, :], offap(xb[:, 0:TOK], toff))

    with nc.named_scope("attn0"):
        with P.scope():
            emit_attn(P, 0, a0, pfx="a0_")
    with nc.named_scope("rdv0"):
        bulk(offap(attn0_sh[0:1024, :], roff), attn0_loc[:, :])
        emit_rendezvous(P, rv, 0)
        bulk(attnT_loc[:, :], offap(attn0_sh[:, 0:TOK], toff))
    with nc.named_scope("ffn0"):
        with P.scope():
            emit_ffn0(P, f0, pfx="f0_")
    with nc.named_scope("rdv1"):
        bulk(offap(x2_sh[:, 0:TOK], toff), x2_loc[:, :])
        emit_rendezvous(P, rv, 1)
    with nc.named_scope("attn1"):
        with P.scope():
            emit_attn(P, 1, a1, pfx="a1_")
    with nc.named_scope("rdv2"):
        bulk(offap(attn1_sh[0:1024, :], roff), attn1_loc[:, :])
        emit_rendezvous(P, rv, 2)
        bulk(attn1T_loc[:, :], offap(attn1_sh[:, 0:TOK], toff))
    with nc.named_scope("ffn1"):
        with P.scope():
            emit_ffn1(P, g, pfx="g_")
    nc.sync.wait_ge(P.st.h, P.st.n)
    return P


def fused_in_maps(ins):
    x = ins["L0_attn_hidden_states"]
    xT_all = np.ascontiguousarray(x.transpose(0, 2, 1)).astype(np.float32)
    w_qkv = ins["L0_attn_w_qkv"]
    wq0, wk0, wv0 = w_qkv[0:2048], w_qkv[2048:4096], w_qkv[4096:6144]
    mask0 = make_mask_master(0)
    mask1 = make_mask_master(1)
    shared_m = {
        "a0_maskm": mask0, "a1_maskm": mask1,
        "f0_woT": np.ascontiguousarray(ins["L0_attn_w_o"].T.astype(np.float32)),
        "f0_down_w": ins["L0_ffn_down_w"], "f0_down_b": pack_bias(ins["L0_ffn_down_b"]),
        "f0_up_w": ins["L0_ffn_up_w"],
        "f0_upcore_b": np.concatenate([pack_bias(ins["L0_ffn_up_b"]),
                                       pack_bias(ins["L0_ffn_core_b"])], axis=1),
        "f0_router_w": ins["L0_ffn_router_w"],
        "f0_router_b": ins["L0_ffn_router_b"].reshape(1, 8).astype(np.float32),
        "f0_cost_t": _l0_cost_consts(),
        "f0_shared_w1": ins["L0_ffn_shared_w1"], "f0_shared_w2": ins["L0_ffn_shared_w2"],
        "f0_shared_b": np.concatenate([pack_bias(ins["L0_ffn_shared_b1"]),
                                       pack_bias(ins["L0_ffn_shared_b2"])], axis=1),
        "f0_core_w": ins["L0_ffn_core_w"],
        "f0_eb1": np.concatenate([pack_bias(ins[f"L0_ffn_e{e}_b1"]) for e in range(8)], axis=1),
        "f0_eb2": pack_b2([ins[f"L0_ffn_e{e}_b2"] for e in range(8)]),
        "f0_idn_h": np.eye(128, dtype=np.float32),
        "f0_sel_h": _sel_mat().astype(BF),
        "g_wo": ins["L1_attn_wo"].astype(np.float32),
        "g_wqgT": np.ascontiguousarray(ins["L1_ffn_wq_gate"].T),
        "g_wquT": np.ascontiguousarray(ins["L1_ffn_wq_up"].T),
        "g_wqdT": np.ascontiguousarray(ins["L1_ffn_wq_down"].T),
        "g_sg_c": pack_bias(ins["L1_ffn_s_gate"]), "g_su_c": pack_bias(ins["L1_ffn_s_up"]),
        "g_sd_c": pack_bias(ins["L1_ffn_s_down"]),
        "nonces": np.random.randint(1, 2 ** 30, size=(1, 64)).astype(np.int32),
    }
    for e in range(8):
        shared_m[f"f0_e{e}_w1"] = ins[f"L0_ffn_e{e}_w1"]
        shared_m[f"f0_e{e}_w2"] = ins[f"L0_ffn_e{e}_w2"]
    in_maps = []
    for core in range(8):
        b, hg = core // 2, core % 2
        m = dict(shared_m)
        m["xb"] = xT_all[b]
        r0, r1 = hg * 1024, hg * 1024 + 1024
        m["a0_wqT"] = np.ascontiguousarray(wq0[r0:r1].T)
        m["a0_wkT"] = np.ascontiguousarray(wk0[r0:r1].T)
        m["a0_wvT"] = np.ascontiguousarray(wv0[r0:r1].T)
        m["a1_wqT"] = np.ascontiguousarray(ins["L1_attn_wq"][:, r0:r1])
        kc0, kc1 = hg * 256, hg * 256 + 256
        m["a1_wkT"] = np.ascontiguousarray(ins["L1_attn_wk"][:, kc0:kc1])
        m["a1_wvT"] = np.ascontiguousarray(ins["L1_attn_wv"][:, kc0:kc1])
        in_maps.append(m)
    return in_maps


def kernel(**inputs):
    ins = {k: np.asarray(v) for k, v in inputs.items()}
    P = get_prog("fused", build_fused)
    in_maps = fused_in_maps(ins)
    res = run_bass_kernel_spmd(P.nc, in_maps, core_ids=list(range(8)))
    out = np.empty((4, 2048, 4096), dtype=np.float32)
    for core in range(8):
        b, hf = core // 2, core % 2
        out[b][:, hf * TOK:(hf + 1) * TOK] = res.results[core]["outT"]
    return np.ascontiguousarray(out.transpose(0, 2, 1))



def make_mask_master(layer):
    if layer == 0:
        p = np.arange(128)[:, None]
        c = np.arange(1024)[None, :]
        return (c >= p + 512).astype(BF)
    else:
        p = np.arange(128)[:, None]
        c = np.arange(2688)[None, :]
        d = c - p - 640
        return (~((d >= 1) & (d <= 1024))).astype(BF)


_PROG_CACHE = {}


def get_prog(key, builder):
    if key not in _PROG_CACHE:
        _PROG_CACHE[key] = builder()
    return _PROG_CACHE[key]


def run_l0_attn(x, w_qkv):
    P = get_prog("attn0", lambda: build_attn(0))
    mask = make_mask_master(0)
    xT = np.ascontiguousarray(x.transpose(0, 2, 1))
    wq = w_qkv[0:2048]
    wk = w_qkv[2048:4096]
    wv = w_qkv[4096:6144]
    in_maps = []
    for core in range(8):
        b, hg = core // 2, core % 2
        r0, r1 = hg * 1024, hg * 1024 + 1024
        in_maps.append({
            "xT": xT[b],
            "wqT": np.ascontiguousarray(wq[r0:r1].T),
            "wkT": np.ascontiguousarray(wk[r0:r1].T),
            "wvT": np.ascontiguousarray(wv[r0:r1].T),
            "maskm": mask,
        })
    res = run_bass_kernel_spmd(P.nc, in_maps, core_ids=list(range(8)))
    out = np.empty((4, 2048, 4096), dtype=BF)
    for core in range(8):
        b, hg = core // 2, core % 2
        out[b, hg * 1024:(hg + 1) * 1024] = res.results[core]["attnout"]
    return out


def run_l1_attn(x_fm, wq, wk, wv):
    P = get_prog("attn1", lambda: build_attn(1))
    mask = make_mask_master(1)
    in_maps = []
    for core in range(8):
        b, hg = core // 2, core % 2
        qc0, qc1 = hg * 1024, hg * 1024 + 1024
        kc0, kc1 = hg * 256, hg * 256 + 256
        in_maps.append({
            "xT": x_fm[b],
            "wqT": np.ascontiguousarray(wq[:, qc0:qc1]),
            "wkT": np.ascontiguousarray(wk[:, kc0:kc1]),
            "wvT": np.ascontiguousarray(wv[:, kc0:kc1]),
            "maskm": mask,
        })
    res = run_bass_kernel_spmd(P.nc, in_maps, core_ids=list(range(8)))
    out = np.empty((4, 2048, 4096), dtype=BF)
    for core in range(8):
        b, hg = core // 2, core % 2
        out[b, hg * 1024:(hg + 1) * 1024] = res.results[core]["attnout"]
    return out



def _sel_mat():
    s = np.zeros((8, 1024), np.float32)
    for e in range(8):
        s[e, e * 128:(e + 1) * 128] = 1.0
    return s


def _l0_cost_consts():
    return np.array([[COST_LAMBDA * 2048.0 * h for h in HID]], dtype=np.float32)


def pack_b2(b2_list):
    out = np.zeros((128, 64), np.float32)
    for e, b2 in enumerate(b2_list):
        out[:, e::8] = b2.reshape(8, 128).T
    return out


def ffn0_in_maps(xT_all, attn_fm, w):
    woT = np.ascontiguousarray(w["w_o"].T.astype(np.float32))
    shared_m = {
        "woT": woT,
        "down_w": w["down_w"], "down_b": pack_bias(w["down_b"]),
        "up_w": w["up_w"],
        "upcore_b": np.concatenate([pack_bias(w["up_b"]), pack_bias(w["core_b"])], axis=1),
        "router_w": w["router_w"], "router_b": w["router_b"].reshape(1, 8),
        "cost_t": _l0_cost_consts(),
        "shared_w1": w["shared_w1"], "shared_w2": w["shared_w2"],
        "shared_b": np.concatenate([pack_bias(w["shared_b1"]), pack_bias(w["shared_b2"])], axis=1),
        "core_w": w["core_w"],
        "eb1": np.concatenate([pack_bias(w[f"e{e}_b1"]) for e in range(8)], axis=1),
        "eb2": pack_b2([w[f"e{e}_b2"] for e in range(8)]),
        "idn_h": np.eye(128, dtype=np.float32),
        "sel_h": _sel_mat().astype(BF),
    }
    for e in range(8):
        shared_m[f"e{e}_w1"] = w[f"e{e}_w1"]
        shared_m[f"e{e}_w2"] = w[f"e{e}_w2"]
    in_maps = []
    for core in range(8):
        b, hf = core // 2, core % 2
        tsl = slice(hf * TOK, (hf + 1) * TOK)
        m = dict(shared_m)
        m["xT"] = np.ascontiguousarray(xT_all[b][:, tsl])
        m["attnT"] = np.ascontiguousarray(attn_fm[b][:, tsl])
        in_maps.append(m)
    return in_maps


def run_ffn0(xT_all, attn_fm, w):
    P = get_prog("ffn0", build_ffn0)
    in_maps = ffn0_in_maps(xT_all, attn_fm, w)
    res = run_bass_kernel_spmd(P.nc, in_maps, core_ids=list(range(8)))
    out = np.empty((4, 2048, 4096), dtype=np.float32)
    for core in range(8):
        b, hf = core // 2, core % 2
        out[b][:, hf * TOK:(hf + 1) * TOK] = res.results[core]["x2T"]
    return out


def kernel_unfused(**inputs):
    ins = {k: np.asarray(v) for k, v in inputs.items()}
    x = ins["L0_attn_hidden_states"]
    xT_all = np.ascontiguousarray(x.transpose(0, 2, 1))

    attn0 = run_l0_attn(x, ins["L0_attn_w_qkv"])

    w0 = {"w_o": ins["L0_attn_w_o"]}
    for k in ("down_w", "down_b", "up_w", "up_b", "router_w", "router_b",
              "shared_w1", "shared_b1", "shared_w2", "shared_b2",
              "core_w", "core_b"):
        w0[k] = ins["L0_ffn_" + k]
    for e in range(8):
        for p in ("w1", "b1", "w2", "b2"):
            w0[f"e{e}_{p}"] = ins[f"L0_ffn_e{e}_{p}"]
    x2_fm = run_ffn0(xT_all, attn0, w0)

    attn1 = run_l1_attn(x2_fm, ins["L1_attn_wq"], ins["L1_attn_wk"], ins["L1_attn_wv"])

    w1 = {"wo": ins["L1_attn_wo"]}
    for k in ("wq_gate", "s_gate", "wq_up", "s_up", "wq_down", "s_down"):
        w1[k] = ins["L1_ffn_" + k]
    out_fm = run_ffn1(x2_fm, attn1, w1)
    return np.ascontiguousarray(out_fm.transpose(0, 2, 1))


# version 12 (final)
# speedup vs baseline: 1.3726x; 1.0315x over previous
import math
import dataclasses
from contextlib import ExitStack, contextmanager
import numpy as np
import ml_dtypes

import concourse.bass as bass
import concourse.mybir as mybir
from concourse.bass_utils import run_bass_kernel_spmd

F32 = mybir.dt.float32
BF16 = mybir.dt.bfloat16
I32 = mybir.dt.int32
AF = mybir.ActivationFunctionType
ALU = mybir.AluOpType

D = 2048
B = 4
S = 4096
TOK = 2048
FT = D // 128

EPS = 1e-5
COST_LAMBDA = 1e-07
HID = [1024, 2048, 3072, 4096, 1024, 2048, 3072, 4096]
GELU = AF.Gelu

BF = ml_dtypes.bfloat16



class Sem:
    def __init__(self, nc, name):
        self.h = nc.alloc_semaphore(name=name)
        self.n = 0

    def inc(self, instr, k=1):
        instr.then_inc(self.h, k)
        self.n += k
        return self.n


class Prog:
    def __init__(self):
        self.nc = bass.Bass("TRN2", target_bir_lowering=False)
        nc = self.nc
        self.mm = Sem(nc, "mm")
        self.act = Sem(nc, "act")
        self.dve = Sem(nc, "dve")
        self.gp = Sem(nc, "gp")
        self.ld = Sem(nc, "ld")
        self.lg = Sem(nc, "lg")
        self.st = Sem(nc, "st")

    def sbuf(self, name, shape, dtype):
        if getattr(self, "_scopes", None):
            return self._scopes[-1].enter_context(self.nc.sbuf_tensor(name, shape, dtype))
        return self.nc.alloc_sbuf_tensor(name, shape, dtype)

    def psum(self, name, shape, dtype=F32):
        if getattr(self, "_scopes", None):
            return self._scopes[-1].enter_context(self.nc.psum_tensor(name, shape, dtype))
        return self.nc.alloc_psum_tensor(name, shape, dtype)

    def mark(self, name):
        try:
            self.nc.enter_named_scope(name, False)
            self.nc.leave_named_scope(name, 0, False)
        except Exception:
            pass

    @contextmanager
    def scope(self):
        if not hasattr(self, "_scopes"):
            self._scopes = []
        es = ExitStack()
        self._scopes.append(es)
        try:
            yield
        finally:
            self._scopes.pop()
            es.close()

    def barrier_engines(self, engines=None, extra=()):
        nc = self.nc
        engines = engines or [nc.tensor, nc.scalar, nc.vector, nc.sync, nc.gpsimd]
        for s in (self.mm, self.act, self.dve, self.gp, self.ld, self.st, *extra):
            for e in engines:
                e.wait_ge(s.h, s.n)


def hbm_fm_view(ap, p=128):
    return ap.rearrange("(f p) t -> p f t", p=p)


def offap(ap, toff):
    if isinstance(toff, int) and toff == 0:
        return ap
    return dataclasses.replace(ap, offset=toff + ap.offset)


def load_weight_fm(P, dst_sb, src_hbm, nM, nK, eng=None):
    nc = P.nc
    eng = eng or nc.gpsimd
    src = src_hbm.rearrange("(k p) mc -> p k mc", p=128)
    eng.wait_ge(P.lg.h, P.lg.n)
    d = eng.dma_start(out=dst_sb[:, :].rearrange("p (k mc) -> p k mc", k=nK), in_=src)
    P.lg.inc(d, 16)


def w_tile(w_sb, m, k, nM):
    c0 = (k * nM + m) * 128
    return w_sb[:, c0:c0 + 128]


def pack_bias(b):
    n = b.shape[0]
    return np.ascontiguousarray(b.reshape(n // 128, 128).T.astype(np.float32))



class LnScratch:
    def __init__(self, P, T, name="lns", xb=None, sq=None, st_ps=None, bc_ps=None):
        self.T = T
        self.xb = xb if xb is not None else P.sbuf(name + "_xb", [128, FT * T], BF16)
        self.sq = sq if sq is not None else P.sbuf(name + "_sq", [128, FT * T], BF16)
        self.st_ps = st_ps if st_ps is not None else P.psum(name + "_stps", [1, 2 * T], F32)
        self.bc_ps = bc_ps if bc_ps is not None else P.psum(name + "_bcps", [128, T], F32)
        self.stats = P.sbuf(name + "_stats", [1, 6 * T], F32)
        self.ab = P.sbuf(name + "_ab", [128, 2 * T], BF16)


def emit_ln(P, scr, src_slices, dst_slices, ntok, consts, sync_end=True):
    nc = P.nc
    ones_bf, ones_row, cst = consts
    mm, act, dve = P.mm, P.act, P.dve
    T = scr.T
    v = nc.vector
    v.wait_ge(dve.h, dve.n)
    nc.scalar.wait_ge(dve.h, dve.n)
    cast = src_slices(0).dtype == F32
    if cast:
        for ft in range(FT):
            ins = v.tensor_copy(scr.xb[:, ft * ntok:(ft + 1) * ntok], src_slices(ft))
        dve.inc(ins)
        for e in (nc.scalar, nc.tensor):
            e.wait_ge(dve.h, dve.n)
        ssl = lambda ft: scr.xb[:, ft * ntok:(ft + 1) * ntok]
    else:
        ssl = src_slices
    for ft in range(FT):
        ins = nc.scalar.activation(scr.sq[:, ft * ntok:(ft + 1) * ntok], ssl(ft), AF.Square)
    act.inc(ins)
    nc.tensor.wait_ge(act.h, act.n)
    nc.tensor.wait_ge(dve.h, dve.n)
    for ft in range(FT):
        nc.tensor.matmul(scr.st_ps[0:1, 0:ntok], ones_bf[:, 0:1], ssl(ft),
                         start=(ft == 0), stop=(ft == FT - 1))
    for ft in range(FT):
        ins = nc.tensor.matmul(scr.st_ps[0:1, T:T + ntok], ones_bf[:, 0:1],
                               scr.sq[:, ft * ntok:(ft + 1) * ntok],
                               start=(ft == 0), stop=(ft == FT - 1))
    mm.inc(ins)
    nc.vector.wait_ge(mm.h, mm.n)
    mean = scr.stats[0:1, 0:ntok]
    aa = scr.stats[0:1, T:T + ntok]
    bb = scr.stats[0:1, 2 * T:2 * T + ntok]
    ex2 = scr.stats[0:1, 3 * T:3 * T + ntok]
    var = scr.stats[0:1, 4 * T:4 * T + ntok]
    lnv = scr.stats[0:1, 5 * T:5 * T + ntok]
    dve.inc(v.tensor_scalar_mul(mean, scr.st_ps[0:1, 0:ntok], 1.0 / D))
    dve.inc(v.tensor_scalar_mul(ex2, scr.st_ps[0:1, T:T + ntok], 1.0 / D))
    v.wait_ge(dve.h, dve.n)
    dve.inc(v.tensor_tensor(var, mean, mean, ALU.mult))
    v.wait_ge(dve.h, dve.n)
    dve.inc(v.tensor_sub(var, ex2, var))
    nc.scalar.wait_ge(dve.h, dve.n)
    act.inc(nc.scalar.activation(lnv, var, AF.Ln, bias=cst[0:1, 0:1]))
    nc.scalar.wait_ge(act.h, act.n)
    act.inc(nc.scalar.activation(aa, lnv, AF.Exp, scale=cst[0:1, 1:2]))
    nc.vector.wait_ge(act.h, act.n)
    dve.inc(v.tensor_tensor(bb, mean, aa, ALU.mult))
    v.wait_ge(dve.h, dve.n)
    dve.inc(v.tensor_scalar_mul(bb, bb, -1.0))
    nc.tensor.wait_ge(dve.h, dve.n)
    nc.tensor.wait_ge(act.h, act.n)
    mm.inc(nc.tensor.matmul(scr.bc_ps[:, 0:ntok], ones_row[0:1, :], aa, start=True, stop=True))
    nc.scalar.wait_ge(mm.h, mm.n)
    act.inc(nc.scalar.copy(scr.ab[:, 0:ntok], scr.bc_ps[:, 0:ntok]))
    nc.tensor.wait_ge(act.h, act.n)
    mm.inc(nc.tensor.matmul(scr.bc_ps[:, 0:ntok], ones_row[0:1, :], bb, start=True, stop=True))
    nc.scalar.wait_ge(mm.h, mm.n)
    act.inc(nc.scalar.copy(scr.ab[:, T:T + ntok], scr.bc_ps[:, 0:ntok]))
    nc.vector.wait_ge(act.h, act.n)
    for ft in range(FT):
        ins = v.tensor_tensor(dst_slices(ft), ssl(ft), scr.ab[:, 0:ntok], ALU.mult)
    dve.inc(ins)
    v.wait_ge(dve.h, dve.n)
    for ft in range(FT):
        ins = v.tensor_tensor(dst_slices(ft), dst_slices(ft), scr.ab[:, T:T + ntok], ALU.add)
    dve.inc(ins)
    if sync_end:
        for e in (nc.tensor, nc.scalar, nc.sync, nc.gpsimd):
            e.wait_ge(dve.h, dve.n)



def emit_attn(P, layer, io, pfx="a_"):
    nc = P.nc
    v = nc.vector
    if layer == 0:
        NQ, NKV, HD = 8, 8, 128
        scale = 1.0 / math.sqrt(128.0)
        MW = 1024
        VW = KW0 = 1024
        VST = HD
    else:
        NQ, NKV, HD = 16, 4, 64
        scale = 1.0 / math.sqrt(64.0)
        MW = 2688
        VST = 65
    QW = NQ * HD
    KW = NKV * HD
    VWID = NKV * VST
    QB = 512
    NQB = S // QB
    NKT = S // 128
    xT = io["xT"]; wqT = io["wqT"]; wkT = io["wkT"]; wvT = io["wvT"]
    maskm = io["maskm"]; attnout = io["attnout"]; roff = io.get("roff", 0)
    qh = io["qh"]; kh = io["kh"]; vh = io["vh"]
    mm, act, dve, ld, st = P.mm, P.act, P.dve, P.ld, P.st

    ones_bf = P.sbuf(pfx + "ones_bf", [128, 128], BF16)
    ones_row = P.sbuf(pfx + "ones_row", [1, 128], F32)
    cst = P.sbuf(pfx + "cst", [128, 3], F32)
    nc.gpsimd.memset(ones_bf[:, :], 1.0)
    nc.gpsimd.memset(ones_row[:, :], 1.0)
    nc.gpsimd.memset(cst[:, 0:1], EPS)
    nc.gpsimd.memset(cst[:, 1:2], -0.5)
    P.gp.inc(nc.gpsimd.memset(cst[:, 2:3], scale))
    for e in (nc.tensor, nc.vector, nc.scalar, nc.sync):
        e.wait_ge(P.gp.h, P.gp.n)

    mask_sb = P.sbuf(pfx + "mask_sb", [128, MW], BF16)
    nc.sync.dma_start(out=mask_sb[:, :], in_=maskm[:, :]).then_inc(ld.h, 16)
    ld.n += 16

    P.mark(pfx + "qkv")
    with P.scope():
        wq_sb = P.sbuf(pfx + "wq_sb", [128, (QW // 128) * FT * 128], BF16)
        wk_sb = P.sbuf(pfx + "wk_sb", [128, (KW // 128) * FT * 128], BF16)
        load_weight_fm(P, wq_sb, wqT, QW // 128, FT)
        load_weight_fm(P, wk_sb, wkT, KW // 128, FT)
        wv_sb = P.sbuf(pfx + "wv_sb", [128, FT * KW], BF16)
        nc.gpsimd.wait_ge(P.lg.h, P.lg.n)
        d = nc.gpsimd.dma_start(out=wv_sb[:, :].rearrange("p (f c) -> p f c", f=FT),
                                in_=hbm_fm_view(wvT))
        P.lg.inc(d, 16)

        TB = 512
        x_sb = P.sbuf(pfx + "x_sb", [128, FT * TB], BF16)
        ln_sb2 = [P.sbuf(pfx + f"ln_sb{i}", [128, FT * TB], BF16) for i in range(2)]
        lnscr = LnScratch(P, TB, name=pfx + "lns")
        qk_ps = [P.psum(pfx + f"qk_ps{i}", [128, TB], F32) for i in range(2)]
        qko_sb = P.sbuf(pfx + "qko_sb", [128, 2 * TB], BF16)
        qsem = [Sem(nc, pfx + f"qsem{i}") for i in range(2)]
        vsem = [Sem(nc, pfx + f"vsem{i}") for i in range(2)]
        vo_sb = P.sbuf(pfx + "vo_sb", [128, 2 * VWID], BF16)
        if layer == 1:
            P.gp.inc(nc.gpsimd.memset(vo_sb[:, :], 1.0))
            nc.scalar.wait_ge(P.gp.h, P.gp.n)

        xv = hbm_fm_view(xT)
        nc.sync.wait_ge(ld.h, ld.n)
        nc.sync.wait_ge(P.lg.h, P.lg.n)
        gslot = [0]
        ev_cnt = {}
        ln_done = {}
        qs_end = {}

        def load_x(tb):
            nc.gpsimd.wait_ge(P.lg.h, P.lg.n)
            if tb - 1 in ln_done:
                g = ln_done[tb - 1]
                nc.gpsimd.wait_ge(mm.h, g[0])
                nc.gpsimd.wait_ge(act.h, g[1])
                nc.gpsimd.wait_ge(dve.h, g[2])
            nc.gpsimd.dma_start(out=x_sb[:, :].rearrange("p (f t) -> p f t", f=FT),
                                in_=xv[:, :, tb * TB:(tb + 1) * TB]).then_inc(P.lg.h, 16)
            P.lg.n += 16

        def do_ln(tb):
            for e in (nc.vector, nc.scalar):
                e.wait_ge(P.lg.h, P.lg.n)
                e.wait_ge(ld.h, ld.n)
            nc.vector.wait_ge(mm.h, qs_end.get(tb - 2, 0))
            emit_ln(P, lnscr,
                    lambda ft: x_sb[:, ft * TB:(ft + 1) * TB],
                    lambda ft: ln_sb2[tb % 2][:, ft * TB:(ft + 1) * TB],
                    TB, (ones_bf, ones_row, cst), sync_end=False)
            ln_done[tb] = (mm.n, act.n, dve.n)

        load_x(0)
        do_ln(0)
        for tb in range(S // TB):
            ln_sb = ln_sb2[tb % 2]
            nc.tensor.wait_ge(dve.h, ln_done[tb][2])
            for which, wsb, nm, dsth in ((0, wq_sb, QW // 128, qh), (1, wk_sb, KW // 128, kh)):
                for m in range(nm):
                    bank = gslot[0] % 2
                    if gslot[0] >= 2:
                        nc.tensor.wait_ge(act.h, ev_cnt[gslot[0] - 2])
                    for ft in range(FT):
                        ins = nc.tensor.matmul(qk_ps[bank][:, :], w_tile(wsb, m, ft, nm),
                                               ln_sb[:, ft * TB:(ft + 1) * TB],
                                               start=(ft == 0), stop=(ft == FT - 1))
                    mm.inc(ins)
                    nc.scalar.wait_ge(mm.h, mm.n)
                    nc.scalar.wait_ge(qsem[bank].h, qsem[bank].n)
                    act.inc(nc.scalar.copy(qko_sb[:, bank * TB:(bank + 1) * TB], qk_ps[bank][:, :]))
                    ev_cnt[gslot[0]] = act.n
                    gslot[0] += 1
                    nc.sync.wait_ge(act.h, act.n)
                    d = nc.sync.dma_start(out=dsth[m * 128:(m + 1) * 128, tb * TB:(tb + 1) * TB],
                                          in_=qko_sb[:, bank * TB:(bank + 1) * TB])
                    qsem[bank].inc(d, 16)
                if which == 0 and tb + 1 < S // TB:
                    load_x(tb + 1)
                    do_ln(tb + 1)
            for tt in range(TB // 128):
                bank = tt % 2
                nw = (KW + 511) // 512
                for nb in range(nw):
                    ncol = min(512, KW - nb * 512)
                    pbank = gslot[0] % 2
                    if gslot[0] >= 2:
                        nc.tensor.wait_ge(act.h, ev_cnt[gslot[0] - 2])
                    for ft in range(FT):
                        ins = nc.tensor.matmul(
                            qk_ps[pbank][:, 0:ncol],
                            ln_sb[:, ft * TB + tt * 128: ft * TB + tt * 128 + 128],
                            wv_sb[:, ft * KW + nb * 512: ft * KW + nb * 512 + ncol],
                            start=(ft == 0), stop=(ft == FT - 1))
                    mm.inc(ins)
                    nc.scalar.wait_ge(mm.h, mm.n)
                    nc.scalar.wait_ge(vsem[bank].h, vsem[bank].n)
                    if layer == 0:
                        dstv = vo_sb[:, bank * VWID + nb * 512: bank * VWID + nb * 512 + ncol]
                    else:
                        dstv = vo_sb[:, bank * VWID:(bank + 1) * VWID].rearrange(
                            "p (g c) -> p g c", c=VST)[:, :, 0:64]
                    act.inc(nc.scalar.copy(dstv, qk_ps[pbank][:, 0:ncol].rearrange(
                        "p (g c) -> p g c", c=64) if layer == 1 else qk_ps[pbank][:, 0:ncol]))
                    ev_cnt[gslot[0]] = act.n
                    gslot[0] += 1
                nc.sync.wait_ge(act.h, act.n)
                t0 = tb * TB + tt * 128
                d = nc.sync.dma_start(out=vh[t0:t0 + 128, :],
                                      in_=vo_sb[:, bank * VWID: bank * VWID + VWID])
                vsem[bank].inc(d, 16)
            qs_end[tb] = mm.n

        P.barrier_engines(extra=qsem + vsem)

    P.mark(pfx + "core")
    def kt_list(qb):
        out = []
        if layer == 0:
            for kt in range(4 * qb + 4):
                if kt >= 4 * qb:
                    delta = kt * 128 - qb * QB
                    out.append((kt, 512 - delta))
                else:
                    out.append((kt, None))
        else:
            for kt in range(NKT):
                delta = kt * 128 - qb * QB
                if -513 <= delta <= -128:
                    continue
                if -1151 <= delta <= 510:
                    out.append((kt, 640 - delta))
                else:
                    out.append((kt, None))
        return out

    with P.scope():
        L1 = layer == 1
        k_sb = [P.sbuf(pfx + f"k_sb{i}", [128, S], BF16) for i in range(2)]
        v_sb = [P.sbuf(pfx + f"v_sb{i}", [128, NKT * VST], BF16) for i in range(2)]
        q_sb = [P.sbuf(pfx + f"q_sb{i}", [128, QB], BF16) for i in range(2)]
        e_pair = [P.sbuf(pfx + f"e_pair{i}", [128, 2 * QB], BF16) for i in range(3)]
        s_pair = [P.psum(pfx + f"s_pair{i}", [128, 2 * QB], F32) for i in range(2)]
        if L1:
            o_psA = P.psum(pfx + "o_psA", [65, QB], F32)
            o_psB = P.psum(pfx + "o_psB", [65, QB], F32)
            nm_ps = P.psum(pfx + "nm_ps", [128, QB], F32)
            t_sb = P.sbuf(pfx + "t_sb", [65, QB], F32)
        else:
            o_ps = [P.psum(pfx + f"o_ps{i}", [128, QB], F32) for i in range(2)]
            sum_ps = [P.psum(pfx + f"sum_ps{i}", [128, QB], F32) for i in range(2)]
        rsum = P.sbuf(pfx + "rsum", [1, 2 * QB], BF16)
        rsum32 = P.sbuf(pfx + "rsum32", [1, 2 * QB], F32)
        rb_sb = P.sbuf(pfx + "rb_sb", [128, 2 * QB], BF16)
        ao_sb = P.sbuf(pfx + "ao_sb", [HD, 2 * QB], BF16)
        ones_rbf = P.sbuf(pfx + "ones_rbf", [1, 128], BF16)
        P.gp.inc(nc.gpsimd.memset(ones_rbf[:, :], 1.0))
        for e in (nc.tensor, nc.vector, nc.scalar, nc.sync):
            e.wait_ge(P.gp.h, P.gp.n)
        asem = [Sem(nc, pfx + f"asem{i}") for i in range(2)]

        ghead = lambda h: h // (NQ // NKV)
        kvsem = [Sem(nc, pfx + f"kvsem{i}") for i in range(2)]
        q2sem = [Sem(nc, pfx + f"q2sem{i}") for i in range(2)]
        kvld = {}
        qld = {}
        last_head_mm = [0]
        qs_hist = {}
        sp_cnt_hist = []
        exp_free_hist = []
        av_free_hist = []
        norm_hist = {}
        pend = {}
        pend_rbfree = [0]

        def load_kv(h):
            hb = h % 2
            g = ghead(h)
            nc.sync.wait_ge(mm.h, last_head_mm[0])
            if L1:
                nc.sync.dma_start(out=k_sb[hb][0:64, :],
                                  in_=kh[g * 64:(g + 1) * 64, :]).then_inc(kvsem[hb].h, 16)
                kvsem[hb].n += 16
                nc.sync.dma_start(out=k_sb[hb][64:128, :],
                                  in_=kh[g * 64:(g + 1) * 64, :]).then_inc(kvsem[hb].h, 16)
                kvsem[hb].n += 16
            else:
                nc.sync.dma_start(out=k_sb[hb][:, :],
                                  in_=kh[g * HD:(g + 1) * HD, :]).then_inc(kvsem[hb].h, 16)
                kvsem[hb].n += 16
            vv = vh.rearrange("(tt p) c -> p tt c", p=128)[:, :, g * VST:(g + 1) * VST]
            nc.sync.dma_start(out=v_sb[hb][:, :].rearrange("p (tt c) -> p tt c", c=VST),
                              in_=vv).then_inc(kvsem[hb].h, 16)
            kvsem[hb].n += 16
            kvld[h] = (hb, kvsem[hb].n)

        def load_q(h, qb, qi):
            qbuf = qi % 2
            if qi >= 2:
                nc.sync.wait_ge(mm.h, qs_hist[qi - 2])
            src = qh[h * HD:(h + 1) * HD, qb * QB:(qb + 1) * QB]
            nc.sync.dma_start(out=q_sb[qbuf][0:HD, :], in_=src).then_inc(q2sem[qbuf].h, 16)
            q2sem[qbuf].n += 16
            if L1:
                nc.sync.dma_start(out=q_sb[qbuf][64:128, :], in_=src).then_inc(q2sem[qbuf].h, 16)
                q2sem[qbuf].n += 16
            qld[qi] = (qbuf, q2sem[qbuf].n)

        def norm_top():
            if not pend or not L1:
                return
            ob = pend["obank"]
            nc.vector.wait_ge(mm.h, pend["mm_last"])
            nc.vector.wait_ge(asem[ob].h, asem[ob].n)
            v.wait_ge(dve.h, dve.n)
            dve.inc(v.tensor_scalar_mul(t_sb[:, :], o_psA[:, :], 1.0))
            v.wait_ge(dve.h, dve.n)
            dve.inc(v.tensor_tensor(t_sb[:, :], o_psB[:, :], t_sb[:, :], ALU.add))
            pend["addB"] = dve.n

        def norm_recip():
            if not pend:
                return
            ob = pend["obank"]
            if L1:
                v.wait_ge(dve.h, pend["addB"])
                dve.inc(v.reciprocal(rsum32[0:1, ob * QB:(ob + 1) * QB], t_sb[64:65, :]))
            else:
                nc.vector.wait_ge(mm.h, pend["mm_last"])
                nc.vector.wait_ge(asem[ob].h, asem[ob].n)
                dve.inc(v.reciprocal(rsum32[0:1, ob * QB:(ob + 1) * QB], sum_ps[ob][0:1, :]))
            v.wait_ge(dve.h, dve.n)
            dve.inc(v.tensor_scalar_mul(rsum[0:1, ob * QB:(ob + 1) * QB],
                                        rsum32[0:1, ob * QB:(ob + 1) * QB], 1.0))
            pend["recip"] = dve.n

        def norm_rb():
            if not pend:
                return
            ob = pend["obank"]
            nc.tensor.wait_ge(dve.h, pend["recip"])
            if L1 and pend_rbfree[0]:
                nc.tensor.wait_ge(dve.h, pend_rbfree[0])
            tgt = nm_ps if L1 else sum_ps[ob]
            mm.inc(nc.tensor.matmul(tgt[:, :], ones_rbf[0:1, :],
                                    rsum[0:1, ob * QB:(ob + 1) * QB],
                                    start=True, stop=True))
            pend["rb"] = mm.n

        def norm_tail():
            if not pend:
                return
            ob = pend["obank"]
            nc.vector.wait_ge(mm.h, pend["rb"])
            if L1:
                dve.inc(v.tensor_tensor(ao_sb[0:HD, ob * QB:(ob + 1) * QB],
                                        nm_ps[0:HD, :], t_sb[0:HD, :], ALU.mult))
                rbfree = dve.n
            else:
                dve.inc(v.tensor_scalar_mul(rb_sb[:, ob * QB:(ob + 1) * QB],
                                            sum_ps[ob][:, :], 1.0))
                rbfree = dve.n
                v.wait_ge(dve.h, dve.n)
                dve.inc(v.tensor_tensor(ao_sb[0:HD, ob * QB:(ob + 1) * QB],
                                        o_ps[ob][0:HD, :],
                                        rb_sb[0:HD, ob * QB:(ob + 1) * QB], ALU.mult))
            nc.sync.wait_ge(dve.h, dve.n)
            d = nc.sync.dma_start(
                out=offap(attnout[pend["h"] * HD:(pend["h"] + 1) * HD,
                                  pend["qb"] * QB:(pend["qb"] + 1) * QB], roff),
                in_=ao_sb[0:HD, ob * QB:(ob + 1) * QB])
            asem[ob].inc(d, 16)
            norm_hist[pend["qi"]] = {"tt": dve.n, "rbfree": rbfree,
                                     "addB": pend.get("addB", 0)}
            pend_rbfree[0] = rbfree if L1 else 0
            pend.clear()

        load_kv(0)
        qi_total = 0
        for h in range(NQ):
            if h + 1 < NQ:
                load_kv(h + 1)
            hb = h % 2
            for qb in range(NQB):
                qi = qi_total
                if qb == 0:
                    load_q(h, 0, qi)
                if qb + 1 < NQB:
                    load_q(h, qb + 1, qi + 1)
                qbuf = qi % 2
                kts = kt_list(qb)
                n = len(kts)
                pairs = [kts[2 * j:2 * j + 2] for j in range((n + 1) // 2)]
                npairs = len(pairs)
                obank = qi % 2
                gp_base = len(sp_cnt_hist)
                j_recip = min(1, npairs - 1)
                j_rb = min(3, npairs - 1)
                j_tail = min(4, npairs - 1)

                def emit_SP(j):
                    gpj = gp_base + j
                    buf = gpj % 2
                    if gpj >= 2:
                        nc.tensor.wait_ge(act.h, exp_free_hist[gpj - 2])
                    for idx, (kt, moff) in enumerate(pairs[j]):
                        if L1:
                            ins = nc.tensor.matmul(
                                s_pair[buf][:, idx * QB:(idx + 1) * QB],
                                k_sb[hb][64 * idx:64 * idx + 64, kt * 128:(kt + 1) * 128],
                                q_sb[qbuf][64 * idx:64 * idx + 64, :],
                                start=True, stop=True, tile_position=(64 * idx, 0))
                        else:
                            ins = nc.tensor.matmul(
                                s_pair[buf][:, idx * QB:(idx + 1) * QB],
                                k_sb[hb][:, kt * 128:(kt + 1) * 128],
                                q_sb[qbuf][:, :], start=True, stop=True)
                        mm.inc(ins)
                    sp_cnt_hist.append(mm.n)

                norm_top()
                nc.tensor.wait_ge(kvsem[kvld[h][0]].h, kvld[h][1])
                nc.tensor.wait_ge(q2sem[qld[qi][0]].h, qld[qi][1])
                emit_SP(0)
                if npairs > 1:
                    emit_SP(1)
                for j in range(npairs):
                    gpj = gp_base + j
                    buf = gpj % 2
                    ebuf = gpj % 3
                    pr = pairs[j]
                    W = len(pr) * QB
                    nc.scalar.wait_ge(mm.h, sp_cnt_hist[gpj])
                    if gpj >= 3:
                        nc.scalar.wait_ge(mm.h, av_free_hist[gpj - 3])
                    act.inc(nc.scalar.activation(e_pair[ebuf][:, 0:W], s_pair[buf][:, 0:W],
                                                 AF.Exp, scale=cst[:, 2:3]))
                    exp_free_hist.append(act.n)
                    rdy = []
                    for idx, (kt, moff) in enumerate(pr):
                        if moff is not None:
                            nc.vector.wait_ge(act.h, act.n)
                            dve.inc(v.tensor_tensor(
                                e_pair[ebuf][:, idx * QB:(idx + 1) * QB],
                                e_pair[ebuf][:, idx * QB:(idx + 1) * QB],
                                mask_sb[:, moff:moff + QB], ALU.mult))
                            rdy.append((dve, dve.n))
                        else:
                            rdy.append((act, act.n))
                    if j == j_recip:
                        norm_recip()
                    if j == j_rb:
                        norm_rb()
                    if j == j_tail:
                        norm_tail()
                    if j + 2 < npairs:
                        emit_SP(j + 2)
                    first = j == 0
                    lastp = j == npairs - 1
                    for idx, (kt, moff) in enumerate(pr):
                        sem_c, cnt = rdy[idx]
                        nc.tensor.wait_ge(sem_c.h, cnt)
                        kfirst = first and idx == 0
                        klast = lastp and idx == len(pr) - 1
                        if L1:
                            if kfirst and pend and pend.get("addB"):
                                nc.tensor.wait_ge(dve.h, pend["addB"])
                            for ph in range(2):
                                ins = nc.tensor.matmul(
                                    (o_psA if ph == 0 else o_psB)[:, :],
                                    v_sb[hb][64 * ph:64 * ph + 64, kt * VST:kt * VST + VST],
                                    e_pair[ebuf][64 * ph:64 * ph + 64, idx * QB:(idx + 1) * QB],
                                    start=kfirst, stop=klast, tile_position=(64 * ph, 0))
                                mm.inc(ins)
                        else:
                            if kfirst and qi >= 2:
                                nc.tensor.wait_ge(dve.h, norm_hist[qi - 2]["tt"])
                                nc.tensor.wait_ge(dve.h, norm_hist[qi - 2]["rbfree"])
                            ins = nc.tensor.matmul(
                                o_ps[obank][:, :], v_sb[hb][:, kt * VST:(kt + 1) * VST],
                                e_pair[ebuf][:, idx * QB:(idx + 1) * QB],
                                start=kfirst, stop=klast)
                            mm.inc(ins)
                            ins = nc.tensor.matmul(
                                sum_ps[obank][0:1, :], ones_bf[:, 0:1],
                                e_pair[ebuf][:, idx * QB:(idx + 1) * QB],
                                start=kfirst, stop=klast)
                            mm.inc(ins)
                    av_free_hist.append(mm.n)
                qs_hist[qi] = mm.n
                pend.update(qi=qi, obank=obank, h=h, qb=qb, mm_last=mm.n)
                qi_total += 1
            last_head_mm[0] = qs_hist[qi_total - 1]
        norm_top()
        norm_recip()
        norm_rb()
        norm_tail()
        P.barrier_engines(extra=asem + kvsem + q2sem)
    nc.sync.wait_ge(st.h, st.n)


def build_attn(layer):
    P = Prog()
    nc = P.nc
    if layer == 0:
        QW, KW, MW, VW = 1024, 1024, 1024, 1024
    else:
        QW, KW, MW, VW = 1024, 256, 2688, 260
    io = {
        "xT": nc.dram_tensor("xT", [D, S], F32, kind="ExternalInput").ap(),
        "wqT": nc.dram_tensor("wqT", [D, QW], F32, kind="ExternalInput").ap(),
        "wkT": nc.dram_tensor("wkT", [D, KW], F32, kind="ExternalInput").ap(),
        "wvT": nc.dram_tensor("wvT", [D, KW], F32, kind="ExternalInput").ap(),
        "maskm": nc.dram_tensor("maskm", [128, MW], BF16, kind="ExternalInput").ap(),
        "attnout": nc.dram_tensor("attnout", [QW, S], BF16, kind="ExternalOutput").ap(),
        "qh": nc.dram_tensor("qh", [QW, S], BF16).ap(),
        "kh": nc.dram_tensor("kh", [KW, S], BF16).ap(),
        "vh": nc.dram_tensor("vh", [S, VW], BF16).ap(),
    }
    with P.scope():
        emit_attn(P, layer, io)
    return P




def emit_ffn0(P, io, pfx="f0_"):
    nc = P.nc
    v = nc.vector
    mm, act, dve, ld, st, lg = P.mm, P.act, P.dve, P.ld, P.st, P.lg
    T = TOK
    TBA = 512
    NTA = T // TBA
    TBB = 256
    NTB = T // TBB
    NE = 8
    toff = io.get("toff", 0)
    xT = io["xT"]; attnT = io["attnT"]; x2T = io["x2T"]
    woT = io["woT"]; down_w = io["down_w"]; down_b = io["down_b"]
    up_w = io["up_w"]; upcore_b = io["upcore_b"]
    router_w = io["router_w"]; router_b = io["router_b"]; cost_t = io["cost_t"]
    shared_w1 = io["shared_w1"]; shared_w2 = io["shared_w2"]; shared_b = io["shared_b"]
    core_w = io["core_w"]; ew1 = io["ew1"]; ew2 = io["ew2"]
    eb1 = io["eb1"]; eb2 = io["eb2"]; idn_h = io["idn_h"]; sel_h = io["sel_h"]
    x1h = io["x1h"]; ln2h = io["ln2h"]; gph = io["gph"]
    EHID = HID + [1024]

    ones_bf = P.sbuf(pfx + "ones_bf", [128, 1], BF16)
    ones_row = P.sbuf(pfx + "ones_row", [1, 128], F32)
    cst = P.sbuf(pfx + "cst", [128, 3], F32)
    idn = P.sbuf(pfx + "idn", [128, 128], F32)
    nc.gpsimd.memset(ones_bf[:, :], 1.0)
    nc.gpsimd.memset(ones_row[:, :], 1.0)
    nc.gpsimd.memset(cst[:, 0:1], EPS)
    P.gp.inc(nc.gpsimd.memset(cst[:, 1:2], -0.5))
    nc.sync.dma_start(out=idn[:, :], in_=idn_h[:, :]).then_inc(ld.h, 16); ld.n += 16
    sel_sb = P.sbuf(pfx + "sel_sb", [8, 1024], BF16)
    nc.sync.dma_start(out=sel_sb[:, :], in_=sel_h[:, :]).then_inc(ld.h, 16); ld.n += 16

    pool = P.psum(pfx + "pool", [128, 4096], F32)
    ln_st = pool[0:1, 0:1024]
    ln_bc = pool[:, 1024:1536]
    ps_m = pool[:, 1024:1536]
    ps_x = pool[:, 1536:2048]

    x_tb = P.sbuf(pfx + "x_tb", [128, FT * 512], F32)
    ln_tb = P.sbuf(pfx + "ln_tb", [128, FT * 512], BF16)
    rw_sb = P.sbuf(pfx + "rw_sb", [128, FT * 8], BF16)
    cb_sb = P.sbuf(pfx + "cb_sb", [128, T], BF16)
    ct_bf = P.sbuf(pfx + "ct_bf", [8, T], BF16)
    b2c_sb = P.sbuf(pfx + "b2c_sb", [8, 1024], BF16)
    ctot_sb = P.sbuf(pfx + "ctot_sb", [128, 8], F32)
    biases = P.sbuf(pfx + "biases", [128, 304], F32)
    db_c = biases[:, 0:8]
    ucb_c = biases[:, 8:40]
    fb_c = biases[:, 40:56]
    sb_c = biases[:, 56:72]
    b1_c = biases[:, 72:232]
    b2_c = biases[:, 232:296]
    small = P.sbuf(pfx + "small", [128, 64], F32)
    rbadj = small[0:1, 0:8]
    coef = P.sbuf(pfx + "coef", [128, 64], F32)
    gb1_sb = P.sbuf(pfx + "gb1_sb", [128, 160], BF16)
    ce_sb = P.sbuf(pfx + "ce_sb", [128, 64], F32)
    b2m = P.sbuf(pfx + "b2m", [128, 64], F32)

    def bar():
        P.barrier_engines()

    nc.gpsimd.wait_ge(lg.h, lg.n)
    nc.gpsimd.dma_start(out=rw_sb[:, :].rearrange("p (f c) -> p f c", f=FT),
                        in_=hbm_fm_view(router_w)).then_inc(lg.h, 16); lg.n += 16
    for dst, srcap in ((db_c, down_b), (ucb_c, upcore_b), (sb_c, shared_b),
                       (b1_c, eb1), (b2_c, eb2)):
        nc.sync.dma_start(out=dst, in_=srcap[:, :]).then_inc(ld.h, 16); ld.n += 16
    nc.sync.dma_start(out=small[0:1, 8:16], in_=router_b[:, :]).then_inc(ld.h, 16); ld.n += 16
    nc.sync.dma_start(out=small[0:1, 16:24], in_=cost_t[:, :]).then_inc(ld.h, 16); ld.n += 16
    bar()
    dve.inc(v.tensor_sub(rbadj, small[0:1, 8:16], small[0:1, 16:24]))
    dve.inc(v.tensor_add(fb_c, ucb_c[:, 0:16], ucb_c[:, 16:32]))
    bar()

    xv = hbm_fm_view(xT)
    av = hbm_fm_view(attnT)
    x1v = hbm_fm_view(x1h)
    lnv = hbm_fm_view(ln2h)

    P.mark("f0A")
    a_scope = P.scope(); a_scope.__enter__()
    big = P.sbuf(pfx + "big", [128, 3 * FT * 512], BF16)
    attn_tb = big[:, 0:FT * 512]
    ln_xb = big[:, FT * 512:2 * FT * 512]
    ln_sq = big[:, 2 * FT * 512:3 * FT * 512]
    lnscr = LnScratch(P, 512, name=pfx + "lns", xb=ln_xb, sq=ln_sq,
                      st_ps=pool[0:1, 0:1024], bc_ps=ln_bc)
    wo_sb = P.sbuf(pfx + "wo_sb", [128, FT * FT * 128], BF16)
    load_weight_fm(P, wo_sb, woT, FT, FT)
    psx = [pool[:, 1536:2048], pool[:, 2048:2560]]
    nc.tensor.wait_ge(lg.h, lg.n)
    add_cnt = {}
    gm = 0
    for tb in range(NTA):
        tsl = slice(tb * TBA, (tb + 1) * TBA)
        nc.sync.wait_ge(st.h, st.n)
        nc.sync.wait_ge(mm.h, mm.n)
        nc.sync.wait_ge(dve.h, dve.n)
        nc.sync.dma_start(out=x_tb[:, :].rearrange("p (f t) -> p f t", f=FT),
                          in_=offap(xv[:, :, tsl], toff)).then_inc(ld.h, 16); ld.n += 16
        nc.sync.dma_start(out=attn_tb.rearrange("p (f t) -> p f t", f=FT),
                          in_=offap(av[:, :, tsl], toff)).then_inc(ld.h, 16); ld.n += 16
        nc.tensor.wait_ge(ld.h, ld.n)
        for m in range(FT):
            if gm >= 2:
                nc.tensor.wait_ge(dve.h, add_cnt[gm - 2])
            for ft in range(FT):
                ins = nc.tensor.matmul(psx[gm % 2][:, :], w_tile(wo_sb, m, ft, FT),
                                       attn_tb[:, ft * 512:(ft + 1) * 512],
                                       start=(ft == 0), stop=(ft == FT - 1))
            mm.inc(ins)
            nc.vector.wait_ge(mm.h, mm.n)
            dve.inc(v.tensor_add(x_tb[:, m * 512:(m + 1) * 512],
                                 psx[gm % 2][:, :], x_tb[:, m * 512:(m + 1) * 512]))
            add_cnt[gm] = dve.n
            gm += 1
        nc.sync.wait_ge(dve.h, dve.n)
        nc.sync.dma_start(out=x1v[:, :, tsl],
                          in_=x_tb[:, :].rearrange("p (f t) -> p f t", f=FT)).then_inc(st.h, 16)
        st.n += 16
        emit_ln(P, lnscr,
                lambda ft: x_tb[:, ft * 512:(ft + 1) * 512],
                lambda ft: ln_tb[:, ft * 512:(ft + 1) * 512],
                512, (ones_bf, ones_row, cst))
        nc.sync.wait_ge(dve.h, dve.n)
        nc.sync.dma_start(out=lnv[:, :, tsl],
                          in_=ln_tb[:, :].rearrange("p (f t) -> p f t", f=FT)).then_inc(st.h, 16)
        st.n += 16
        for e in (nc.vector, nc.tensor, nc.scalar):
            e.wait_ge(st.h, st.n)
    bar()
    a_scope.__exit__(None, None, None)

    P.mark("f0B")
    b_scope = P.scope(); b_scope.__enter__()
    down_sb = P.sbuf(pfx + "down_sb", [128, FT * 8 * 128], BF16)
    g_sb = P.sbuf(pfx + "g_sb", [128, 8 * T], BF16)
    load_weight_fm(P, down_sb, down_w, 8, FT)
    nc.tensor.wait_ge(lg.h, lg.n)
    for tb in range(NTB):
        tsl = slice(tb * TBB, (tb + 1) * TBB)
        nc.sync.wait_ge(mm.h, mm.n)
        nc.sync.dma_start(out=ln_tb[:, 0:FT * TBB].rearrange("p (f t) -> p f t", f=FT),
                          in_=lnv[:, :, tsl]).then_inc(ld.h, 16); ld.n += 16
        nc.tensor.wait_ge(ld.h, ld.n)
        for m in range(8):
            nc.tensor.wait_ge(act.h, act.n)
            for ft in range(FT):
                ins = nc.tensor.matmul(ps_x[:, 0:TBB], w_tile(down_sb, m, ft, 8),
                                       ln_tb[:, ft * TBB:(ft + 1) * TBB],
                                       start=(ft == 0), stop=(ft == FT - 1))
            mm.inc(ins)
            nc.scalar.wait_ge(mm.h, mm.n)
            act.inc(nc.scalar.activation(g_sb[:, m * T + tb * TBB: m * T + (tb + 1) * TBB],
                                         ps_x[:, 0:TBB], GELU, bias=db_c[:, m:m + 1]))
    bar()

    P.mark("f0C")
    for tt in range(T // 128):
        tbb = (tt * 128) // TBB
        o = (tt * 128) % TBB
        if o == 0:
            nc.sync.wait_ge(mm.h, mm.n)
            nc.sync.dma_start(out=ln_tb[:, 0:FT * TBB].rearrange("p (f t) -> p f t", f=FT),
                              in_=lnv[:, :, tbb * TBB:(tbb + 1) * TBB]).then_inc(ld.h, 16)
            ld.n += 16
        nc.tensor.wait_ge(ld.h, ld.n)
        nc.tensor.wait_ge(act.h, act.n)
        for ft in range(FT):
            nc.tensor.matmul(ps_m[:, 0:8],
                             ln_tb[:, ft * TBB + o: ft * TBB + o + 128],
                             rw_sb[:, ft * 8:(ft + 1) * 8],
                             start=(ft == 0), stop=False)
        mm.inc(nc.tensor.matmul(ps_m[:, 0:8], ones_row[0:1, :], rbadj,
                                start=False, stop=True))
        nc.vector.wait_ge(mm.h, mm.n)
        mx = coef[:, 8:9]; sm = coef[:, 9:10]; rr = coef[:, 10:11]
        m1 = coef[:, 11:12]; m2 = coef[:, 12:13]; dd = coef[:, 13:14]; g1 = coef[:, 14:15]
        e_r = coef[:, 16:24]; sel1 = coef[:, 24:32]; msk = coef[:, 32:40]
        sel2 = coef[:, 40:48]; cc = coef[:, 48:56]
        dve.inc(v.reduce_max(mx, ps_m[:, 0:8], mybir.AxisListType.X, negate=True))
        nc.scalar.wait_ge(dve.h, dve.n)
        act.inc(nc.scalar.activation(e_r, ps_m[:, 0:8], AF.Exp, bias=mx))
        nc.vector.wait_ge(act.h, act.n)
        dve.inc(v.reduce_sum(sm, e_r, mybir.AxisListType.X)); v.wait_ge(dve.h, dve.n)
        dve.inc(v.reciprocal(rr, sm))
        dve.inc(v.reduce_max(m1, e_r, mybir.AxisListType.X)); v.wait_ge(dve.h, dve.n)
        dve.inc(v.tensor_scalar(sel1, e_r, m1, None, ALU.is_equal)); v.wait_ge(dve.h, dve.n)
        dve.inc(v.scalar_tensor_tensor(msk, sel1, -1e30, e_r, ALU.mult, ALU.add)); v.wait_ge(dve.h, dve.n)
        dve.inc(v.reduce_max(m2, msk, mybir.AxisListType.X)); v.wait_ge(dve.h, dve.n)
        dve.inc(v.tensor_scalar(sel2, msk, m2, None, ALU.is_equal))
        dve.inc(v.tensor_sub(dd, m1, m2)); v.wait_ge(dve.h, dve.n)
        dve.inc(v.tensor_tensor(dd, dd, rr, ALU.mult)); nc.scalar.wait_ge(dve.h, dve.n)
        act.inc(nc.scalar.activation(g1, dd, AF.Sigmoid))
        nc.vector.wait_ge(act.h, act.n)
        dve.inc(v.tensor_scalar(cc, sel1, g1, None, ALU.mult))
        dve.inc(v.tensor_scalar(msk, sel2, g1, None, ALU.mult)); v.wait_ge(dve.h, dve.n)
        dve.inc(v.tensor_sub(msk, sel2, msk)); v.wait_ge(dve.h, dve.n)
        dve.inc(v.tensor_add(cc, cc, msk)); nc.tensor.wait_ge(dve.h, dve.n)
        mm.inc(nc.tensor.transpose(ps_m[0:8, 128:256], cc, idn[:, :]))
        nc.scalar.wait_ge(mm.h, mm.n)
        act.inc(nc.scalar.copy(ct_bf[:, tt * 128:(tt + 1) * 128], ps_m[0:8, 128:256]))
        nc.vector.wait_ge(act.h, act.n)
    bar()

    P.mark("f0D")
    act.inc(nc.scalar.activation(gb1_sb[:, 0:160], b1_c, GELU))
    bar()
    hofs = [0]
    for e in range(NE):
        hofs.append(hofs[-1] + HID[e] // 128)
    with P.scope():
        w2big = P.sbuf(pfx + "w2big", [128, 32 * 1024], BF16)
        for e in range(NE):
            nh = HID[e] // 128
            nc.gpsimd.wait_ge(lg.h, lg.n)
            nc.gpsimd.wait_ge(mm.h, mm.n)
            nc.gpsimd.dma_start(
                out=w2big[:, 0:nh * 1024].rearrange("p (k c) -> p k c", k=nh),
                in_=ew2[e].rearrange("(k p) c -> p k c", p=128)).then_inc(lg.h, 16)
            lg.n += 16
            nc.tensor.wait_ge(lg.h, lg.n)
            nc.tensor.wait_ge(act.h, act.n)
            for m2 in range(8):
                for hk in range(nh):
                    ins = nc.tensor.matmul(ps_m[:, 256 + m2:257 + m2],
                                           w2big[:, hk * 1024 + m2 * 128: hk * 1024 + (m2 + 1) * 128],
                                           gb1_sb[:, hofs[e] + hk: hofs[e] + hk + 1],
                                           start=(hk == 0), stop=(hk == nh - 1))
                mm.inc(ins)
            nc.vector.wait_ge(mm.h, mm.n)
            dve.inc(v.tensor_add(ce_sb[:, e:e + 57:8], ps_m[:, 256:264], b2_c[:, e:e + 57:8]))
            nc.tensor.wait_ge(dve.h, dve.n)
    bar()
    for m2 in range(8):
        dve.inc(v.reduce_sum(ctot_sb[:, m2:m2 + 1], ce_sb[:, m2 * 8:(m2 + 1) * 8],
                             mybir.AxisListType.X))
    v.wait_ge(dve.h, dve.n)
    dve.inc(v.scalar_tensor_tensor(ctot_sb[:, :], sb_c[:, 8:16], 0.1, ctot_sb[:, :],
                                   ALU.mult, ALU.add))
    dve.inc(v.tensor_sub(b2m[:, :], b2_c, ce_sb[:, :]))
    nc.tensor.wait_ge(dve.h, dve.n)
    for m2 in range(8):
        colap = b2m[:, m2 * 8:(m2 + 1) * 8]
        nc.tensor.wait_ge(act.h, act.n)
        mm.inc(nc.tensor.transpose(ps_m[0:8, 128:256], colap, idn[:, :]))
        nc.scalar.wait_ge(mm.h, mm.n)
        act.inc(nc.scalar.copy(b2c_sb[:, m2 * 128:(m2 + 1) * 128], ps_m[0:8, 128:256]))
    bar()

    P.mark("f0E1")
    with P.scope():
        TBE = 512
        NTE = T // TBE
        wsB2 = P.sbuf(pfx + "wsB2", [128, 2 * 8 * 128], BF16)
        gpw = P.sbuf(pfx + "gpw", [128, 2 * T], BF16)
        a_ps = [pool[:, 0:512], pool[:, 512:1024]]
        w1sem = [Sem(nc, pfx + f"w1sem{i}") for i in range(2)]
        gpsem = [Sem(nc, pfx + f"gpsem{i}") for i in range(2)]
        exphist = []
        mulhist = []
        wload = {}
        ehm = [(e, hm) for e in range(9) for hm in range((HID + [1024])[e] // 128)]

        def load_w1(idx):
            e, hm = ehm[idx]
            sl = idx % 2
            w1src = ew1[e] if e < NE else shared_w1
            nc.gpsimd.wait_ge(lg.h, lg.n)
            if idx >= 2:
                nc.gpsimd.wait_ge(mm.h, wuse[idx - 2])
            d = nc.gpsimd.dma_start(
                out=wsB2[:, sl * 1024:(sl + 1) * 1024].rearrange("p (k c) -> p k c", k=8),
                in_=w1src.rearrange("(k p) c -> p k c", p=128)[:, :, hm * 128:(hm + 1) * 128])
            w1sem[sl].inc(d, 16)
            wload[idx] = (sl, w1sem[sl].n)

        wuse = {}
        load_w1(0)
        load_w1(1)
        eprev = -1
        for idx, (e, hm) in enumerate(ehm):
            if e != eprev:
                eprev = e
                if e < NE:
                    for tb4 in range(NTA):
                        nc.tensor.wait_ge(act.h, act.n)
                        nc.tensor.wait_ge(dve.h, dve.n)
                        mm.inc(nc.tensor.matmul(ps_m[:, 0:512], sel_sb[:, e * 128:(e + 1) * 128],
                                                ct_bf[:, tb4 * 512:(tb4 + 1) * 512],
                                                start=True, stop=True))
                        nc.scalar.wait_ge(mm.h, mm.n)
                        act.inc(nc.scalar.copy(cb_sb[:, tb4 * 512:(tb4 + 1) * 512], ps_m[:, 0:512]))
                    nc.vector.wait_ge(act.h, act.n)
            sl, wtgt = wload[idx]
            gsl = idx % 2
            bias = (b1_c[:, hofs[e] + hm: hofs[e] + hm + 1] if e < NE
                    else sb_c[:, hm:hm + 1])
            nc.tensor.wait_ge(w1sem[sl].h, wtgt)
            for tb in range(NTE):
                bank = tb % 2
                if len(exphist) >= 2:
                    nc.tensor.wait_ge(act.h, exphist[-2])
                for kk in range(8):
                    ins = nc.tensor.matmul(a_ps[bank][:, :],
                                           wsB2[:, sl * 1024 + kk * 128: sl * 1024 + (kk + 1) * 128],
                                           g_sb[:, kk * T + tb * TBE: kk * T + (tb + 1) * TBE],
                                           start=(kk == 0), stop=(kk == 7))
                mm.inc(ins)
                nc.scalar.wait_ge(mm.h, mm.n)
                if tb == 0 and idx >= 2:
                    nc.scalar.wait_ge(gpsem[gsl].h, gpsem[gsl].n)
                act.inc(nc.scalar.activation(gpw[:, gsl * T + tb * TBE: gsl * T + (tb + 1) * TBE],
                                             a_ps[bank][:, :], GELU, bias=bias))
                exphist.append(act.n)
                nc.vector.wait_ge(act.h, act.n)
                if e < NE:
                    dve.inc(v.tensor_tensor(gpw[:, gsl * T + tb * TBE: gsl * T + (tb + 1) * TBE],
                                            gpw[:, gsl * T + tb * TBE: gsl * T + (tb + 1) * TBE],
                                            cb_sb[:, tb * TBE:(tb + 1) * TBE], ALU.mult))
                else:
                    dve.inc(v.tensor_scalar_mul(gpw[:, gsl * T + tb * TBE: gsl * T + (tb + 1) * TBE],
                                                gpw[:, gsl * T + tb * TBE: gsl * T + (tb + 1) * TBE], 0.1))
            wuse[idx] = mm.n
            if idx + 2 < len(ehm):
                load_w1(idx + 2)
            nc.sync.wait_ge(dve.h, dve.n)
            d = nc.sync.dma_start(out=gph[e][hm * 128:(hm + 1) * 128, :],
                                  in_=gpw[:, gsl * T:(gsl + 1) * T])
            gpsem[gsl].inc(d, 16)
        P.barrier_engines(extra=gpsem + w1sem)
    b_scope.__exit__(None, None, None)

    P.mark("f0E2")
    with P.scope():
        TBS = 512
        NTS = T // TBS
        GRP = 6
        S_sb = P.sbuf(pfx + "S_sb", [128, 8 * T], F32)
        e2_scope = P.scope(); e2_scope.__enter__()
        w2s = P.sbuf(pfx + "w2s", [128, 2 * GRP * 1024], BF16)
        gprd = P.sbuf(pfx + "gprd", [128, 2 * GRP * T], BF16)
        w2sem = [Sem(nc, pfx + f"w2sem{i}") for i in range(2)]
        gdsem = [Sem(nc, pfx + f"gdsem{i}") for i in range(2)]
        psets = [pool[:, 0:2048], pool[:, 2048:4096]]

        allhk = [(e, hk) for e in range(9) for hk in range((HID + [1024])[e] // 128)]
        groups = [allhk[i:i + GRP] for i in range(0, len(allhk), GRP)]
        grpuse = {}
        evhist = {}

        def load_grp(gi):
            sl = gi % 2
            if gi >= 2:
                nc.gpsimd.wait_ge(mm.h, grpuse[gi - 2])
                nc.sync.wait_ge(mm.h, grpuse[gi - 2])
            for j, (e, hk) in enumerate(groups[gi]):
                w2src = ew2[e] if e < NE else shared_w2
                nc.gpsimd.wait_ge(lg.h, lg.n)
                d = nc.gpsimd.dma_start(
                    out=w2s[:, (sl * GRP + j) * 1024:(sl * GRP + j + 1) * 1024],
                    in_=w2src[hk * 128:(hk + 1) * 128, :])
                w2sem[sl].inc(d, 16)
                d = nc.sync.dma_start(
                    out=gprd[:, (sl * GRP + j) * T:(sl * GRP + j + 1) * T],
                    in_=gph[e][hk * 128:(hk + 1) * 128, :])
                gdsem[sl].inc(d, 16)
            gload[gi] = (sl, w2sem[sl].n, gdsem[sl].n)

        gload = {}
        load_grp(0)
        hidx = 0
        for gi, grp in enumerate(groups):
            if gi + 1 < len(groups):
                load_grp(gi + 1)
            sl, wt, gt = gload[gi]
            nc.tensor.wait_ge(w2sem[sl].h, wt)
            nc.tensor.wait_ge(gdsem[sl].h, gt)
            for tb in range(NTS):
                for hf in range(2):
                    pset = psets[hf]
                    nc.tensor.wait_ge(dve.h, evhist.get(hidx - 2, 0))
                    for j in range(len(grp)):
                        for mi in range(4):
                            m2 = hf * 4 + mi
                            ins = nc.tensor.matmul(
                                pset[:, mi * 512:(mi + 1) * 512],
                                w2s[:, (sl * GRP + j) * 1024 + m2 * 128:
                                    (sl * GRP + j) * 1024 + (m2 + 1) * 128],
                                gprd[:, (sl * GRP + j) * T + tb * TBS:
                                     (sl * GRP + j) * T + (tb + 1) * TBS],
                                start=(j == 0), stop=(j == len(grp) - 1))
                    mm.inc(ins)
                    nc.vector.wait_ge(mm.h, mm.n)
                    for mi in range(4):
                        m2 = hf * 4 + mi
                        dst = S_sb[:, m2 * T + tb * TBS: m2 * T + (tb + 1) * TBS]
                        if gi == 0:
                            ins = v.tensor_copy(dst, pset[:, mi * 512:(mi + 1) * 512])
                        else:
                            ins = v.tensor_tensor(dst, pset[:, mi * 512:(mi + 1) * 512],
                                                  dst, ALU.add)
                    dve.inc(ins)
                    evhist[hidx] = dve.n
                    hidx += 1
            grpuse[gi] = mm.n
        P.barrier_engines(extra=w2sem + gdsem)
        for tb in range(NTA):
            tsl = slice(tb * TBA, (tb + 1) * TBA)
            for m2 in range(8):
                ins = nc.tensor.matmul(pool[:, (m2 % 4) * 512:(m2 % 4 + 1) * 512],
                                       b2c_sb[:, m2 * 128:(m2 + 1) * 128],
                                       ct_bf[:, tsl], start=True, stop=True)
                mm.inc(ins)
                nc.vector.wait_ge(mm.h, mm.n)
                ins2 = v.scalar_tensor_tensor(
                    S_sb[:, m2 * T + tb * TBA: m2 * T + (tb + 1) * TBA],
                    pool[:, (m2 % 4) * 512:(m2 % 4 + 1) * 512],
                    ctot_sb[:, m2:m2 + 1],
                    S_sb[:, m2 * T + tb * TBA: m2 * T + (tb + 1) * TBA],
                    ALU.add, ALU.add)
                dve.inc(ins2)
                nc.tensor.wait_ge(dve.h, dve.n)
        bar()

        e2_scope.__exit__(None, None, None)
        P.mark("f0F")
        fevac = {}
        gln_tb = P.sbuf(pfx + "gln_tb", [128, FT * 512], BF16)
        smid_bf = P.sbuf(pfx + "smid", [128, 8 * 512], BF16)
        wsA2 = P.sbuf(pfx + "wsA2", [128, 2 * FT * 128], BF16)
        wsU2 = P.sbuf(pfx + "wsU2", [128, 2 * 8 * 128], BF16)
        casem = [Sem(nc, pfx + f"casem{i}") for i in range(2)]
        cusem = [Sem(nc, pfx + f"cusem{i}") for i in range(2)]
        fuse = {}
        fload = {}

        def load_f(fi):
            tb, m = divmod(fi, FT)
            sl = fi % 2
            nc.gpsimd.wait_ge(lg.h, lg.n)
            if fi >= 2:
                nc.gpsimd.wait_ge(mm.h, fuse[fi - 2])
            d = nc.gpsimd.dma_start(
                out=wsA2[:, sl * 2048:(sl + 1) * 2048].rearrange("p (f c) -> p f c", f=FT),
                in_=hbm_fm_view(core_w)[:, :, m * 128:(m + 1) * 128])
            casem[sl].inc(d, 16)
            nc.gpsimd.wait_ge(lg.h, lg.n)
            d = nc.gpsimd.dma_start(
                out=wsU2[:, sl * 1024:(sl + 1) * 1024].rearrange("p (k c) -> p k c", k=8),
                in_=up_w.rearrange("(k p) c -> p k c", p=128)[:, :, m * 128:(m + 1) * 128])
            cusem[sl].inc(d, 16)
            fload[fi] = (sl, casem[sl].n, cusem[sl].n)

        load_f(0)
        load_f(1)
        for tb in range(NTA):
            tsl = slice(tb * TBA, (tb + 1) * TBA)
            nc.sync.wait_ge(st.h, st.n)
            nc.sync.wait_ge(dve.h, dve.n)
            nc.sync.dma_start(out=ln_tb[:, :].rearrange("p (f t) -> p f t", f=FT),
                              in_=lnv[:, :, tsl]).then_inc(ld.h, 16); ld.n += 16
            nc.sync.dma_start(out=x_tb[:, :].rearrange("p (f t) -> p f t", f=FT),
                              in_=x1v[:, :, tsl]).then_inc(ld.h, 16); ld.n += 16
            nc.scalar.wait_ge(ld.h, ld.n)
            nc.scalar.wait_ge(dve.h, dve.n)
            for ft in range(FT):
                ins = nc.scalar.activation(gln_tb[:, ft * 512:(ft + 1) * 512],
                                           ln_tb[:, ft * 512:(ft + 1) * 512], GELU)
            act.inc(ins)
            nc.vector.wait_ge(act.h, act.n)
            for m2 in range(8):
                ins = v.tensor_copy(smid_bf[:, m2 * 512:(m2 + 1) * 512],
                                    S_sb[:, m2 * T + tb * TBA: m2 * T + (tb + 1) * TBA])
            dve.inc(ins)
            for e_ in (nc.tensor,):
                e_.wait_ge(act.h, act.n); e_.wait_ge(dve.h, dve.n)
            for m in range(FT):
                fi = tb * FT + m
                sl, ca, cu = fload[fi]
                nc.tensor.wait_ge(casem[sl].h, ca)
                nc.tensor.wait_ge(cusem[sl].h, cu)
                bank = fi % 2
                pf = pool[:, (2 + bank) * 512:(3 + bank) * 512]
                if fi >= 2:
                    nc.tensor.wait_ge(dve.h, fevac[fi - 2])
                for kk in range(8):
                    nc.tensor.matmul(pf, wsU2[:, sl * 1024 + kk * 128: sl * 1024 + (kk + 1) * 128],
                                     smid_bf[:, kk * 512:(kk + 1) * 512],
                                     start=(kk == 0), stop=False)
                for ft in range(FT):
                    ins = nc.tensor.matmul(pf, wsA2[:, sl * 2048 + ft * 128: sl * 2048 + (ft + 1) * 128],
                                           gln_tb[:, ft * 512:(ft + 1) * 512],
                                           start=False, stop=(ft == FT - 1))
                mm.inc(ins)
                fuse[fi] = mm.n
                if fi + 2 < NTA * FT:
                    load_f(fi + 2)
                nc.vector.wait_ge(mm.h, mm.n)
                dve.inc(v.scalar_tensor_tensor(x_tb[:, m * 512:(m + 1) * 512],
                                               pf, fb_c[:, m:m + 1],
                                               x_tb[:, m * 512:(m + 1) * 512],
                                               ALU.add, ALU.add))
                fevac[fi] = dve.n
            nc.sync.wait_ge(dve.h, dve.n)
            nc.sync.dma_start(out=offap(hbm_fm_view(x2T)[:, :, tsl], toff),
                              in_=x_tb[:, :].rearrange("p (f t) -> p f t", f=FT)).then_inc(st.h, 16)
            st.n += 16
        P.barrier_engines(extra=casem + cusem)
    nc.sync.wait_ge(st.h, st.n)


def build_ffn0():
    P = Prog()
    nc = P.nc
    NE = 8
    io = {
        "xT": nc.dram_tensor("xT", [D, TOK], F32, kind="ExternalInput").ap(),
        "attnT": nc.dram_tensor("attnT", [D, TOK], BF16, kind="ExternalInput").ap(),
        "woT": nc.dram_tensor("woT", [D, D], F32, kind="ExternalInput").ap(),
        "down_w": nc.dram_tensor("down_w", [D, 1024], F32, kind="ExternalInput").ap(),
        "down_b": nc.dram_tensor("down_b", [128, 8], F32, kind="ExternalInput").ap(),
        "up_w": nc.dram_tensor("up_w", [1024, D], F32, kind="ExternalInput").ap(),
        "upcore_b": nc.dram_tensor("upcore_b", [128, 32], F32, kind="ExternalInput").ap(),
        "router_w": nc.dram_tensor("router_w", [D, 8], F32, kind="ExternalInput").ap(),
        "router_b": nc.dram_tensor("router_b", [1, 8], F32, kind="ExternalInput").ap(),
        "cost_t": nc.dram_tensor("cost_t", [1, 8], F32, kind="ExternalInput").ap(),
        "shared_w1": nc.dram_tensor("shared_w1", [1024, 1024], F32, kind="ExternalInput").ap(),
        "shared_w2": nc.dram_tensor("shared_w2", [1024, 1024], F32, kind="ExternalInput").ap(),
        "shared_b": nc.dram_tensor("shared_b", [128, 16], F32, kind="ExternalInput").ap(),
        "core_w": nc.dram_tensor("core_w", [D, D], F32, kind="ExternalInput").ap(),
        "ew1": [nc.dram_tensor(f"e{e}_w1", [1024, HID[e]], F32, kind="ExternalInput").ap() for e in range(NE)],
        "ew2": [nc.dram_tensor(f"e{e}_w2", [HID[e], 1024], F32, kind="ExternalInput").ap() for e in range(NE)],
        "eb1": nc.dram_tensor("eb1", [128, 160], F32, kind="ExternalInput").ap(),
        "eb2": nc.dram_tensor("eb2", [128, 64], F32, kind="ExternalInput").ap(),
        "idn_h": nc.dram_tensor("idn_h", [128, 128], F32, kind="ExternalInput").ap(),
        "sel_h": nc.dram_tensor("sel_h", [8, 1024], BF16, kind="ExternalInput").ap(),
        "x2T": nc.dram_tensor("x2T", [D, TOK], F32, kind="ExternalOutput").ap(),
        "x1h": nc.dram_tensor("x1h", [D, TOK], F32).ap(),
        "ln2h": nc.dram_tensor("ln2h", [D, TOK], BF16).ap(),
        "gph": [nc.dram_tensor(f"gph{e}", [(HID + [1024])[e], TOK], BF16).ap() for e in range(9)],
    }
    with P.scope():
        emit_ffn0(P, io)
    return P



def emit_ffn1(P, io, pfx="g_"):
    nc = P.nc
    v = nc.vector
    mm, act, dve, ld, st, lg = P.mm, P.act, P.dve, P.ld, P.st, P.lg
    T = TOK
    TBA = 512
    NTA = T // TBA
    NH = 44
    toff = io.get("toff", 0)
    x2T = io["x2T"]; attnT = io["attnT"]; wo = io["wo"]
    wqgT = io["wqgT"]; wquT = io["wquT"]; wqdT = io["wqdT"]
    sg_c = io["sg_c"]; su_c = io["su_c"]; sd_c = io["sd_c"]
    outT = io["outT"]; x3h = io["x3h"]; ln3h = io["ln3h"]; prodh = io["prodh"]

    ones_bf = P.sbuf(pfx + "ones_bf", [128, 1], BF16)
    ones_row = P.sbuf(pfx + "ones_row", [1, 128], F32)
    cst = P.sbuf(pfx + "cst", [128, 3], F32)
    nc.gpsimd.memset(ones_bf[:, :], 1.0)
    nc.gpsimd.memset(ones_row[:, :], 1.0)
    nc.gpsimd.memset(cst[:, 0:1], EPS)
    P.gp.inc(nc.gpsimd.memset(cst[:, 1:2], -0.5))

    pool = P.psum(pfx + "pool", [128, 4096], F32)
    ln_bc = pool[:, 1024:1536]
    ps_o = pool[:, 1536:2048]
    ps_u = pool[:, 1024:1536]

    x_tb = P.sbuf(pfx + "x_tb", [128, FT * 512], F32)
    ln3_sb = P.sbuf(pfx + "ln3_sb", [128, FT * T], BF16)
    scl = P.sbuf(pfx + "scl", [128, 2 * NH + FT], F32)
    sg_s = scl[:, 0:NH]
    su_s = scl[:, NH:2 * NH]
    sd_s = scl[:, 2 * NH:2 * NH + FT]

    def bar():
        P.barrier_engines()

    nc.sync.dma_start(out=sg_s, in_=sg_c[:, :]).then_inc(ld.h, 16); ld.n += 16
    nc.sync.dma_start(out=su_s, in_=su_c[:, :]).then_inc(ld.h, 16); ld.n += 16
    nc.sync.dma_start(out=sd_s, in_=sd_c[:, :]).then_inc(ld.h, 16); ld.n += 16
    bar()

    xv = hbm_fm_view(x2T)
    av = hbm_fm_view(attnT)
    x3v = hbm_fm_view(x3h)
    lnv = hbm_fm_view(ln3h)

    P.mark("g1A")
    aevac = {}
    bact = {}
    pmul = {}
    with P.scope():
        big = P.sbuf(pfx + "big", [128, 3 * FT * 512], BF16)
        attn_tb = big[:, 0:FT * 512]
        lnscr = LnScratch(P, 512, name=pfx + "lns", xb=big[:, FT * 512:2 * FT * 512],
                          sq=big[:, 2 * FT * 512:3 * FT * 512],
                          st_ps=pool[0:1, 0:1024], bc_ps=ln_bc)
        wsA = P.sbuf(pfx + "wsA", [128, 2 * FT * 128], BF16)
        wosem = [Sem(nc, pfx + f"wosem{i}") for i in range(2)]
        wouse = {}
        woload = {}

        def load_wo(fi):
            sl = fi % 2
            m = fi % FT
            nc.gpsimd.wait_ge(lg.h, lg.n)
            if fi >= 2:
                nc.gpsimd.wait_ge(mm.h, wouse[fi - 2])
            d = nc.gpsimd.dma_start(
                out=wsA[:, sl * 2048:(sl + 1) * 2048].rearrange("p (f c) -> p f c", f=FT),
                in_=hbm_fm_view(wo)[:, :, m * 128:(m + 1) * 128])
            wosem[sl].inc(d, 16)
            woload[fi] = (sl, wosem[sl].n)

        load_wo(0)
        load_wo(1)
        for tb in range(NTA):
            tsl = slice(tb * TBA, (tb + 1) * TBA)
            nc.sync.wait_ge(st.h, st.n)
            nc.sync.wait_ge(mm.h, mm.n)
            nc.sync.wait_ge(dve.h, dve.n)
            nc.sync.wait_ge(act.h, act.n)
            nc.sync.dma_start(out=x_tb[:, :].rearrange("p (f t) -> p f t", f=FT),
                              in_=offap(xv[:, :, tsl], toff)).then_inc(ld.h, 16); ld.n += 16
            nc.sync.dma_start(out=attn_tb.rearrange("p (f t) -> p f t", f=FT),
                              in_=offap(av[:, :, tsl], toff)).then_inc(ld.h, 16); ld.n += 16
            for m in range(FT):
                fi = tb * FT + m
                sl, wt = woload[fi]
                bank = fi % 2
                ps_b = pool[:, (2 + bank) * 512:(3 + bank) * 512]
                for e in (nc.tensor,):
                    e.wait_ge(wosem[sl].h, wt); e.wait_ge(ld.h, ld.n)
                    e.wait_ge(dve.h, aevac.get(fi - 2, 0))
                for ft in range(FT):
                    ins = nc.tensor.matmul(ps_b, wsA[:, sl * 2048 + ft * 128: sl * 2048 + (ft + 1) * 128],
                                           attn_tb[:, ft * 512:(ft + 1) * 512],
                                           start=(ft == 0), stop=(ft == FT - 1))
                mm.inc(ins)
                wouse[fi] = mm.n
                if fi + 2 < NTA * FT:
                    load_wo(fi + 2)
                nc.vector.wait_ge(mm.h, mm.n)
                dve.inc(v.tensor_add(x_tb[:, m * 512:(m + 1) * 512],
                                     ps_b, x_tb[:, m * 512:(m + 1) * 512]))
                aevac[fi] = dve.n
            nc.sync.wait_ge(dve.h, dve.n)
            nc.sync.dma_start(out=x3v[:, :, tsl],
                              in_=x_tb[:, :].rearrange("p (f t) -> p f t", f=FT)).then_inc(st.h, 16)
            st.n += 16
            emit_ln(P, lnscr,
                    lambda ft: x_tb[:, ft * 512:(ft + 1) * 512],
                    lambda ft: ln3_sb[:, ft * T + tb * TBA: ft * T + (tb + 1) * TBA],
                    512, (ones_bf, ones_row, cst))
            nc.sync.wait_ge(dve.h, dve.n)
        P.barrier_engines(extra=wosem)

    P.mark("g1B")
    with P.scope():
        wgb = P.sbuf(pfx + "wgb", [128, 2 * FT * 128], BF16)
        wub = P.sbuf(pfx + "wub", [128, 2 * FT * 128], BF16)
        prw = P.sbuf(pfx + "prw", [128, 2 * T], BF16)
        pra = P.sbuf(pfx + "pra", [128, 2 * 512], BF16)
        wgsem = [Sem(nc, pfx + f"wgsem{i}") for i in range(2)]
        prsem = [Sem(nc, pfx + f"prsem{i}") for i in range(2)]
        guse = {}
        gldd = {}

        def load_gu(hm):
            sl = hm % 2
            nc.gpsimd.wait_ge(lg.h, lg.n)
            if hm >= 2:
                nc.gpsimd.wait_ge(mm.h, guse[hm - 2])
                nc.gpsimd.wait_ge(dve.h, gsub[hm - 2])
            d = nc.gpsimd.dma_start(
                out=wgb[:, sl * 2048:(sl + 1) * 2048].rearrange("p (f c) -> p f c", f=FT),
                in_=wqgT.rearrange("(k p) c -> p k c", p=128)[:, :, hm * 128:(hm + 1) * 128])
            wgsem[sl].inc(d, 16)
            nc.gpsimd.wait_ge(lg.h, lg.n)
            d = nc.gpsimd.dma_start(
                out=wub[:, sl * 2048:(sl + 1) * 2048].rearrange("p (f c) -> p f c", f=FT),
                in_=wquT.rearrange("(k p) c -> p k c", p=128)[:, :, hm * 128:(hm + 1) * 128])
            wgsem[sl].inc(d, 16)
            gldd[hm] = (sl, wgsem[sl].n)

        gsub = {}
        load_gu(0)
        load_gu(1)
        for hm in range(NH):
            sl, wt = gldd[hm]
            nc.vector.wait_ge(wgsem[sl].h, wt)
            nc.vector.wait_ge(mm.h, guse.get(hm - 2, 0))
            dve.inc(v.tensor_scalar_add(wgb[:, sl * 2048:(sl + 1) * 2048],
                                        wgb[:, sl * 2048:(sl + 1) * 2048], -128.0))
            dve.inc(v.tensor_scalar_add(wub[:, sl * 2048:(sl + 1) * 2048],
                                        wub[:, sl * 2048:(sl + 1) * 2048], -128.0))
            gsub[hm] = dve.n
            nc.tensor.wait_ge(dve.h, gsub[hm])
            for tb in range(NTA):
                bank = tb % 2
                ps_g = pool[:, bank * 512:(bank + 1) * 512]
                ps_u = pool[:, (2 + bank) * 512:(3 + bank) * 512]
                nc.tensor.wait_ge(act.h, bact.get(('g', bank), 0))
                for ft in range(FT):
                    ins = nc.tensor.matmul(ps_g, wgb[:, sl * 2048 + ft * 128: sl * 2048 + (ft + 1) * 128],
                                           ln3_sb[:, ft * T + tb * TBA: ft * T + (tb + 1) * TBA],
                                           start=(ft == 0), stop=(ft == FT - 1))
                mm.inc(ins)
                gmm = mm.n
                nc.tensor.wait_ge(act.h, bact.get(('u', bank), 0))
                for ft in range(FT):
                    ins = nc.tensor.matmul(ps_u, wub[:, sl * 2048 + ft * 128: sl * 2048 + (ft + 1) * 128],
                                           ln3_sb[:, ft * T + tb * TBA: ft * T + (tb + 1) * TBA],
                                           start=(ft == 0), stop=(ft == FT - 1))
                mm.inc(ins)
                nc.scalar.wait_ge(mm.h, gmm)
                nc.scalar.wait_ge(dve.h, pmul.get(('pra', bank), 0))
                act.inc(nc.scalar.activation(pra[:, bank * 512:(bank + 1) * 512], ps_g, AF.Silu,
                                             scale=sg_s[:, hm:hm + 1]))
                bact[('g', bank)] = act.n
                nc.scalar.wait_ge(mm.h, mm.n)
                if tb == 0:
                    nc.scalar.wait_ge(prsem[hm % 2].h, prsem[hm % 2].n)
                act.inc(nc.scalar.activation(prw[:, (hm % 2) * T + tb * TBA:(hm % 2) * T + (tb + 1) * TBA],
                                             ps_u, AF.Copy, scale=su_s[:, hm:hm + 1]))
                bact[('u', bank)] = act.n
                nc.vector.wait_ge(act.h, act.n)
                dve.inc(v.tensor_tensor(prw[:, (hm % 2) * T + tb * TBA:(hm % 2) * T + (tb + 1) * TBA],
                                        pra[:, bank * 512:(bank + 1) * 512],
                                        prw[:, (hm % 2) * T + tb * TBA:(hm % 2) * T + (tb + 1) * TBA],
                                        ALU.mult))
                pmul[('pra', bank)] = dve.n
            guse[hm] = mm.n
            if hm + 2 < NH:
                load_gu(hm + 2)
            nc.sync.wait_ge(dve.h, dve.n)
            d = nc.sync.dma_start(out=prodh[hm * 128:(hm + 1) * 128, :],
                                  in_=prw[:, (hm % 2) * T:(hm % 2 + 1) * T])
            prsem[hm % 2].inc(d, 16)
        P.barrier_engines(extra=wgsem + prsem)

    P.mark("g1C")
    with P.scope():
        CDEP = 4
        wdb = P.sbuf(pfx + "wdb", [128, CDEP * 512], BF16)
        prb = P.sbuf(pfx + "prb", [128, CDEP * 1024], BF16)
        wdsem = [Sem(nc, pfx + f"wdsem{i}") for i in range(CDEP)]
        pbsem = [Sem(nc, pfx + f"pbsem{i}") for i in range(CDEP)]
        cuse = {}
        cload = {}
        cidx = [(mq, tb2, hk) for mq in range(4) for tb2 in range(2) for hk in range(NH)]

        def load_c(ci):
            mq, tb2, hk = cidx[ci]
            sl = ci % CDEP
            nc.gpsimd.wait_ge(lg.h, lg.n)
            if ci >= CDEP:
                nc.gpsimd.wait_ge(mm.h, cuse[ci - CDEP])
                nc.gpsimd.wait_ge(dve.h, csub.get(ci - CDEP, 0))
            d = nc.gpsimd.dma_start(out=wdb[:, sl * 512:(sl + 1) * 512],
                                    in_=wqdT[hk * 128:(hk + 1) * 128, mq * 512:(mq + 1) * 512])
            wdsem[sl].inc(d, 16)
            nc.scalar.wait_ge(mm.h, cuse.get(ci - CDEP, 0))
            d = nc.scalar.dma_start(out=prb[:, sl * 1024:(sl + 1) * 1024],
                                    in_=prodh[hk * 128:(hk + 1) * 128, tb2 * 1024:(tb2 + 1) * 1024])
            pbsem[sl].inc(d, 16)
            cload[ci] = (sl, wdsem[sl].n, pbsem[sl].n)

        csub = {}
        for _ci0 in range(CDEP):
            load_c(_ci0)
        ci = 0
        for mq in range(4):
            for tb2 in range(2):
                t0 = tb2 * 1024
                nc.sync.wait_ge(dve.h, dve.n)
                nc.sync.wait_ge(st.h, st.n)
                nc.sync.dma_start(out=x_tb[:, 0:4 * 1024].rearrange("p (f t) -> p f t", f=4),
                                  in_=x3v[:, mq * 4:(mq + 1) * 4, t0:t0 + 1024]).then_inc(ld.h, 16)
                ld.n += 16
                for hk in range(NH):
                    sl, wt, pt = cload[ci]
                    nc.vector.wait_ge(wdsem[sl].h, wt)
                    nc.vector.wait_ge(mm.h, cuse.get(ci - CDEP, 0))
                    dve.inc(v.tensor_scalar_add(wdb[:, sl * 512:(sl + 1) * 512],
                                                wdb[:, sl * 512:(sl + 1) * 512], -128.0))
                    csub[ci] = dve.n
                    for e in (nc.tensor,):
                        e.wait_ge(dve.h, csub[ci])
                        e.wait_ge(pbsem[sl].h, pt)
                    for mi in range(4):
                        for half in range(2):
                            ins = nc.tensor.matmul(
                                pool[:, (mi * 2 + half) * 512:(mi * 2 + half + 1) * 512],
                                wdb[:, sl * 512 + mi * 128: sl * 512 + (mi + 1) * 128],
                                prb[:, sl * 1024 + half * 512: sl * 1024 + (half + 1) * 512],
                                start=(hk == 0), stop=(hk == NH - 1))
                    mm.inc(ins)
                    cuse[ci] = mm.n
                    if ci + CDEP < len(cidx):
                        load_c(ci + CDEP)
                    ci += 1
                nc.vector.wait_ge(mm.h, mm.n)
                nc.vector.wait_ge(ld.h, ld.n)
                for mi in range(4):
                    m = mq * 4 + mi
                    for half in range(2):
                        ins = v.scalar_tensor_tensor(
                            x_tb[:, mi * 1024 + half * 512: mi * 1024 + (half + 1) * 512],
                            pool[:, (mi * 2 + half) * 512:(mi * 2 + half + 1) * 512],
                            sd_s[:, m:m + 1],
                            x_tb[:, mi * 1024 + half * 512: mi * 1024 + (half + 1) * 512],
                            ALU.mult, ALU.add)
                dve.inc(ins)
                nc.sync.wait_ge(dve.h, dve.n)
                nc.sync.dma_start(out=hbm_fm_view(outT)[:, mq * 4:(mq + 1) * 4, t0:t0 + 1024],
                                  in_=x_tb[:, 0:4 * 1024].rearrange("p (f t) -> p f t", f=4)).then_inc(st.h, 16)
                st.n += 16
        P.barrier_engines(extra=wdsem + pbsem)
    nc.sync.wait_ge(st.h, st.n)


def build_ffn1():
    P = Prog()
    nc = P.nc
    NH = 44
    io = {
        "x2T": nc.dram_tensor("x2T", [D, TOK], F32, kind="ExternalInput").ap(),
        "attnT": nc.dram_tensor("attnT", [D, TOK], BF16, kind="ExternalInput").ap(),
        "wo": nc.dram_tensor("wo", [D, D], F32, kind="ExternalInput").ap(),
        "wqgT": nc.dram_tensor("wqgT", [D, 5632], I32, kind="ExternalInput").ap(),
        "wquT": nc.dram_tensor("wquT", [D, 5632], I32, kind="ExternalInput").ap(),
        "wqdT": nc.dram_tensor("wqdT", [5632, D], I32, kind="ExternalInput").ap(),
        "sg_c": nc.dram_tensor("sg_c", [128, NH], F32, kind="ExternalInput").ap(),
        "su_c": nc.dram_tensor("su_c", [128, NH], F32, kind="ExternalInput").ap(),
        "sd_c": nc.dram_tensor("sd_c", [128, FT], F32, kind="ExternalInput").ap(),
        "outT": nc.dram_tensor("outT", [D, TOK], F32, kind="ExternalOutput").ap(),
        "x3h": nc.dram_tensor("x3h", [D, TOK], F32).ap(),
        "ln3h": nc.dram_tensor("ln3h", [D, TOK], BF16).ap(),
        "prodh": nc.dram_tensor("prodh", [5632, TOK], BF16).ap(),
    }
    with P.scope():
        emit_ffn1(P, io)
    return P


def ffn1_in_maps(x2_fm, attn_fm, w):
    wqgT = np.ascontiguousarray(w["wq_gate"].T)
    wquT = np.ascontiguousarray(w["wq_up"].T)
    wqdT = np.ascontiguousarray(w["wq_down"].T)
    shared_m = {
        "wo": w["wo"].astype(np.float32),
        "wqgT": wqgT, "wquT": wquT, "wqdT": wqdT,
        "sg_c": pack_bias(w["s_gate"]), "su_c": pack_bias(w["s_up"]),
        "sd_c": pack_bias(w["s_down"]),
    }
    in_maps = []
    for core in range(8):
        b, hf = core // 2, core % 2
        tsl = slice(hf * TOK, (hf + 1) * TOK)
        m = dict(shared_m)
        m["x2T"] = np.ascontiguousarray(x2_fm[b][:, tsl])
        m["attnT"] = np.ascontiguousarray(attn_fm[b][:, tsl])
        in_maps.append(m)
    return in_maps


def run_ffn1(x2_fm, attn_fm, w):
    P = get_prog("ffn1", build_ffn1)
    in_maps = ffn1_in_maps(x2_fm, attn_fm, w)
    res = run_bass_kernel_spmd(P.nc, in_maps, core_ids=list(range(8)))
    out = np.empty((4, 2048, 4096), dtype=np.float32)
    for core in range(8):
        b, hf = core // 2, core % 2
        out[b][:, hf * TOK:(hf + 1) * TOK] = res.results[core]["outT"]
    return out



def emit_rendezvous(P, rv, idx):
    nc = P.nc
    st = P.st
    nc.sync.wait_ge(st.h, st.n)
    nc.sync.dma_start(out=rv["myslot"][idx], in_=rv["nsb"][0:1, idx * 16:(idx + 1) * 16]
                      ).then_inc(st.h, 16)
    st.n += 16
    nc.sync.wait_ge(st.h, st.n)
    fv, cnt, tgt, psem = rv["fv"], rv["cnt"], rv["tgt"], rv["psem"]
    nc.sync.reg_load(tgt, rv["nsb"][0:1, idx * 16:idx * 16 + 1])
    nc.sync.reg_mov(fv, 0)

    def cond():
        a = nc.sync.snap(fv) != nc.sync.snap(tgt)
        b = nc.sync.snap(cnt) < 2_000_000_000
        return a & b

    with nc.sync.While(cond):
        nc.sync.reg_add(cnt, cnt, 16)
        nc.sync.dma_start(out=rv["fsb"][0:1, :], in_=rv["otherslot"][idx]).then_inc(psem.h, 16)
        nc.sync.wait_ge(psem.h, nc.sync.snap(cnt))
        nc.sync.reg_load(fv, rv["fsb"][0:1, 0:1])
    P.gp.inc(nc.sync.nop())
    for e in (nc.tensor, nc.scalar, nc.vector, nc.gpsimd, nc.sync):
        e.wait_ge(P.gp.h, P.gp.n)


def build_fused():
    P = Prog()
    nc = P.nc
    NE = 8

    def din(name, shape, dtype=F32):
        return nc.dram_tensor(name, shape, dtype, kind="ExternalInput").ap()

    xb = din("xb", [D, S])
    attn0_sh = nc.dram_tensor("attn0_sh", [D, S], BF16, addr_space="Shared").ap()
    attn1_sh = nc.dram_tensor("attn1_sh", [D, S], BF16, addr_space="Shared").ap()
    x2_sh = nc.dram_tensor("x2_sh", [D, S], F32, addr_space="Shared").ap()
    attn0_loc = nc.dram_tensor("attn0_loc", [1024, S], BF16).ap()
    attn1_loc = nc.dram_tensor("attn1_loc", [1024, S], BF16).ap()
    x_loc = nc.dram_tensor("x_loc", [D, TOK], F32).ap()
    attnT_loc = nc.dram_tensor("attnT_loc", [D, TOK], BF16).ap()
    attn1T_loc = nc.dram_tensor("attn1T_loc", [D, TOK], BF16).ap()
    x2_loc = nc.dram_tensor("x2_loc", [D, TOK], F32).ap()

    a0 = {
        "xT": xb,
        "wqT": din("a0_wqT", [D, 1024]), "wkT": din("a0_wkT", [D, 1024]),
        "wvT": din("a0_wvT", [D, 1024]),
        "maskm": din("a0_maskm", [128, 1024], BF16),
        "qh": nc.dram_tensor("a0_qh", [1024, S], BF16).ap(),
        "kh": nc.dram_tensor("a0_kh", [1024, S], BF16).ap(),
        "vh": nc.dram_tensor("a0_vh", [S, 1024], BF16).ap(),
        "attnout": attn0_loc,
    }
    f0 = {
        "xT": x_loc,
        "attnT": attnT_loc,
        "x2T": x2_loc,
        "woT": din("f0_woT", [D, D]),
        "down_w": din("f0_down_w", [D, 1024]), "down_b": din("f0_down_b", [128, 8]),
        "up_w": din("f0_up_w", [1024, D]), "upcore_b": din("f0_upcore_b", [128, 32]),
        "router_w": din("f0_router_w", [D, 8]), "router_b": din("f0_router_b", [1, 8]),
        "cost_t": din("f0_cost_t", [1, 8]),
        "shared_w1": din("f0_shared_w1", [1024, 1024]),
        "shared_w2": din("f0_shared_w2", [1024, 1024]),
        "shared_b": din("f0_shared_b", [128, 16]),
        "core_w": din("f0_core_w", [D, D]),
        "ew1": [din(f"f0_e{e}_w1", [1024, HID[e]]) for e in range(NE)],
        "ew2": [din(f"f0_e{e}_w2", [HID[e], 1024]) for e in range(NE)],
        "eb1": din("f0_eb1", [128, 160]), "eb2": din("f0_eb2", [128, 64]),
        "idn_h": din("f0_idn_h", [128, 128]), "sel_h": din("f0_sel_h", [8, 1024], BF16),
        "x1h": nc.dram_tensor("f0_x1h", [D, TOK], F32).ap(),
        "ln2h": nc.dram_tensor("f0_ln2h", [D, TOK], BF16).ap(),
        "gph": [nc.dram_tensor(f"f0_gph{e}", [(HID + [1024])[e], TOK], BF16).ap() for e in range(9)],
    }
    a1 = {
        "xT": x2_sh,
        "wqT": din("a1_wqT", [D, 1024]), "wkT": din("a1_wkT", [D, 256]),
        "wvT": din("a1_wvT", [D, 256]),
        "maskm": din("a1_maskm", [128, 2688], BF16),
        "qh": nc.dram_tensor("a1_qh", [1024, S], BF16).ap(),
        "kh": nc.dram_tensor("a1_kh", [256, S], BF16).ap(),
        "vh": nc.dram_tensor("a1_vh", [S, 260], BF16).ap(),
        "attnout": attn1_loc,
    }
    g = {
        "x2T": x2_loc,
        "attnT": attn1T_loc,
        "wo": din("g_wo", [D, D]),
        "wqgT": din("g_wqgT", [D, 5632], I32), "wquT": din("g_wquT", [D, 5632], I32),
        "wqdT": din("g_wqdT", [5632, D], I32),
        "sg_c": din("g_sg_c", [128, 44]), "su_c": din("g_su_c", [128, 44]),
        "sd_c": din("g_sd_c", [128, FT]),
        "outT": nc.dram_tensor("outT", [D, TOK], F32, kind="ExternalOutput").ap(),
        "x3h": nc.dram_tensor("g_x3h", [D, TOK], F32).ap(),
        "ln3h": nc.dram_tensor("g_ln3h", [D, TOK], BF16).ap(),
        "prodh": nc.dram_tensor("g_prodh", [5632, TOK], BF16).ap(),
    }
    nonces = din("nonces", [1, 64], I32)
    fls = [nc.dram_tensor(f"fl{i}", [2, 16], I32, addr_space="Shared").ap() for i in range(4)]

    pid = nc.partition_id([mybir.EngineType.SP])
    parity = pid % 2
    toff = nc.sync.snap(parity * TOK)
    roff = nc.sync.snap(parity * (1024 * S))
    flmy = nc.sync.snap(parity * 16)
    flot = nc.sync.snap((1 - parity) * 16)

    rv = {}
    rv["nsb"] = P.sbuf("rv_nsb", [1, 64], I32)
    rv["fsb"] = P.sbuf("rv_fsb", [1, 16], I32)
    rv["fv"] = nc.alloc_register(mybir.EngineType.SP, "rv_fv")
    rv["cnt"] = nc.alloc_register(mybir.EngineType.SP, "rv_cnt")
    rv["tgt"] = nc.alloc_register(mybir.EngineType.SP, "rv_tgt")
    rv["psem"] = Sem(nc, "rv_psem")
    rv["myslot"] = [dataclasses.replace(f[0:1, 0:16], offset=flmy) for f in fls]
    rv["otherslot"] = [dataclasses.replace(f[0:1, 0:16], offset=flot) for f in fls]
    nc.sync.dma_start(out=rv["nsb"][:, :], in_=nonces[:, :]).then_inc(P.ld.h, 16)
    P.ld.n += 16
    nc.sync.reg_mov(rv["cnt"], 0)
    nc.sync.wait_ge(P.ld.h, P.ld.n)

    def bulk(dst_ap, src_ap):
        nc.sync.dma_start(out=dst_ap, in_=src_ap).then_inc(P.st.h, 16)
        P.st.n += 16
        nc.sync.wait_ge(P.st.h, P.st.n)

    bulk(x_loc[:, :], offap(xb[:, 0:TOK], toff))

    with nc.named_scope("attn0"):
        with P.scope():
            emit_attn(P, 0, a0, pfx="a0_")
    with nc.named_scope("rdv0"):
        bulk(offap(attn0_sh[0:1024, :], roff), attn0_loc[:, :])
        emit_rendezvous(P, rv, 0)
        bulk(attnT_loc[:, :], offap(attn0_sh[:, 0:TOK], toff))
    with nc.named_scope("ffn0"):
        with P.scope():
            emit_ffn0(P, f0, pfx="f0_")
    with nc.named_scope("rdv1"):
        bulk(offap(x2_sh[:, 0:TOK], toff), x2_loc[:, :])
        emit_rendezvous(P, rv, 1)
    with nc.named_scope("attn1"):
        with P.scope():
            emit_attn(P, 1, a1, pfx="a1_")
    with nc.named_scope("rdv2"):
        bulk(offap(attn1_sh[0:1024, :], roff), attn1_loc[:, :])
        emit_rendezvous(P, rv, 2)
        bulk(attn1T_loc[:, :], offap(attn1_sh[:, 0:TOK], toff))
    with nc.named_scope("ffn1"):
        with P.scope():
            emit_ffn1(P, g, pfx="g_")
    nc.sync.wait_ge(P.st.h, P.st.n)
    return P


def fused_in_maps(ins):
    x = ins["L0_attn_hidden_states"]
    xT_all = np.ascontiguousarray(x.transpose(0, 2, 1)).astype(np.float32)
    w_qkv = ins["L0_attn_w_qkv"]
    wq0, wk0, wv0 = w_qkv[0:2048], w_qkv[2048:4096], w_qkv[4096:6144]
    mask0 = make_mask_master(0)
    mask1 = make_mask_master(1)
    shared_m = {
        "a0_maskm": mask0, "a1_maskm": mask1,
        "f0_woT": np.ascontiguousarray(ins["L0_attn_w_o"].T.astype(np.float32)),
        "f0_down_w": ins["L0_ffn_down_w"], "f0_down_b": pack_bias(ins["L0_ffn_down_b"]),
        "f0_up_w": ins["L0_ffn_up_w"],
        "f0_upcore_b": np.concatenate([pack_bias(ins["L0_ffn_up_b"]),
                                       pack_bias(ins["L0_ffn_core_b"])], axis=1),
        "f0_router_w": ins["L0_ffn_router_w"],
        "f0_router_b": ins["L0_ffn_router_b"].reshape(1, 8).astype(np.float32),
        "f0_cost_t": _l0_cost_consts(),
        "f0_shared_w1": ins["L0_ffn_shared_w1"], "f0_shared_w2": ins["L0_ffn_shared_w2"],
        "f0_shared_b": np.concatenate([pack_bias(ins["L0_ffn_shared_b1"]),
                                       pack_bias(ins["L0_ffn_shared_b2"])], axis=1),
        "f0_core_w": ins["L0_ffn_core_w"],
        "f0_eb1": np.concatenate([pack_bias(ins[f"L0_ffn_e{e}_b1"]) for e in range(8)], axis=1),
        "f0_eb2": pack_b2([ins[f"L0_ffn_e{e}_b2"] for e in range(8)]),
        "f0_idn_h": np.eye(128, dtype=np.float32),
        "f0_sel_h": _sel_mat().astype(BF),
        "g_wo": ins["L1_attn_wo"].astype(np.float32),
        "g_wqgT": np.ascontiguousarray(ins["L1_ffn_wq_gate"].T),
        "g_wquT": np.ascontiguousarray(ins["L1_ffn_wq_up"].T),
        "g_wqdT": np.ascontiguousarray(ins["L1_ffn_wq_down"].T),
        "g_sg_c": pack_bias(ins["L1_ffn_s_gate"]), "g_su_c": pack_bias(ins["L1_ffn_s_up"]),
        "g_sd_c": pack_bias(ins["L1_ffn_s_down"]),
        "nonces": np.random.randint(1, 2 ** 30, size=(1, 64)).astype(np.int32),
    }
    for e in range(8):
        shared_m[f"f0_e{e}_w1"] = ins[f"L0_ffn_e{e}_w1"]
        shared_m[f"f0_e{e}_w2"] = ins[f"L0_ffn_e{e}_w2"]
    in_maps = []
    for core in range(8):
        b, hg = core // 2, core % 2
        m = dict(shared_m)
        m["xb"] = xT_all[b]
        r0, r1 = hg * 1024, hg * 1024 + 1024
        m["a0_wqT"] = np.ascontiguousarray(wq0[r0:r1].T)
        m["a0_wkT"] = np.ascontiguousarray(wk0[r0:r1].T)
        m["a0_wvT"] = np.ascontiguousarray(wv0[r0:r1].T)
        m["a1_wqT"] = np.ascontiguousarray(ins["L1_attn_wq"][:, r0:r1])
        kc0, kc1 = hg * 256, hg * 256 + 256
        m["a1_wkT"] = np.ascontiguousarray(ins["L1_attn_wk"][:, kc0:kc1])
        m["a1_wvT"] = np.ascontiguousarray(ins["L1_attn_wv"][:, kc0:kc1])
        in_maps.append(m)
    return in_maps


def kernel(**inputs):
    ins = {k: np.asarray(v) for k, v in inputs.items()}
    P = get_prog("fused", build_fused)
    in_maps = fused_in_maps(ins)
    res = run_bass_kernel_spmd(P.nc, in_maps, core_ids=list(range(8)))
    out = np.empty((4, 2048, 4096), dtype=np.float32)
    for core in range(8):
        b, hf = core // 2, core % 2
        out[b][:, hf * TOK:(hf + 1) * TOK] = res.results[core]["outT"]
    return np.ascontiguousarray(out.transpose(0, 2, 1))



def make_mask_master(layer):
    if layer == 0:
        p = np.arange(128)[:, None]
        c = np.arange(1024)[None, :]
        return (c >= p + 512).astype(BF)
    else:
        p = np.arange(128)[:, None]
        c = np.arange(2688)[None, :]
        d = c - p - 640
        return (~((d >= 1) & (d <= 1024))).astype(BF)


_PROG_CACHE = {}


def get_prog(key, builder):
    if key not in _PROG_CACHE:
        _PROG_CACHE[key] = builder()
    return _PROG_CACHE[key]


def run_l0_attn(x, w_qkv):
    P = get_prog("attn0", lambda: build_attn(0))
    mask = make_mask_master(0)
    xT = np.ascontiguousarray(x.transpose(0, 2, 1))
    wq = w_qkv[0:2048]
    wk = w_qkv[2048:4096]
    wv = w_qkv[4096:6144]
    in_maps = []
    for core in range(8):
        b, hg = core // 2, core % 2
        r0, r1 = hg * 1024, hg * 1024 + 1024
        in_maps.append({
            "xT": xT[b],
            "wqT": np.ascontiguousarray(wq[r0:r1].T),
            "wkT": np.ascontiguousarray(wk[r0:r1].T),
            "wvT": np.ascontiguousarray(wv[r0:r1].T),
            "maskm": mask,
        })
    res = run_bass_kernel_spmd(P.nc, in_maps, core_ids=list(range(8)))
    out = np.empty((4, 2048, 4096), dtype=BF)
    for core in range(8):
        b, hg = core // 2, core % 2
        out[b, hg * 1024:(hg + 1) * 1024] = res.results[core]["attnout"]
    return out


def run_l1_attn(x_fm, wq, wk, wv):
    P = get_prog("attn1", lambda: build_attn(1))
    mask = make_mask_master(1)
    in_maps = []
    for core in range(8):
        b, hg = core // 2, core % 2
        qc0, qc1 = hg * 1024, hg * 1024 + 1024
        kc0, kc1 = hg * 256, hg * 256 + 256
        in_maps.append({
            "xT": x_fm[b],
            "wqT": np.ascontiguousarray(wq[:, qc0:qc1]),
            "wkT": np.ascontiguousarray(wk[:, kc0:kc1]),
            "wvT": np.ascontiguousarray(wv[:, kc0:kc1]),
            "maskm": mask,
        })
    res = run_bass_kernel_spmd(P.nc, in_maps, core_ids=list(range(8)))
    out = np.empty((4, 2048, 4096), dtype=BF)
    for core in range(8):
        b, hg = core // 2, core % 2
        out[b, hg * 1024:(hg + 1) * 1024] = res.results[core]["attnout"]
    return out



def _sel_mat():
    s = np.zeros((8, 1024), np.float32)
    for e in range(8):
        s[e, e * 128:(e + 1) * 128] = 1.0
    return s


def _l0_cost_consts():
    return np.array([[COST_LAMBDA * 2048.0 * h for h in HID]], dtype=np.float32)


def pack_b2(b2_list):
    out = np.zeros((128, 64), np.float32)
    for e, b2 in enumerate(b2_list):
        out[:, e::8] = b2.reshape(8, 128).T
    return out


def ffn0_in_maps(xT_all, attn_fm, w):
    woT = np.ascontiguousarray(w["w_o"].T.astype(np.float32))
    shared_m = {
        "woT": woT,
        "down_w": w["down_w"], "down_b": pack_bias(w["down_b"]),
        "up_w": w["up_w"],
        "upcore_b": np.concatenate([pack_bias(w["up_b"]), pack_bias(w["core_b"])], axis=1),
        "router_w": w["router_w"], "router_b": w["router_b"].reshape(1, 8),
        "cost_t": _l0_cost_consts(),
        "shared_w1": w["shared_w1"], "shared_w2": w["shared_w2"],
        "shared_b": np.concatenate([pack_bias(w["shared_b1"]), pack_bias(w["shared_b2"])], axis=1),
        "core_w": w["core_w"],
        "eb1": np.concatenate([pack_bias(w[f"e{e}_b1"]) for e in range(8)], axis=1),
        "eb2": pack_b2([w[f"e{e}_b2"] for e in range(8)]),
        "idn_h": np.eye(128, dtype=np.float32),
        "sel_h": _sel_mat().astype(BF),
    }
    for e in range(8):
        shared_m[f"e{e}_w1"] = w[f"e{e}_w1"]
        shared_m[f"e{e}_w2"] = w[f"e{e}_w2"]
    in_maps = []
    for core in range(8):
        b, hf = core // 2, core % 2
        tsl = slice(hf * TOK, (hf + 1) * TOK)
        m = dict(shared_m)
        m["xT"] = np.ascontiguousarray(xT_all[b][:, tsl])
        m["attnT"] = np.ascontiguousarray(attn_fm[b][:, tsl])
        in_maps.append(m)
    return in_maps


def run_ffn0(xT_all, attn_fm, w):
    P = get_prog("ffn0", build_ffn0)
    in_maps = ffn0_in_maps(xT_all, attn_fm, w)
    res = run_bass_kernel_spmd(P.nc, in_maps, core_ids=list(range(8)))
    out = np.empty((4, 2048, 4096), dtype=np.float32)
    for core in range(8):
        b, hf = core // 2, core % 2
        out[b][:, hf * TOK:(hf + 1) * TOK] = res.results[core]["x2T"]
    return out


def kernel_unfused(**inputs):
    ins = {k: np.asarray(v) for k, v in inputs.items()}
    x = ins["L0_attn_hidden_states"]
    xT_all = np.ascontiguousarray(x.transpose(0, 2, 1))

    attn0 = run_l0_attn(x, ins["L0_attn_w_qkv"])

    w0 = {"w_o": ins["L0_attn_w_o"]}
    for k in ("down_w", "down_b", "up_w", "up_b", "router_w", "router_b",
              "shared_w1", "shared_b1", "shared_w2", "shared_b2",
              "core_w", "core_b"):
        w0[k] = ins["L0_ffn_" + k]
    for e in range(8):
        for p in ("w1", "b1", "w2", "b2"):
            w0[f"e{e}_{p}"] = ins[f"L0_ffn_e{e}_{p}"]
    x2_fm = run_ffn0(xT_all, attn0, w0)

    attn1 = run_l1_attn(x2_fm, ins["L1_attn_wq"], ins["L1_attn_wk"], ins["L1_attn_wv"])

    w1 = {"wo": ins["L1_attn_wo"]}
    for k in ("wq_gate", "s_gate", "wq_up", "s_up", "wq_down", "s_down"):
        w1[k] = ins["L1_ffn_" + k]
    out_fm = run_ffn1(x2_fm, attn1, w1)
    return np.ascontiguousarray(out_fm.transpose(0, 2, 1))
